# version 04
# baseline (speedup 1.0000x reference)
import numpy as np
import ml_dtypes
from contextlib import ExitStack
import concourse.bass as bass
import concourse.mybir as mybir
from concourse.bass_utils import run_bass_kernel_spmd

F32 = mybir.dt.float32
BF16 = mybir.dt.bfloat16
I32 = mybir.dt.int32
AF = mybir.ActivationFunctionType
ALU = mybir.AluOpType
AX = mybir.AxisListType

B, S, D = 16, 2048, 1024
NCORES = 8
BL = B // NCORES
TOK = BL * S
NH, DH = 16, 64
NPAIR = NH // 2
FH = 3072
EXP = 4096
KG, E1, S1, E2 = 64, 8, 2, 4
G = S // KG
M1 = E1 * S1
LL = G * M1
N2 = LL // E2
HID = 4096
DC = D // 128
ENGINES = ("sync", "gpsimd", "scalar", "vector", "tensor")
NEG = -1.0e30


def _lhsT(w):
    K, M = w.shape
    return np.ascontiguousarray(
        w.reshape(K // 128, 128, M // 128, 128).transpose(2, 1, 0, 3))


def _fm(v):
    return np.ascontiguousarray(np.asarray(v).reshape(-1, 128).T)


class _Smalls:
    def __init__(self):
        self.cols = []
        self.map = {}
        self.n = 0

    def add(self, name, arr):
        arr = np.asarray(arr, np.float32).reshape(128, -1)
        self.map[name] = self.n
        self.n += arr.shape[1]
        self.cols.append(arr)

    def finalize(self):
        return np.ascontiguousarray(np.concatenate(self.cols, axis=1))


def prep_inputs(inp):
    d = {}
    wq = inp["L0_attn_Wq"].reshape(NPAIR, 2, D, DH)
    wq = wq.transpose(0, 2, 1, 3).reshape(NPAIR, D, 128)
    d["wq"] = np.stack([_lhsT(w)[0] for w in wq])
    wk = inp["L0_attn_Wk"].reshape(NPAIR, 2, D, DH).transpose(0, 2, 1, 3).reshape(NPAIR, D, 128)
    d["wk"] = np.stack([_lhsT(w)[0] for w in wk])
    d["wv"] = np.ascontiguousarray(inp["L0_attn_Wv"].transpose(1, 0, 2).reshape(D, D))
    d["wo0"] = _lhsT(inp["L0_attn_Wo"])
    d["wg0"] = _lhsT(np.ascontiguousarray(inp["L0_ffn_w_gate"].T))
    d["wu0"] = _lhsT(np.ascontiguousarray(inp["L0_ffn_w_up"].T))
    d["wd0"] = _lhsT(np.ascontiguousarray(inp["L0_ffn_w_down"].T))
    d["w1"] = _lhsT(inp["L1_attn_W1"])
    d["w2"] = _lhsT(inp["L1_attn_W2"])
    d["wz"] = _lhsT(inp["L1_attn_Wz"])
    d["wh"] = _lhsT(inp["L1_attn_Wh"])
    d["wo1"] = _lhsT(inp["L1_attn_Wo"])
    ws1 = inp["L1_ffn_W_slot1"].reshape(D, M1, D).transpose(1, 0, 2)
    d["ws1"] = np.stack([_lhsT(w) for w in ws1])
    d["w1f"] = np.stack([_lhsT(w) for w in inp["L1_ffn_W1_first"]])
    d["w2f"] = np.stack([_lhsT(w) for w in inp["L1_ffn_W2_first"]])
    ws2 = inp["L1_ffn_W_slot2"].reshape(D, E2, D).transpose(1, 0, 2)
    d["ws2"] = np.stack([_lhsT(w) for w in ws2])
    d["w1s"] = np.stack([_lhsT(w) for w in inp["L1_ffn_W1_second"]])
    d["w2s"] = np.stack([_lhsT(w) for w in inp["L1_ffn_W2_second"]])

    sm = _Smalls()
    sm.add("bq", inp["L0_attn_bq"].reshape(NPAIR, 128).T)
    sm.add("bk", inp["L0_attn_bk"].reshape(NPAIR, 128).T)
    sm.add("bo0", _fm(inp["L0_attn_bo"]))
    sm.add("b1", _fm(inp["L1_attn_b1"]))
    sm.add("b2", _fm(inp["L1_attn_b2"]))
    sm.add("bz", _fm(inp["L1_attn_bz"]))
    sm.add("bh", _fm(inp["L1_attn_bh"]))
    sm.add("bo1", _fm(inp["L1_attn_bo"]))
    sm.add("g1", _fm(inp["L1_ffn_gamma1"]))
    sm.add("gs1", _fm(inp["L1_ffn_gamma_slot1"]))
    sm.add("g2", _fm(inp["L1_ffn_gamma2"]))
    sm.add("gs2", _fm(inp["L1_ffn_gamma_slot2"]))
    sm.add("b1f", np.concatenate([_fm(v) for v in inp["L1_ffn_b1_first"]], 1))
    sm.add("b2f", np.concatenate([_fm(v) for v in inp["L1_ffn_b2_first"]], 1))
    sm.add("b1s", np.concatenate([_fm(v) for v in inp["L1_ffn_b1_second"]], 1))
    sm.add("b2s", np.concatenate([_fm(v) for v in inp["L1_ffn_b2_second"]], 1))
    sm.add("wg", _fm(inp["L1_ffn_Wg"][:, 0]))
    d["smalls"] = sm.finalize()
    d["smmap"] = sm.map
    row = np.zeros((1, 1040), np.float32)
    row[0, :D] = inp["L0_attn_bv"].reshape(D)
    row[0, D:D + E1] = inp["L1_ffn_expert_gates"]
    row[0, D + E1] = inp["L1_ffn_bg"][0]
    d["smrow"] = row
    return d



class Trk:
    __slots__ = ("w", "r")

    def __init__(self):
        self.w = None
        self.r = {}

    def rd(self, tok):
        if tok is not None:
            s, v = tok
            if self.r.get(s, 0) < v:
                self.r[s] = v

    def wr(self, tok):
        if tok is not None:
            self.w = tok
            self.r = {}

    def w_waits(self):
        out = [(s, v) for s, v in self.r.items()]
        if self.w is not None:
            out.append(self.w)
        return out

    def r_waits(self):
        return [self.w] if self.w is not None else []


class Prog:
    def __init__(self, nc):
        self.nc = nc
        self.rec = {e: [] for e in ENGINES}
        self.semh = {}
        self.cnt = {}
        self.wm = {e: {} for e in ENGINES}
        for e in ENGINES:
            self._sem("s_" + e)
        self.bounds = {}
        self.fence = {}
        for q in ("sync", "gpsimd", "scalar"):
            for sfx in ("", "B", "C"):
                self._sem("d" + sfx + "_" + q)
                self.bounds["d" + sfx + "_" + q] = [0]
                self.fence["d" + sfx + "_" + q] = None
        self.tag_since_wait = {e: False for e in ENGINES}
        self.last_tag = {e: 0 for e in ENGINES}
        self.ninstr = 0

    def _sem(self, name):
        self.semh[name] = self.nc.alloc_semaphore(name)
        self.cnt[name] = 0

    def wait(self, eng, tok):
        if tok is None:
            return
        s, v = tok
        if s in self.bounds and not (eng in ("sync", "gpsimd", "scalar") and s.endswith("_" + eng)):
            bl = self.bounds[s]
            if bl[-1] < v:
                cur = self.cnt[s]
                bl.append(cur)
                self.fence[s] = cur
                v = cur
            else:
                import bisect
                v = bl[bisect.bisect_left(bl, v)]
        if self.wm[eng].get(s, 0) >= v:
            return
        self.wm[eng][s] = v
        self.rec[eng].append(("w", s, v))
        self.tag_since_wait[eng] = False

    def op(self, eng, meth, *args, waits=(), tag=False, **kw):
        for t in waits:
            self.wait(eng, t)
        tok = None
        sem = None
        if tag:
            if self.tag_since_wait[eng]:
                sname = "s_" + eng
                lv = self.last_tag[eng]
                self.rec[eng].append(("w", sname, lv))
                if self.wm[eng].get(sname, 0) < lv:
                    self.wm[eng][sname] = lv
            sname = "s_" + eng
            self.cnt[sname] += 1
            tok = (sname, self.cnt[sname])
            sem = sname
            self.tag_since_wait[eng] = True
            self.last_tag[eng] = self.cnt[sname]
        self.rec[eng].append(("i", meth, args, kw, sem, 1))
        self.ninstr += 1
        return tok

    def dma(self, q, out, in_, waits=(), ch=0, **kw):
        sname = "d" + ("", "B", "C")[ch] + "_" + q
        if self.fence[sname] is not None:
            fv = self.fence[sname]
            self.fence[sname] = None
            if self.wm[q].get(sname, 0) < fv:
                self.wm[q][sname] = fv
                self.rec[q].append(("w", sname, fv))
        for t in waits:
            self.wait(q, t)
        self.cnt[sname] += 16
        self.rec[q].append(("i", "dma_start", (), dict(out=out, in_=in_, **kw), sname, 16))
        self.ninstr += 1
        return (sname, self.cnt[sname])

    def barrier(self):
        for q in ("sync", "gpsimd", "scalar"):
            for sfx in ("", "B", "C"):
                sname = "d" + sfx + "_" + q
                cur = self.cnt[sname]
                if self.bounds[sname][-1] < cur:
                    self.bounds[sname].append(cur)
                self.fence[sname] = None
                self.wait(q, (sname, cur))
        toks = [self.op(e, "nop", tag=True) for e in ENGINES]
        for e in ENGINES:
            for t in toks:
                self.wait(e, t)

    def finish(self):
        for q in ("sync", "gpsimd", "scalar"):
            for sfx in ("", "B", "C"):
                sname = "d" + sfx + "_" + q
                cur = self.cnt[sname]
                if self.bounds[sname][-1] < cur:
                    self.bounds[sname].append(cur)
                self.wait("sync", (sname, cur))

    def replay(self):
        with self.nc.Block() as blk:
            for e in ENGINES:
                getattr(blk, e)(lambda eng, e=e: self._run(eng, e))

    def _run(self, eng, ename):
        semh = self.semh
        for item in self.rec[ename]:
            if item[0] == "w":
                eng.wait_ge(semh[item[1]], item[2])
            else:
                _, meth, args, kw, sem, inc = item
                try:
                    inst = getattr(eng, meth)(*args, **kw)
                except Exception as ex:
                    def nm(a):
                        try:
                            return "%s%s" % (a.tensor.name, list(a.shape))
                        except Exception:
                            return str(a)[:40]
                    raise RuntimeError(
                        "replay failed: engine=%s meth=%s args=%s kw=%s: %s"
                        % (ename, meth, [nm(a) for a in args],
                           {k: nm(v) for k, v in kw.items()}, ex)) from ex
                if sem is not None:
                    inst.then_inc(semh[sem], inc)


class Ctx:
    def __init__(self, nc, p, stack):
        self.nc = nc
        self.p = p
        self.hbm = {}
        self.sb = {}
        self.trk = {}
        self.sm = None
        self.gstack = stack
        self.stack = stack
        self._frames = []

    def push(self):
        fr = ExitStack()
        self._frames.append(fr)
        self.stack = fr

    def pop(self):
        fr = self._frames.pop()
        fr.close()
        self.stack = self._frames[-1] if self._frames else self.gstack
        self.p.barrier()

    def sbuf(self, name, shape, dtype):
        self._uniq = getattr(self, "_uniq", 0) + 1
        h = self.stack.enter_context(
            self.nc.sbuf_tensor("%s_u%d" % (name, self._uniq), shape, dtype))
        ap = h[:]
        self.sb[name] = ap
        self.trk[name] = Trk()
        return ap

    def views(self, name, n, shape, dtype):
        ap = self.sbuf(name, [shape[0], n * shape[1]], dtype)
        out = []
        for i in range(n):
            out.append(ap[:, i * shape[1]:(i + 1) * shape[1]])
            self.trk[name + str(i)] = Trk()
        return out

    def psum(self, name, shape, dtype):
        h = self.gstack.enter_context(self.nc.psum_tensor(name, shape, dtype))
        ap = h[:]
        self.sb[name] = ap
        self.trk[name] = Trk()
        return ap

    def cp(self, name, prow=False):
        j = self.cpmap[name]
        ap = self.sb["cpool"]
        return ap[0:1, j:j + 1] if prow else ap[:, j:j + 1]

    def smallc(self, name, off=0, n=1):
        col = self.sm[name] + off
        return self.sb["smalls"][:, col:col + n]


CONSTS = {"eps5": 1e-5, "s8": 0.125, "dq": 1.0 / 127.0, "neg1": -1.0, "one": 1.0}


def emit_consts(c):
    p = c.p
    cpool = c.sbuf("cpool", [128, len(CONSTS)], F32)
    c.cpmap = {}
    for j, (nme, val) in enumerate(CONSTS.items()):
        c.cpmap[nme] = j
        p.op("gpsimd", "memset", cpool[:, j:j + 1], float(val), tag=True)
    ones_f = c.sbuf("ones_f", [128, 128], F32)
    ident_b = c.sbuf("ident_b", [128, 128], BF16)
    mask = c.sbuf("mask", [128, 128], F32)
    smalls = c.sbuf("smalls", [128, c.hbm["smalls"].shape[1]], F32)
    smrow = c.sbuf("smrow", [1, 1040], F32)
    t1 = p.op("gpsimd", "memset", ones_f, 1.0, tag=True)
    t3 = p.op("gpsimd", "memset", ident_b, 0.0, tag=True)
    t4 = p.op("gpsimd", "memset", mask, 0.0, tag=True)
    p.op("gpsimd", "affine_select", waits=[t3], tag=True, out=ident_b,
         in_=ident_b, compare_op=ALU.not_equal, fill=1.0, base=0,
         pattern=[[-1, 128]], channel_multiplier=1)
    p.op("gpsimd", "affine_select", waits=[t4], tag=True, out=mask,
         in_=mask, compare_op=ALU.is_ge, fill=NEG, base=0,
         pattern=[[-1, 128]], channel_multiplier=1)
    p.dma("sync", smalls, c.hbm["smalls"])
    p.dma("sync", smrow, c.hbm["smrow"])
    p.barrier()


def row_recip_sqrt(c, out_row, in_row, eps, scale=1.0, waits=(), floor=None):
    p = c.p
    n = out_row.shape[-1]
    tmp = c.sb["rowtmp"][0:1, 0:n]
    assert eps == 1e-5
    tk = p.op("scalar", "activation", tmp, in_row, AF.Sqrt, bias=c.cp("eps5", prow=True),
              waits=list(waits) + c.trk["rowtmp"].w_waits(), tag=True)
    c.trk["rowtmp"].wr(tk)
    if floor is not None:
        tk = p.op("vector", "tensor_scalar_max", tmp, tmp, float(floor), waits=[tk], tag=True)
        c.trk["rowtmp"].wr(tk)
    tk = p.op("vector", "reciprocal", out_row, tmp, waits=[tk], tag=True)
    c.trk["rowtmp"].rd(tk)
    if scale != 1.0:
        tk = p.op("vector", "tensor_scalar_mul", out_row, out_row, float(scale),
                  waits=[tk], tag=True)
    return tk



DBG_L0 = False
L0_SKIP = ()


def emit_l0(c):
    p = c.p
    t = c.trk
    x = c.hbm["x"]
    xn0_s = c.hbm["xn0_s"]
    x1_s = c.hbm["x1_s"]
    c.push()

    pbig = c.sb["pbig"]
    pv0, pv1, pv2 = c.sb["pv0"], c.sb["pv1"], c.sb["pv2"]
    pt = c.sb["pt"]

    xch = c.views("xch", 2, [128, 512], F32)
    arow = c.sbuf("arow", [1, 512], F32)
    brow = c.sbuf("brow", [1, 512], F32)
    c.sbuf("rowtmp", [1, 1024], F32)
    for nme in ("mrow", "vrow"):
        c.sbuf(nme, [1, 512], F32)
    abc = c.sbuf("abc", [128, 512], F32)
    bbc = c.sbuf("bbc", [128, 512], F32)
    xn = c.sbuf("xn", [128, DC * 512], BF16)
    _shr = 16 if "smallbuf" in L0_SKIP else 1
    wq_sb = c.sbuf("wq_sb", [128, NPAIR * 1024 // _shr], BF16)
    wk_sb = c.sbuf("wk_sb", [128, NPAIR * 1024 // _shr], BF16)
    wsh = c.sbuf("wsh", [128, DC * 1024 // _shr], BF16)
    q_sb = c.sbuf("q_sb", [128, NPAIR * 2048 // _shr], BF16)
    k_sb = c.sbuf("k_sb", [128, NPAIR * 2048 // _shr], BF16)
    v_sb = c.sbuf("v_sb", [128, 16 * 1024 // _shr], BF16)
    for pr in range(NPAIR):
        t["q%d" % pr] = Trk(); t["k%d" % pr] = Trk(); t["o%d" % pr] = Trk()
    for tc in range(16):
        t["v%d" % tc] = Trk()
    psb = c.views("psb", 1, [128, 2048], BF16)
    ptsb = c.views("ptsb", 1, [128, 2048], BF16)
    lsb = c.views("lsb", 1, [128, 1], F32)
    nm4 = c.sbuf("nm4", [128, 4], F32)
    l4 = c.sbuf("l4", [128, 4], F32)
    nmax = c.views("nmax", 1, [128, 1], F32)
    ebias = c.views("ebias", 1, [128, 1], F32)
    rsb = c.views("rsb", 1, [128, 1], F32)
    tmpf = c.views("tmpf", 2, [128, 512], F32)
    bv_bc = c.sbuf("bv_bc", [128, D], F32)
    t["ptg0"] = Trk(); t["ptg1"] = Trk()

    ones = c.sb["ones_f"]
    tbv = None
    for hf in range(2 if "bvbc" not in L0_SKIP else 0):
        pvh, trh = ((pv0, t["pv0"]), (pv1, t["pv1"]))[hf]
        bm = p.op("tensor", "matmul", pvh[:, :], ones[0:1, :],
                  c.sb["smrow"][0:1, 512 * hf:512 * (hf + 1)],
                  waits=trh.w_waits(), start=True, stop=True, tag=True)
        trh.wr(bm)
        tbv = p.op("scalar", "copy", out=bv_bc[:, 512 * hf:512 * (hf + 1)], in_=pvh[:, :],
                   waits=[bm], tag=True)
        trh.rd(tbv)
    t["bv_bc"].wr(tbv)

    def load_wsh(src_list):
        tok = None
        for i, src in enumerate(src_list):
            tok = p.dma("gpsimd", wsh[:, 1024 * i:1024 * (i + 1)], src,
                        waits=t["wsh"].w_waits() if i == 0 else ())
        t["wsh"].wr(tok)
        return tok

    for pr in range(NPAIR if "wload" not in L0_SKIP else 0):
        tok = p.dma("gpsimd", wq_sb[:, 1024 * pr:1024 * (pr + 1)], c.hbm["wq"][pr])
        t["wq_sb"].wr(tok)
        tok = p.dma("gpsimd", wk_sb[:, 1024 * pr:1024 * (pr + 1)], c.hbm["wk"][pr])
        t["wk_sb"].wr(tok)
    wq_tok = t["wq_sb"].w
    wk_tok = t["wk_sb"].w

    for s in range(BL):
        wv_tok = None if "wload" in L0_SKIP else load_wsh(
            [c.hbm["wv"][128 * dc:128 * (dc + 1), :] for dc in range(DC)])
        for ch in range(4):
            cc = slice(512 * ch, 512 * (ch + 1))
            if "stats" in L0_SKIP:
                xtoks = []
                for dc in range(DC):
                    par = dc % 2
                    ltk = p.dma("sync", xch[par], x[s, dc, :, cc],
                                waits=t["xch%d" % par].w_waits(), ch=par)
                    t["xch%d" % par].wr(ltk)
                    xnv = xn[:, 512 * dc:512 * (dc + 1)]
                    tk2 = p.op("vector", "tensor_copy", xnv, xch[par],
                               waits=[ltk] + (t["xn"].w_waits() if dc == 0 else []), tag=True)
                    t["xch%d" % par].rd(tk2)
                    xtoks.append(tk2)
                    dtk = p.dma("sync", xn0_s[s, dc, :, cc], xnv, waits=[tk2])
                    t["xn"].rd(dtk)
                t["xn"].wr(xtoks[-1])
            else:
                mm1 = mm2 = None
                for dc in range(DC):
                    par = dc % 2
                    tk = p.dma("sync", xch[par], x[s, dc, :, cc],
                               waits=t["xch%d" % par].w_waits(), ch=par)
                    t["xch%d" % par].wr(tk)
                    stk = p.op("scalar", "activation", tmpf[1], xch[par],
                               AF.Square, waits=[tk] + t["tmpf1"].w_waits(), tag=True)
                    t["tmpf1"].wr(stk); t["xch%d" % par].rd(stk)
                    mm1 = p.op("tensor", "matmul", pv0[0:1, :], ones[:, 0:1], xch[par],
                               waits=[tk] + (t["pv0"].w_waits() if dc == 0 else []),
                               start=(dc == 0), stop=(dc == DC - 1), tag=True)
                    mm2 = p.op("tensor", "matmul", pv1[0:1, :], ones[:, 0:1], tmpf[1],
                               waits=[stk] + (t["pv1"].w_waits() if dc == 0 else []),
                               start=(dc == 0), stop=(dc == DC - 1), tag=True)
                    t["tmpf1"].rd(mm2)
                    t["xch%d" % par].rd(mm1)
                t["pv0"].wr(mm1)
                t["pv1"].wr(mm2)
                mrow = c.sb["mrow"][0:1, :]
                vrow = c.sb["vrow"][0:1, :]
                m2r = c.sb["rowtmp"][0:1, 512:1024]
                tk_m = p.op("vector", "tensor_scalar_mul", mrow, pv0[0:1, :], 1.0 / D,
                            waits=[mm1] + t["mrow"].w_waits(), tag=True)
                t["pv0"].rd(tk_m); t["mrow"].wr(tk_m)
                tk_v = p.op("vector", "tensor_scalar_mul", vrow, pv1[0:1, :], 1.0 / D,
                            waits=[mm2] + t["vrow"].w_waits(), tag=True)
                t["pv1"].rd(tk_v); t["vrow"].wr(tk_v)
                tk_m2 = p.op("vector", "tensor_mul", m2r, mrow, mrow,
                             waits=[tk_m] + t["rowtmp"].w_waits(), tag=True)
                t["rowtmp"].wr(tk_m2)
                tk_v2 = p.op("vector", "tensor_sub", vrow, vrow, m2r,
                             waits=[tk_m2, tk_v], tag=True)
                t["vrow"].wr(tk_v2); t["rowtmp"].rd(tk_v2)
                av = arow[0:1, :]
                bv = brow[0:1, :]
                tk_a = row_recip_sqrt(c, av, vrow, 1e-5, waits=[tk_v2])
                t["vrow"].rd(tk_a); t["arow"].wr(tk_a)
                tk_b = p.op("vector", "scalar_tensor_tensor", bv, mrow, -1.0, av,
                            ALU.mult, ALU.mult, waits=[tk_a, tk_m] + t["brow"].w_waits(),
                            tag=True)
                t["mrow"].rd(tk_b); t["arow"].rd(tk_b); t["brow"].wr(tk_b)
                bca = p.op("tensor", "matmul", pv0[:, :], ones[0:1, :], av,
                           waits=[tk_a] + t["pv0"].w_waits(), start=True, stop=True, tag=True)
                t["arow"].rd(bca); t["pv0"].wr(bca)
                bcb = p.op("tensor", "matmul", pv1[:, :], ones[0:1, :], bv,
                           waits=[tk_b] + t["pv1"].w_waits(), start=True, stop=True, tag=True)
                t["brow"].rd(bcb); t["pv1"].wr(bcb)
                ca = p.op("scalar", "copy", waits=[bca] + t["abc"].w_waits(), tag=True,
                          out=abc, in_=pv0[:, :])
                t["abc"].wr(ca); t["pv0"].rd(ca)
                cb = p.op("scalar", "copy", waits=[bcb] + t["bbc"].w_waits(), tag=True,
                          out=bbc, in_=pv1[:, :])
                t["bbc"].wr(cb); t["pv1"].rd(cb)
                xtoks = []
                dtks = []
                for dc in range(DC):
                    par = dc % 2
                    ltk = p.dma("sync", xch[par], x[s, dc, :, cc],
                                waits=t["xch%d" % par].w_waits(), ch=par)
                    t["xch%d" % par].wr(ltk)
                    xv = xch[par]
                    xnv = xn[:, 512 * dc:512 * (dc + 1)]
                    tk1 = p.op("vector", "tensor_mul", tmpf[0], xv, abc,
                               waits=[ca, ltk] + t["tmpf0"].w_waits(), tag=True)
                    t["tmpf0"].wr(tk1); t["abc"].rd(tk1); t["xch%d" % par].rd(tk1)
                    tk2 = p.op("vector", "tensor_tensor", xnv, tmpf[0], bbc, ALU.add,
                               waits=[tk1, cb] + (t["xn"].w_waits() if dc == 0 else []),
                               tag=True)
                    t["tmpf0"].rd(tk2); t["bbc"].rd(tk2)
                    xtoks.append(tk2)
                    dtk = p.dma("sync", xn0_s[s, dc, :, cc], xnv, waits=[tk2])
                    dtks.append(dtk)
                t["xn"].wr(xtoks[-1])
                for dtk in dtks:
                    t["xn"].rd(dtk)
            for pr in range(NPAIR if "qk" not in L0_SKIP else 0):
                ps_q, trq = (pv0, t["pv0"]) if pr % 2 == 0 else (pv1, t["pv1"])
                mmq = None
                for kc in range(DC):
                    w = [xtoks[kc], wq_tok] + (trq.w_waits() if kc == 0 else [])
                    mmq = p.op("tensor", "matmul", ps_q[:, :],
                               wq_sb[:, 1024 * pr + 128 * kc:1024 * pr + 128 * (kc + 1)],
                               xn[:, 512 * kc:512 * (kc + 1)], waits=w,
                               start=(kc == 0), stop=(kc == DC - 1), tag=(kc == DC - 1))
                    t["xn"].rd(mmq)
                trq.wr(mmq)
                eq = p.op("scalar", "activation",
                          q_sb[:, 2048 * pr + cc.start:2048 * pr + cc.stop], ps_q[:, :],
                          AF.Identity, bias=c.smallc("bq", pr),
                          waits=[mmq] + t["q%d" % pr].w_waits(), tag=True)
                trq.rd(eq); t["q%d" % pr].wr(eq)
                mmk = None
                for kc in range(DC):
                    w = [xtoks[kc], wk_tok] + (trq.w_waits() if kc == 0 else [])
                    mmk = p.op("tensor", "matmul", ps_q[:, :],
                               wk_sb[:, 1024 * pr + 128 * kc:1024 * pr + 128 * (kc + 1)],
                               xn[:, 512 * kc:512 * (kc + 1)], waits=w,
                               start=(kc == 0), stop=(kc == DC - 1), tag=(kc == DC - 1))
                    t["xn"].rd(mmk)
                trq.wr(mmk)
                ek = p.op("scalar", "activation",
                          k_sb[:, 2048 * pr + cc.start:2048 * pr + cc.stop], ps_q[:, :],
                          AF.Identity, bias=c.smallc("bk", pr),
                          waits=[mmk] + t["k%d" % pr].w_waits(), tag=True)
                trq.rd(ek); t["k%d" % pr].wr(ek)
            for t4 in range(4 if "v" not in L0_SKIP else 0):
                tc = 4 * ch + t4
                for half in range(2):
                    i = (t4 * 2 + half) % 4
                    pv = pbig[:, 512 * i:512 * (i + 1)]
                    trv = t["pb%d" % i]
                    mmv = None
                    for kc in range(DC):
                        w = [xtoks[kc], wv_tok] + (trv.w_waits() if kc == 0 else [])
                        mmv = p.op("tensor", "matmul", pv,
                                   xn[:, 512 * kc + 128 * t4:512 * kc + 128 * (t4 + 1)],
                                   wsh[:, 1024 * kc + 512 * half:1024 * kc + 512 * (half + 1)],
                                   waits=w, start=(kc == 0), stop=(kc == DC - 1),
                                   tag=(kc == DC - 1))
                        t["xn"].rd(mmv); t["wsh"].rd(mmv)
                    trv.wr(mmv)
                    ev = p.op("vector", "tensor_tensor",
                              v_sb[:, 1024 * tc + 512 * half:1024 * tc + 512 * (half + 1)],
                              pv, bv_bc[:, 512 * half:512 * (half + 1)], ALU.add,
                              waits=[mmv, tbv] + t["v%d" % tc].w_waits(), tag=True)
                    trv.rd(ev); t["v%d" % tc].wr(ev)

        if DBG_L0 and s == 0:
            for pr in range(NPAIR):
                dtk = p.dma("sync", c.hbm["dbg_q"][:, 2048 * pr:2048 * (pr + 1)],
                            q_sb[:, 2048 * pr:2048 * (pr + 1)], waits=[t["q%d" % pr].w])
                t["q%d" % pr].rd(dtk)
                dtk = p.dma("sync", c.hbm["dbg_k"][:, 2048 * pr:2048 * (pr + 1)],
                            k_sb[:, 2048 * pr:2048 * (pr + 1)], waits=[t["k%d" % pr].w])
                t["k%d" % pr].rd(dtk)
            for tc in range(16):
                dtk = p.dma("sync", c.hbm["dbg_v"][:, 1024 * tc:1024 * (tc + 1)],
                            v_sb[:, 1024 * tc:1024 * (tc + 1)], waits=[t["v%d" % tc].w])
                t["v%d" % tc].rd(dtk)
        prev = None
        NSC = 16
        if "attn" in L0_SKIP:
            NSC = 0
        elif "attn1" in L0_SKIP:
            NSC = 1
        elif "attn4" in L0_SKIP:
            NSC = 4
        elif "attn8" in L0_SKIP:
            NSC = 8
        for sc in range(NSC):
            for h in range(NH):
                pr, h2 = h // 2, h % 2
                par = 0
                nt = sc + 1
                t_len = nt * 128
                qrows = slice(64 * h2, 64 * (h2 + 1))
                smtoks = []
                for jb in range((t_len + 511) // 512):
                    wdt = min(512, t_len - 512 * jb)
                    wts = [t["q%d" % pr].w, t["k%d" % pr].w] + t["pb%d" % jb].w_waits()
                    mm = p.op("tensor", "matmul", pbig[:, 512 * jb:512 * jb + wdt],
                              q_sb[qrows, 2048 * pr + 128 * sc:2048 * pr + 128 * (sc + 1)],
                              k_sb[qrows, 2048 * pr + 512 * jb:2048 * pr + 512 * jb + wdt],
                              waits=wts, start=True, stop=True, tag=True)
                    t["pb%d" % jb].wr(mm)
                    t["q%d" % pr].rd(mm)
                    t["k%d" % pr].rd(mm)
                    smtoks.append(mm)
                if prev is not None:
                    _l0_pt_pv(c, *prev)
                    prev = None
                tk_mask = p.op("vector", "tensor_tensor", pbig[:, t_len - 128:t_len],
                               pbig[:, t_len - 128:t_len], c.sb["mask"], ALU.add,
                               waits=[smtoks[-1]], tag=True)
                t["pb%d" % ((t_len - 1) // 512)].wr(tk_mask)
                njb = (t_len + 511) // 512
                mptoks = []
                for jb in range(njb):
                    wdt = min(512, t_len - 512 * jb)
                    w = [smtoks[jb], tk_mask]
                    if jb == 0:
                        w += t["nm4"].w_waits()
                    tk = p.op("vector", "tensor_reduce", nm4[:, jb:jb + 1],
                              pbig[:, 512 * jb:512 * jb + wdt], AX.X, ALU.max,
                              waits=w, tag=True)
                    t["pb%d" % jb].rd(tk)
                    mptoks.append(tk)
                t["nm4"].wr(mptoks[-1])
                tk_max = p.op("vector", "tensor_reduce", nmax[par], nm4[:, 0:njb],
                              AX.X, ALU.max, negate=True,
                              waits=mptoks + t["nmax" + str(par)].w_waits(), tag=True)
                t["nm4"].rd(tk_max)
                t["nmax" + str(par)].wr(tk_max)
                tk_eb = p.op("vector", "tensor_scalar_mul", ebias[par], nmax[par], 0.125,
                             waits=[tk_max] + t["ebias" + str(par)].w_waits(), tag=True)
                t["ebias" + str(par)].wr(tk_eb); t["nmax" + str(par)].rd(tk_eb)
                eptoks = []
                for jb in range(njb):
                    wdt = min(512, t_len - 512 * jb)
                    w = [tk_eb] + (t["psb%d" % par].w_waits() if jb == 0 else [])
                    if jb == 0:
                        w += t["l4"].w_waits()
                    tk = p.op("scalar", "activation", psb[par][:, 512 * jb:512 * jb + wdt],
                              pbig[:, 512 * jb:512 * jb + wdt], AF.Exp, scale=c.cp("s8"),
                              bias=ebias[par], accum_out=l4[:, jb:jb + 1],
                              waits=w, tag=True)
                    t["pb%d" % jb].rd(tk)
                    eptoks.append(tk)
                tk_exp = eptoks[-1]
                t["ebias" + str(par)].rd(tk_exp)
                t["l4"].wr(tk_exp)
                tk_ls = p.op("vector", "tensor_reduce", lsb[par], l4[:, 0:njb], AX.X,
                             ALU.add, waits=eptoks + t["lsb%d" % par].w_waits(), tag=True)
                t["l4"].rd(tk_ls)
                t["lsb%d" % par].wr(tk_ls)
                tk_r = p.op("vector", "reciprocal", rsb[par], lsb[par],
                            waits=[tk_ls] + t["rsb%d" % par].w_waits(), tag=True)
                t["lsb%d" % par].rd(tk_r); t["rsb%d" % par].wr(tk_r)
                tk_pn = p.op("vector", "tensor_scalar_mul", psb[par][:, 0:t_len],
                             psb[par][:, 0:t_len], rsb[par],
                             waits=eptoks + [tk_r], tag=True)
                t["rsb%d" % par].rd(tk_pn)
                t["psb%d" % par].wr(tk_pn)
                prev = (sc, h, par, nt)
        if prev is not None:
            _l0_pt_pv(c, *prev)

        if DBG_L0 and s == 0:
            for pr in range(NPAIR):
                dtk = p.dma("sync", c.hbm["dbg_o"][:, 2048 * pr:2048 * (pr + 1)],
                            q_sb[:, 2048 * pr:2048 * (pr + 1)], waits=[t["o%d" % pr].w])
                t["o%d" % pr].rd(dtk)
        wo_tok = None if "wload" in L0_SKIP else load_wsh([c.hbm["wo0"][mc] for mc in range(DC)])
        if "outproj" in L0_SKIP and "dummyout" in L0_SKIP:
            zk = p.op("vector", "memset", tmpf[0], 0.0,
                      waits=t["tmpf0"].w_waits(), tag=True)
            t["tmpf0"].wr(zk)
            for mc in range(DC):
                for ch4 in range(4):
                    dtk = p.dma("sync", x1_s[s, mc, :, 512 * ch4:512 * (ch4 + 1)],
                                tmpf[0], waits=[zk])
                    t["tmpf0"].rd(dtk)
        for ch in range(4 if "outproj" not in L0_SKIP else 0):
            cc = slice(512 * ch, 512 * (ch + 1))
            for mc in range(DC):
                xpar = mc % 2
                ltk = p.dma("sync", xch[xpar], x[s, mc, :, cc],
                            waits=t["xch%d" % xpar].w_waits())
                t["xch%d" % xpar].wr(ltk)
                po, tro = ((pv0, t["pv0"]), (pv1, t["pv1"]), (pv2, t["pv2"]))[mc % 3]
                mmo = None
                for kc in range(DC):
                    w = [t["o%d" % kc].w, wo_tok] + (tro.w_waits() if kc == 0 else [])
                    mmo = p.op("tensor", "matmul", po[:, :],
                               wsh[:, 1024 * mc + 128 * kc:1024 * mc + 128 * (kc + 1)],
                               q_sb[:, 2048 * kc + cc.start:2048 * kc + cc.stop],
                               waits=w, start=(kc == 0), stop=(kc == DC - 1),
                               tag=(kc == DC - 1))
                    t["o%d" % kc].rd(mmo)
                    t["wsh"].rd(mmo)
                tro.wr(mmo)
                par = mc % 2
                tb = p.op("scalar", "activation", tmpf[par], po[:, :], AF.Identity,
                          bias=c.smallc("bo0", mc),
                          waits=[mmo] + t["tmpf%d" % par].w_waits(), tag=True)
                tro.rd(tb); t["tmpf%d" % par].wr(tb)
                ta = p.op("vector", "tensor_tensor", tmpf[par], tmpf[par],
                          xch[xpar], ALU.add,
                          waits=[tb, ltk], tag=True)
                t["xch%d" % xpar].rd(ta); t["tmpf%d" % par].wr(ta)
                dtk = p.dma("sync", x1_s[s, mc, :, cc], tmpf[par], waits=[ta])
                t["tmpf%d" % par].rd(dtk)
    c.pop()


def _l0_pt_pv(c, sc, h, par, nt):
    p = c.p
    t = c.trk
    pr, h2 = h // 2, h % 2
    pt = c.sb["pt"]
    pv2 = c.sb["pv2"]
    ptv = c.sb["ptsb"][:, 2048 * par:2048 * (par + 1)] if False else None
    ptv = c.sb["ptsb"][:, 2048 * par:2048 * (par + 1)]
    ptrk = t["ptsb%d" % par]
    pn_tok = t["psb%d" % par].w
    psbv = c.sb["psb"][:, 2048 * par:2048 * (par + 1)]
    groups = []
    for lo in range(0, nt, 4):
        hi = min(lo + 4, nt) - 1
        gtrk = t["pt"]
        tk = None
        for tb in range(lo, hi + 1):
            slot = (tb % 8)
            wts = [pn_tok]
            if tb == lo:
                wts += gtrk.w_waits()
            tk = p.op("tensor", "transpose", pt[:, 128 * slot:128 * (slot + 1)],
                      psbv[:, 128 * tb:128 * (tb + 1)], c.sb["ident_b"],
                      waits=wts, tag=(tb == hi))
        gtrk.wr(tk)
        t["psb%d" % par].rd(tk)
        wdt = (hi - lo + 1) * 128
        ck = p.op("vector", "tensor_copy",
                  ptv[:, 128 * lo:128 * lo + wdt],
                  pt[:, 128 * (lo % 8):128 * (lo % 8) + wdt],
                  waits=[tk] + (ptrk.w_waits() if lo == 0 else []), tag=True)
        gtrk.rd(ck)
        groups.append((lo, hi, ck))
    ptrk.wr(groups[-1][2])
    out = pv2[slice(64 * h2, 64 * (h2 + 1)), 0:128]
    first = True
    mm = None
    for lo, hi, ck in groups:
        for tb in range(lo, hi + 1):
            wts = [ck, t["v%d" % tb].w]
            if first:
                wts += t["pv2"].w_waits()
            mm = p.op("tensor", "matmul", out,
                      c.sb["v_sb"][:, 1024 * tb + 64 * h:1024 * tb + 64 * (h + 1)],
                      ptv[:, 128 * tb:128 * (tb + 1)],
                      waits=wts, start=first, stop=(tb == nt - 1), tag=(tb == nt - 1))
            t["v%d" % tb].rd(mm)
            ptrk.rd(mm)
            first = False
    t["pv2"].wr(mm)
    eo = p.op("scalar", "activation",
              c.sb["q_sb"][slice(64 * h2, 64 * (h2 + 1)),
                           2048 * pr + 128 * sc:2048 * pr + 128 * (sc + 1)],
              out, AF.Copy, waits=[mm] + t["q%d" % pr].w_waits(), tag=True)
    t["pv2"].rd(eo)
    t["o%d" % pr].wr(eo)


def alloc_psum_pool(c):
    c.psum("pbig", [128, 2048], F32)
    for i in range(4):
        c.trk["pb%d" % i] = Trk()
    c.psum("pv0", [128, 512], F32)
    c.psum("pv1", [128, 512], F32)
    c.psum("pv2", [128, 512], F32)
    c.psum("pt", [128, 1024], BF16)



def emit_ffn0(c):
    p = c.p
    t = c.trk
    xn0_s = c.hbm["xn0_s"]
    x1_s = c.hbm["x1_s"]
    xnew_s = c.hbm["xnew_s"]
    xn2_s = c.hbm["xn2_s"]
    c.push()

    pbig = c.sb["pbig"]
    pv0, pv1, pv2 = c.sb["pv0"], c.sb["pv1"], c.sb["pv2"]

    wgb = c.sbuf("wgb", [128, 24 * 1024], BF16)
    wub = c.sbuf("wub", [128, 24 * 1024], BF16)
    wdb = c.sbuf("wdb", [128, 8 * 3072], BF16)
    c.push()
    stg = c.views("stg", 2, [128, 3072], F32)
    DQ = 1.0 / 127.0
    jobs = ([("wg0", wgb, mc, 1024) for mc in range(24)]
            + [("wu0", wub, mc, 1024) for mc in range(24)]
            + [("wd0", wdb, mc, 3072) for mc in range(8)])
    for j, (src, dst, mc, wdt) in enumerate(jobs):
        par = j % 2
        sv = stg[par][:, 0:wdt]
        dtk = p.dma("gpsimd", sv, c.hbm[src][mc], waits=t["stg%d" % par].w_waits(), ch=par)
        t["stg%d" % par].wr(dtk)
        ck = p.op("scalar", "mul", dst[:, wdt * mc:wdt * (mc + 1)], sv, c.cp("dq"),
                  waits=[dtk], tag=True)
        t["stg%d" % par].rd(ck)
        t[{id(wgb): "wgb", id(wub): "wub", id(wdb): "wdb"}[id(dst)]].wr(ck)
    c.pop()
    wg_tok, wu_tok, wd_tok = t["wgb"].w, t["wub"].w, t["wdb"].w

    xnch = c.sbuf("xnch", [128, DC * 256], BF16)
    h_sb = c.sbuf("h_sb", [128, 24 * 256], BF16)
    x1ch = c.views("x1ch", 2, [128, 256], F32)
    xnt = c.views("xnt", 2, [128, 256], F32)
    xnbf = c.sbuf("xnbf", [128, DC * 256], BF16)
    tmpf = c.views("tmpf", 2, [128, 256], F32)
    abc = c.sbuf("abc", [128, 256], F32)
    bbc = c.sbuf("bbc", [128, 256], F32)
    arow = c.sbuf("arow", [1, 256], F32)
    brow = c.sbuf("brow", [1, 256], F32)
    mrow = c.sbuf("mrow", [1, 256], F32)
    vrow = c.sbuf("vrow", [1, 256], F32)
    c.sbuf("rowtmp", [1, 512], F32)

    for s in range(BL):
        for ch in range(8):
            cc = slice(256 * ch, 256 * (ch + 1))
            ltoks = []
            for dc in range(DC):
                tk = p.dma("sync", xnch[:, 256 * dc:256 * (dc + 1)], xn0_s[s, dc, :, cc],
                           waits=(t["xnch"].w_waits() if dc == 0 else ()))
                ltoks.append(tk)
            t["xnch"].wr(ltoks[-1])
            for mc in range(24):
                pg = pbig[:, 512 * (mc % 2):512 * (mc % 2) + 256]
                trg = t["pb%d" % (mc % 2)]
                pu = pbig[:, 1024 + 256 * (mc % 2):1024 + 256 * (mc % 2) + 256]
                tru = t["pb2"]
                mmg = mmu = None
                for kc in range(DC):
                    w = [ltoks[kc], wg_tok] + (trg.w_waits() if kc == 0 else [])
                    mmg = p.op("tensor", "matmul", pg,
                               wgb[:, 1024 * mc + 128 * kc:1024 * mc + 128 * (kc + 1)],
                               xnch[:, 256 * kc:256 * (kc + 1)], waits=w,
                               start=(kc == 0), stop=(kc == DC - 1), tag=(kc == DC - 1))
                    t["xnch"].rd(mmg)
                    w = [ltoks[kc], wu_tok] + (tru.w_waits() if kc == 0 else [])
                    mmu = p.op("tensor", "matmul", pu,
                               wub[:, 1024 * mc + 128 * kc:1024 * mc + 128 * (kc + 1)],
                               xnch[:, 256 * kc:256 * (kc + 1)], waits=w,
                               start=(kc == 0), stop=(kc == DC - 1), tag=(kc == DC - 1))
                    t["xnch"].rd(mmu)
                trg.wr(mmg)
                tru.wr(mmu)
                par = mc % 2
                ts1 = p.op("scalar", "activation", tmpf[par], pg, AF.Silu,
                           waits=[mmg] + t["tmpf%d" % par].w_waits(), tag=True)
                trg.rd(ts1); t["tmpf%d" % par].wr(ts1)
                th = p.op("vector", "tensor_tensor", h_sb[:, 256 * mc:256 * (mc + 1)],
                          tmpf[par], pu, ALU.mult,
                          waits=[ts1, mmu] + (t["h_sb"].w_waits() if mc == 0 else []),
                          tag=True)
                t["tmpf%d" % par].rd(th); tru.rd(th)
            t["h_sb"].wr(th)
            stat_first = True
            smm = None
            for mc2 in range(DC):
                par = mc2 % 2
                pd = (pv0, pv1)[par]
                trd = t["pv%d" % par]
                mmd = None
                for kc in range(24):
                    w = [th, wd_tok] + (trd.w_waits() if kc == 0 else [])
                    mmd = p.op("tensor", "matmul", pd[:, 0:256],
                               wdb[:, 3072 * mc2 + 128 * kc:3072 * mc2 + 128 * (kc + 1)],
                               h_sb[:, 256 * kc:256 * (kc + 1)], waits=w,
                               start=(kc == 0), stop=(kc == 23), tag=(kc == 23))
                    t["h_sb"].rd(mmd)
                trd.wr(mmd)
                xl = p.dma("sync", x1ch[par], x1_s[s, mc2, :, cc],
                           waits=t["x1ch%d" % par].w_waits(), ch=par)
                t["x1ch%d" % par].wr(xl)
                ta = p.op("vector", "tensor_tensor", xnt[par], pd[:, 0:256], x1ch[par],
                          ALU.add, waits=[mmd, xl] + t["xnt%d" % par].w_waits(), tag=True)
                trd.rd(ta); t["x1ch%d" % par].rd(ta); t["xnt%d" % par].wr(ta)
                dnew = p.dma("sync", xnew_s[s, mc2, :, cc], xnt[par], waits=[ta])
                t["xnt%d" % par].rd(dnew)
                cbf = p.op("vector", "tensor_copy", xnbf[:, 256 * mc2:256 * (mc2 + 1)],
                           xnt[par], waits=[ta] + (t["xnbf"].w_waits() if mc2 == 0 else []),
                           tag=True)
                t["xnt%d" % par].rd(cbf)
                sk = p.op("scalar", "activation", tmpf[par], xnt[par], AF.Square,
                          waits=[ta] + t["tmpf%d" % par].w_waits(), tag=True)
                t["xnt%d" % par].rd(sk); t["tmpf%d" % par].wr(sk)
                w = [ta] + (t["pb3"].w_waits() if stat_first else [])
                smm = p.op("tensor", "matmul", pbig[:, 1536:1537][0:1, :].tensor[0:1, 1536:1792]
                           if False else pbig[0:1, 1536:1792], c.sb["ones_f"][:, 0:1],
                           xnt[par], waits=w, start=stat_first, stop=False, tag=True)
                t["xnt%d" % par].rd(smm)
                stat_first = False
                smm = p.op("tensor", "matmul", pbig[0:1, 1792:2048], c.sb["ones_f"][:, 0:1],
                           tmpf[par], start=False, stop=(mc2 == DC - 1),
                           waits=[sk], tag=True)
                t["tmpf%d" % par].rd(smm)
            t["pb3"].wr(smm)
            t["xnbf"].wr(cbf)
            tk_m = p.op("vector", "tensor_scalar_mul", mrow[0:1, :], pbig[0:1, 1536:1792],
                        1.0 / D, waits=[smm] + t["mrow"].w_waits(), tag=True)
            t["pb3"].rd(tk_m); t["mrow"].wr(tk_m)
            tk_v = p.op("vector", "tensor_scalar_mul", vrow[0:1, :], pbig[0:1, 1792:2048],
                        1.0 / D, waits=[smm] + t["vrow"].w_waits(), tag=True)
            t["pb3"].rd(tk_v); t["vrow"].wr(tk_v)
            m2r = c.sb["rowtmp"][0:1, 256:512]
            tk_m2 = p.op("vector", "tensor_mul", m2r, mrow[0:1, :], mrow[0:1, :],
                         waits=[tk_m] + t["rowtmp"].w_waits(), tag=True)
            t["rowtmp"].wr(tk_m2)
            tk_v2 = p.op("vector", "tensor_sub", vrow[0:1, :], vrow[0:1, :], m2r,
                         waits=[tk_m2, tk_v], tag=True)
            t["vrow"].wr(tk_v2); t["rowtmp"].rd(tk_v2)
            tk_a = row_recip_sqrt(c, arow[0:1, :], vrow[0:1, :], 1e-5, waits=[tk_v2])
            t["vrow"].rd(tk_a); t["arow"].wr(tk_a)
            tk_b = p.op("vector", "scalar_tensor_tensor", brow[0:1, :], mrow[0:1, :],
                        -1.0, arow[0:1, :], ALU.mult, ALU.mult,
                        waits=[tk_a, tk_m] + t["brow"].w_waits(), tag=True)
            t["mrow"].rd(tk_b); t["arow"].rd(tk_b); t["brow"].wr(tk_b)
            bca = p.op("tensor", "matmul", pv2[:, 0:256], c.sb["ones_f"][0:1, :],
                       arow[0:1, :], waits=[tk_a] + t["pv2"].w_waits(),
                       start=True, stop=True, tag=True)
            t["arow"].rd(bca); t["pv2"].wr(bca)
            bcb = p.op("tensor", "matmul", pbig[:, 1536:1792], c.sb["ones_f"][0:1, :],
                       brow[0:1, :], waits=[tk_b] + t["pb3"].w_waits(),
                       start=True, stop=True, tag=True)
            t["brow"].rd(bcb); t["pb3"].wr(bcb)
            ca = p.op("scalar", "copy", out=abc, in_=pv2[:, 0:256],
                      waits=[bca] + t["abc"].w_waits(), tag=True)
            t["pv2"].rd(ca); t["abc"].wr(ca)
            cb = p.op("scalar", "copy", out=bbc, in_=pbig[:, 1536:1792],
                      waits=[bcb] + t["bbc"].w_waits(), tag=True)
            t["pb3"].rd(cb); t["bbc"].wr(cb)
            dtks2 = []
            for mc2 in range(DC):
                par = mc2 % 2
                tk1 = p.op("vector", "tensor_mul", tmpf[par],
                           xnbf[:, 256 * mc2:256 * (mc2 + 1)], abc,
                           waits=[ca, cbf] + t["tmpf%d" % par].w_waits(), tag=True)
                t["abc"].rd(tk1); t["xnbf"].rd(tk1); t["tmpf%d" % par].wr(tk1)
                tk2 = p.op("vector", "tensor_tensor", xnch[:, 256 * mc2:256 * (mc2 + 1)],
                           tmpf[par], bbc, ALU.add,
                           waits=[tk1, cb] + (t["xnch"].w_waits() if mc2 == 0 else []),
                           tag=True)
                t["tmpf%d" % par].rd(tk2); t["bbc"].rd(tk2)
                dtk = p.dma("sync", xn2_s[s, mc2, :, cc],
                            xnch[:, 256 * mc2:256 * (mc2 + 1)], waits=[tk2])
                dtks2.append(dtk)
            t["xnch"].wr(tk2)
            for dtk in dtks2:
                t["xnch"].rd(dtk)
    c.pop()



_PHASES = {}


def _register():
    _PHASES["l0"] = emit_l0
    _PHASES["ffn0"] = emit_ffn0
    for nme in ("l1ab", "l1c", "l1ffn"):
        if nme not in _PHASES and ("emit_" + nme) in globals():
            _PHASES[nme] = globals()["emit_" + nme]


WEIGHT_TENS = {
    "wq": F32, "wk": F32, "wv": F32, "wo0": F32,
    "wg0": I32, "wu0": I32, "wd0": I32,
    "w1": F32, "w2": F32, "wz": F32, "wh": F32, "wo1": F32,
    "ws1": F32, "w1f": F32, "w2f": F32, "ws2": F32, "w1s": F32, "w2s": F32,
    "smalls": F32, "smrow": F32,
}

SCRATCH = {
    "xn0_s": ([BL, DC, 128, S], BF16, "l0", ("ffn0",)),
    "x1_s": ([BL, DC, 128, S], F32, "l0", ("ffn0",)),
    "xnew_s": ([BL, DC, 128, S], F32, "ffn0", ("l1c",)),
    "xn2_s": ([BL, DC, 128, S], BF16, "ffn0", ("l1ab", "l1ffn")),
    "xskip_s": ([32, 128, BL, S], BF16, "l1ab", ("l1c",)),
    "h_s": ([32, 128, BL, S], BF16, "l1ab", ("l1c",)),
    "acc_s": ([BL, DC, 128, S], F32, "l1c", ("l1ffn",)),
}

PHASE_USED_W = {
    "l0": ("wq", "wk", "wv", "wo0"),
    "ffn0": ("wg0", "wu0", "wd0"),
    "l1ab": ("w1", "w2", "wz", "wh"),
    "l1c": ("wo1",),
    "l1ffn": ("ws1", "w1f", "w2f", "ws2", "w1s", "w2s"),
}

ALL_PHASES = ("l0", "ffn0", "l1ab", "l1c", "l1ffn")


def build_group(group, prep, final=False):
    _register()
    nc = bass.Bass("TRN2", target_bir_lowering=False, debug=False)
    p = Prog(nc)
    gstack = ExitStack()
    c = Ctx(nc, p, gstack)
    c.sm = prep["smmap"]
    in_names, out_names = [], []

    def need_x(g):
        return "l0" in g

    with gstack:
        if need_x(group):
            c.hbm["x"] = nc.dram_tensor("h_x", [BL, DC, 128, S], F32, kind="ExternalInput").ap()
            in_names.append("x")
        for w in ("smalls", "smrow"):
            arr = prep[w]
            c.hbm[w] = nc.dram_tensor("h_" + w, list(arr.shape), F32, kind="ExternalInput").ap()
            in_names.append(w)
        for ph in group:
            for w in PHASE_USED_W[ph]:
                arr = prep[w]
                c.hbm[w] = nc.dram_tensor("h_" + w, list(arr.shape), WEIGHT_TENS[w],
                                          kind="ExternalInput").ap()
                in_names.append(w)
        for nme, (shp, dt, prod, cons) in SCRATCH.items():
            inside_p = prod in group
            used = [x for x in cons if x in group]
            later = [x for x in cons if x not in group and
                     ALL_PHASES.index(x) > max(ALL_PHASES.index(g) for g in group)]
            if inside_p and not later:
                kind = "Internal"
            elif inside_p and later:
                kind = "ExternalOutput"
            elif used:
                kind = "ExternalInput"
            else:
                continue
            c.hbm[nme] = nc.dram_tensor("h_" + nme, shp, dt, kind=kind).ap()
            if kind == "ExternalInput":
                in_names.append(nme)
            elif kind == "ExternalOutput":
                out_names.append(nme)
        if DBG_L0 and "l0" in group:
            for nme in ("dbg_q", "dbg_k", "dbg_v", "dbg_o"):
                c.hbm[nme] = nc.dram_tensor("h_" + nme, [128, 16384], BF16,
                                            kind="ExternalOutput").ap()
                out_names.append(nme)
        if final or group[-1] == "l1ffn":
            c.hbm["y"] = nc.dram_tensor("h_y", [BL, DC, 128, S], F32,
                                        kind="ExternalOutput").ap()
            out_names.append("y")
        alloc_psum_pool(c)
        emit_consts(c)
        for ph in group:
            _PHASES[ph](c)
        p.finish()
        p.replay()
    return nc, in_names, out_names


def _core_x(inputs, ci):
    xs = inputs["L0_attn_x"][BL * ci:BL * (ci + 1)]
    xs = np.ascontiguousarray(np.transpose(xs, (0, 2, 1)))
    return xs.reshape(BL, DC, 128, S)


GROUPS = [("l0", "ffn0", "l1ab", "l1c", "l1ffn")]


def kernel(**inputs):
    prep = prep_inputs(inputs)
    state = [{} for _ in range(NCORES)]
    for ci in range(NCORES):
        state[ci]["x"] = _core_x(inputs, ci)
    out = None
    for gi, group in enumerate(GROUPS):
        nc, in_names, out_names = build_group(group, prep)
        in_maps = []
        for ci in range(NCORES):
            m = {}
            for nme in in_names:
                if nme == "x":
                    m["h_x"] = state[ci]["x"]
                elif nme in prep:
                    m["h_" + nme] = prep[nme]
                else:
                    m["h_" + nme] = state[ci][nme]
            in_maps.append(m)
        res = run_bass_kernel_spmd(nc, in_maps, core_ids=list(range(NCORES)))
        for ci in range(NCORES):
            for nme in out_names:
                state[ci][nme] = np.asarray(res.results[ci]["h_" + nme])
    ys = []
    for ci in range(NCORES):
        y = state[ci]["y"].reshape(BL, D, S)
        ys.append(np.transpose(y, (0, 2, 1)))
    return np.ascontiguousarray(np.concatenate(ys, 0)).astype(np.float32)



def emit_l1ab(c):
    p = c.p
    t = c.trk
    xn2_s = c.hbm["xn2_s"]
    xskip_s = c.hbm["xskip_s"]
    h_s = c.hbm["h_s"]
    c.push()

    pbig = c.sb["pbig"]
    pv0, pv1, pv2 = c.sb["pv0"], c.sb["pv1"], c.sb["pv2"]

    x_in = c.sbuf("x_in", [128, 32 * 2048], BF16)
    for m in range(32):
        t["xin%d" % m] = Trk()
    carry = c.sbuf("carry", [128, 64], F32)
    nbz = c.sbuf("nbz", [128, 32], F32)
    tnb = p.op("vector", "tensor_scalar_mul", nbz, c.smallc("bz", 0, 32), -1.0, tag=True)
    t["nbz"].wr(tnb)

    for ch in range(2):
        tks = slice(1024 * ch, 1024 * (ch + 1))
        c.push()
        xn2ch = c.sbuf("xn2ch", [128, DC * 2048], BF16)
        w1t = c.views("w1t", 2, [128, 1024], BF16)
        w2t = c.views("w2t", 2, [128, 1024], BF16)
        xsk = c.views("xsk", 2, [128, 2048], BF16)
        ltoks = []
        for dc in range(DC):
            for sq in range(BL):
                tk = p.dma("sync", xn2ch[:, 2048 * dc + 1024 * sq:2048 * dc + 1024 * (sq + 1)],
                           xn2_s[sq, dc, :, tks],
                           waits=(t["xn2ch"].w_waits() if dc == 0 and sq == 0 else ()))
            ltoks.append(tk)
        t["xn2ch"].wr(tk)
        lt = ltoks[-1]
        for mp in range(32):
            par = mp % 2
            tw1 = p.dma("gpsimd", w1t[par], c.hbm["w1"][mp], waits=t["w1t%d" % par].w_waits(), ch=par)
            t["w1t%d" % par].wr(tw1)
            tw2 = p.dma("gpsimd", w2t[par], c.hbm["w2"][mp], waits=t["w2t%d" % par].w_waits(), ch=par)
            t["w2t%d" % par].wr(tw2)
            gmm = [None] * 4
            umm = [None] * 4
            spar = mp % 2
            for kc in range(DC):
                for tt in range(4):
                    pg = pbig[:, 512 * tt:512 * (tt + 1)]
                    trg = t["pb%d" % tt]
                    w = [lt, tw1] + (trg.w_waits() if kc == 0 else [])
                    gmm[tt] = p.op("tensor", "matmul", pg,
                                   w1t[par][:, 128 * kc:128 * (kc + 1)],
                                   xn2ch[:, 2048 * kc + 512 * tt:2048 * kc + 512 * (tt + 1)],
                                   waits=w, start=(kc == 0), stop=(kc == DC - 1),
                                   tag=(kc == DC - 1))
                    t["xn2ch"].rd(gmm[tt])
                    t["w1t%d" % par].rd(gmm[tt])
            for tt in range(4):
                pg = pbig[:, 512 * tt:512 * (tt + 1)]
                trg = t["pb%d" % tt]
                trg.wr(gmm[tt])
                ev = p.op("scalar", "activation",
                          x_in[:, 2048 * mp + 512 * tt:2048 * mp + 512 * (tt + 1)], pg,
                          AF.Silu, bias=c.smallc("b1", mp),
                          waits=[gmm[tt]] + (t["xin%d" % mp].w_waits() if tt == 0 else []),
                          tag=True)
                trg.rd(ev)
                if tt == 3:
                    t["xin%d" % mp].wr(ev)
            for tt in range(4):
                pu = pv0 if tt % 2 == 0 else pv1
                tru = t["pv0"] if tt % 2 == 0 else t["pv1"]
                mm2 = None
                for kc in range(DC):
                    w = [lt, tw2] + (tru.w_waits() if kc == 0 else [])
                    mm2 = p.op("tensor", "matmul", pu[:, :],
                               w2t[par][:, 128 * kc:128 * (kc + 1)],
                               xn2ch[:, 2048 * kc + 512 * tt:2048 * kc + 512 * (tt + 1)],
                               waits=w, start=(kc == 0), stop=(kc == DC - 1),
                               tag=(kc == DC - 1))
                    t["xn2ch"].rd(mm2)
                    t["w2t%d" % par].rd(mm2)
                tru.wr(mm2)
                ev2 = p.op("scalar", "activation", xsk[spar][:, 512 * tt:512 * (tt + 1)],
                           pu[:, :], AF.Silu, bias=c.smallc("b2", mp),
                           waits=[mm2] + (t["xsk%d" % spar].w_waits() if tt == 0 else []),
                           tag=True)
                tru.rd(ev2)
            t["xsk%d" % spar].wr(ev2)
            for sq in range(BL):
                dtk = p.dma("sync", xskip_s[mp, :, sq, tks],
                            xsk[spar][:, 1024 * sq:1024 * (sq + 1)], waits=[ev2])
                t["xsk%d" % spar].rd(dtk)
        c.pop()
        c.push()
        wzt = c.views("wzt", 2, [128, 4096], BF16)
        wht = c.views("wht", 2, [128, 4096], BF16)
        a_sb = c.sbuf("a_sb", [128, 2048], F32)
        z_sb = c.sbuf("z_sb", [128, 2048], F32)
        b_sb = c.sbuf("b_sb", [128, 2048], F32)
        h_bf = c.sbuf("h_bf", [128, 2048], BF16)
        for m in range(32):
            par = m % 2
            twz = p.dma("gpsimd", wzt[par], c.hbm["wz"][m], waits=t["wzt%d" % par].w_waits(), ch=par)
            t["wzt%d" % par].wr(twz)
            twh = p.dma("gpsimd", wht[par], c.hbm["wh"][m], waits=t["wht%d" % par].w_waits(), ch=par)
            t["wht%d" % par].wr(twh)
            atoks = []
            zmm = [None] * 4
            for kc in range(32):
                for tt in range(4):
                    pz = pbig[:, 512 * tt:512 * (tt + 1)]
                    trz = t["pb%d" % tt]
                    w = [t["xin%d" % kc].w, twz] + (trz.w_waits() if kc == 0 else [])
                    zmm[tt] = p.op("tensor", "matmul", pz,
                                   wzt[par][:, 128 * kc:128 * (kc + 1)],
                                   x_in[:, 2048 * kc + 512 * tt:2048 * kc + 512 * (tt + 1)],
                                   waits=w, start=(kc == 0), stop=(kc == 31),
                                   tag=(kc == 31))
                    t["xin%d" % kc].rd(zmm[tt])
                    t["wzt%d" % par].rd(zmm[tt])
            for tt in range(4):
                pz = pbig[:, 512 * tt:512 * (tt + 1)]
                trz = t["pb%d" % tt]
                mm = zmm[tt]
                trz.wr(mm)
                ea = p.op("scalar", "activation", a_sb[:, 512 * tt:512 * (tt + 1)], pz,
                          AF.Sigmoid, scale=c.cp("neg1"), bias=nbz[:, m:m + 1],
                          waits=[mm, tnb] + (t["a_sb"].w_waits() if tt == 0 else []),
                          tag=True)
                trz.rd(ea)
                ez = p.op("scalar", "activation", z_sb[:, 512 * tt:512 * (tt + 1)], pz,
                          AF.Sigmoid, bias=c.smallc("bz", m),
                          waits=[mm] + (t["z_sb"].w_waits() if tt == 0 else []), tag=True)
                trz.rd(ez)
                atoks.append((ea, ez))
            t["a_sb"].wr(atoks[-1][0])
            t["z_sb"].wr(atoks[-1][1])
            btoks = []
            hmm4 = [None] * 4
            for kc in range(32):
                for tt in range(4):
                    ph = pbig[:, 512 * tt:512 * (tt + 1)]
                    trh = t["pb%d" % tt]
                    w = [t["xin%d" % kc].w, twh] + (trh.w_waits() if kc == 0 else [])
                    hmm4[tt] = p.op("tensor", "matmul", ph,
                                    wht[par][:, 128 * kc:128 * (kc + 1)],
                                    x_in[:, 2048 * kc + 512 * tt:2048 * kc + 512 * (tt + 1)],
                                    waits=w, start=(kc == 0), stop=(kc == 31),
                                    tag=(kc == 31))
                    t["wht%d" % par].rd(hmm4[tt])
                    t["xin%d" % kc].rd(hmm4[tt])
            for tt in range(4):
                ph = pbig[:, 512 * tt:512 * (tt + 1)]
                trh = t["pb%d" % tt]
                mm = hmm4[tt]
                trh.wr(mm)
                eb = p.op("vector", "scalar_tensor_tensor",
                          b_sb[:, 512 * tt:512 * (tt + 1)], ph[:, :],
                          c.smallc("bh", m), z_sb[:, 512 * tt:512 * (tt + 1)],
                          ALU.add, ALU.mult,
                          waits=[mm, atoks[tt][1]] + (t["b_sb"].w_waits() if tt == 0 else []),
                          tag=True)
                trh.rd(eb)
                t["z_sb"].rd(eb)
                btoks.append(eb)
            t["b_sb"].wr(btoks[-1])
            stoks = []
            for sq in range(BL):
                cin = carry[:, 2 * m + sq:2 * m + sq + 1]
                init = 0.0 if ch == 0 else cin
                w = [atoks[-1][0], btoks[-1]] + ([t["carry"].w] if ch > 0 else [])
                w += t["h_bf"].w_waits() if sq == 0 else []
                sk = p.op("vector", "tensor_tensor_scan",
                          h_bf[:, 1024 * sq:1024 * (sq + 1)],
                          a_sb[:, 1024 * sq:1024 * (sq + 1)],
                          b_sb[:, 1024 * sq:1024 * (sq + 1)],
                          init, ALU.mult, ALU.add, waits=w, tag=True)
                t["a_sb"].rd(sk)
                t["b_sb"].rd(sk)
                if ch > 0:
                    t["carry"].rd(sk)
                stoks.append(sk)
            t["h_bf"].wr(stoks[-1])
            if ch == 0:
                for sq in range(BL):
                    ck = p.op("vector", "tensor_copy",
                              carry[:, 2 * m + sq:2 * m + sq + 1],
                              h_bf[:, 1024 * (sq + 1) - 1:1024 * (sq + 1)],
                              waits=[stoks[sq]], tag=True)
                    t["h_bf"].rd(ck)
                    t["carry"].wr(ck)
            for sq in range(BL):
                dtk = p.dma("sync", h_s[m, :, sq, tks],
                            h_bf[:, 1024 * sq:1024 * (sq + 1)], waits=[stoks[-1]])
                t["h_bf"].rd(dtk)
        c.pop()
    c.pop()



def emit_l1c(c):
    p = c.p
    t = c.trk
    xskip_s = c.hbm["xskip_s"]
    h_s = c.hbm["h_s"]
    xnew_s = c.hbm["xnew_s"]
    acc_s = c.hbm["acc_s"]
    c.push()
    pv0, pv1, pv2 = c.sb["pv0"], c.sb["pv1"], c.sb["pv2"]

    wo1t = c.sbuf("wo1t", [128, DC * 4096], BF16)
    for mc in range(DC):
        tok = p.dma("gpsimd", wo1t[:, 4096 * mc:4096 * (mc + 1)], c.hbm["wo1"][mc])
    t["wo1t"].wr(tok)
    wo_tok = tok
    ht = c.sbuf("ht", [128, 32 * 512], BF16)
    xst = c.sbuf("xst", [128, 32 * 512], BF16)
    xnt = c.views("xnt", 2, [128, 512], F32)
    tmpe = c.views("tmpe", 2, [128, 512], F32)

    for g8 in range(8):
        sq, tk0 = g8 // 4, 512 * (g8 % 4)
        tcol = slice(tk0, tk0 + 512)
        htoks = []
        for m in range(32):
            tk1 = p.dma("sync", ht[:, 512 * m:512 * (m + 1)], h_s[m, :, sq, tcol],
                        waits=(t["ht"].w_waits() if m == 0 else ()))
            tk2 = p.dma("sync", xst[:, 512 * m:512 * (m + 1)], xskip_s[m, :, sq, tcol],
                        waits=(t["xst"].w_waits() if m == 0 else ()))
            htoks.append((tk1, tk2))
        t["xst"].wr(tk2)
        stoks = []
        for m in range(32):
            sk = p.op("vector", "tensor_tensor", ht[:, 512 * m:512 * (m + 1)],
                      ht[:, 512 * m:512 * (m + 1)], xst[:, 512 * m:512 * (m + 1)],
                      ALU.add, waits=[htoks[m][0], htoks[m][1]] +
                      (t["ht"].w_waits() if m == 0 else []), tag=True)
            t["xst"].rd(sk)
            stoks.append(sk)
        t["ht"].wr(stoks[-1])
        for mc in range(DC):
            po, tro = ((pv0, t["pv0"]), (pv1, t["pv1"]), (pv2, t["pv2"]))[mc % 3]
            mm = None
            for kc in range(32):
                w = [stoks[kc], wo_tok] + (tro.w_waits() if kc == 0 else [])
                mm = p.op("tensor", "matmul", po[:, :],
                          wo1t[:, 4096 * mc + 128 * kc:4096 * mc + 128 * (kc + 1)],
                          ht[:, 512 * kc:512 * (kc + 1)],
                          waits=w, start=(kc == 0), stop=(kc == 31), tag=(kc == 31))
                t["ht"].rd(mm)
                t["wo1t"].rd(mm)
            tro.wr(mm)
            par = mc % 2
            te = p.op("scalar", "activation", tmpe[par], po[:, :], AF.Identity,
                      bias=c.smallc("bo1", mc),
                      waits=[mm] + t["tmpe%d" % par].w_waits(), tag=True)
            tro.rd(te)
            t["tmpe%d" % par].wr(te)
            xl = p.dma("sync", xnt[par], xnew_s[sq, mc, :, tcol],
                       waits=t["xnt%d" % par].w_waits(), ch=par)
            t["xnt%d" % par].wr(xl)
            ta = p.op("vector", "tensor_tensor", tmpe[par], tmpe[par], xnt[par],
                      ALU.add, waits=[te, xl], tag=True)
            t["xnt%d" % par].rd(ta)
            t["tmpe%d" % par].wr(ta)
            dtk = p.dma("sync", acc_s[sq, mc, :, tcol], tmpe[par], waits=[ta])
            t["tmpe%d" % par].rd(dtk)
    c.pop()



def emit_l1ffn(c):
    p = c.p
    t = c.trk
    xn2_s = c.hbm["xn2_s"]
    acc_s = c.hbm["acc_s"]
    y = c.hbm["y"]
    c.push()
    pbig = c.sb["pbig"]
    pv0, pv1, pv2 = c.sb["pv0"], c.sb["pv1"], c.sb["pv2"]
    pt = c.sb["pt"]
    ident = c.sb["ident_b"]
    ones = c.sb["ones_f"]

    def pb(i):
        return pbig[:, 512 * i:512 * (i + 1)]

    cmb = c.sbuf("cmb", [64, 1024], BF16)
    seqf = c.sbuf("seqf", [128, DC * 1024], BF16)
    egbc = c.sbuf("egbc", [128, E1], F32)
    ggc = c.sbuf("ggc", [128, DC], F32)
    b2g = c.sbuf("b2g", [128, 64], F32)
    rowp = c.sbuf("rowp", [1, 1024], F32)
    c.sbuf("rowtmp", [1, 1024], F32)
    tmpq = c.views("tmpq", 2, [128, 1024], F32)

    bm = p.op("tensor", "matmul", pv0[:, 0:E1], ones[0:1, :],
              c.sb["smrow"][0:1, D:D + E1], waits=t["pv0"].w_waits(),
              start=True, stop=True, tag=True)
    t["pv0"].wr(bm)
    te = p.op("scalar", "copy", out=egbc, in_=pv0[:, 0:E1], waits=[bm], tag=True)
    t["pv0"].rd(te); t["egbc"].wr(te)
    tg = p.op("vector", "tensor_mul", ggc, c.smallc("gs1", 0, DC), c.smallc("g1", 0, DC),
              waits=[], tag=True)
    t["ggc"].wr(tg)
    tb2 = None
    for e in range(E1):
        tb2 = p.op("vector", "tensor_scalar", b2g[:, 8 * e:8 * (e + 1)],
                   c.smallc("b2f", 8 * e, 8), egbc[:, e:e + 1], None, ALU.mult,
                   waits=[te] + (t["b2g"].w_waits() if e == 0 else []), tag=True)
        t["egbc"].rd(tb2)
    t["b2g"].wr(tb2)

    def rms_row(out_row, ps_row, waits, n):
        tmp = c.sb["rowtmp"][0:1, 0:n]
        tk = p.op("scalar", "activation", tmp, ps_row, AF.Sqrt,
                  waits=list(waits) + c.trk["rowtmp"].w_waits(), tag=True)
        c.trk["rowtmp"].wr(tk)
        tk = p.op("vector", "tensor_scalar_max", tmp, tmp, 1e-8, waits=[tk], tag=True)
        c.trk["rowtmp"].wr(tk)
        tk = p.op("vector", "reciprocal", out_row, tmp, waits=[tk], tag=True)
        c.trk["rowtmp"].rd(tk)
        tk = p.op("vector", "tensor_scalar_mul", out_row, out_row, 32.0, waits=[tk], tag=True)
        return tk

    c.push()
    xn2f = c.sbuf("xn2f", [128, DC * 4096], BF16)
    seT = c.sbuf("seT", [128, DC * 1024], BF16)
    gm = c.sbuf("gm", [128, DC * 64], BF16)
    slots = c.sbuf("slots", [128, DC * 1024], BF16)
    h1 = c.sbuf("h1", [128, 32 * 128], BF16)
    A1c = c.sbuf("A1c", [128, 32], F32)
    gates_bc = c.sbuf("gates_bc", [128, 1024], F32)
    grow = c.sbuf("grow", [1, 1024], F32)
    xgt = c.sbuf("xgt", [64, 1024], BF16)
    wst = c.views("wst", 2, [128, 1024], BF16)
    wst2 = c.views("wst2", 2, [128, 4096], BF16)

    ltk = None
    for dc in range(DC):
        for sq in range(BL):
            ltk = p.dma("sync", xn2f[:, 4096 * dc + 2048 * sq:4096 * dc + 2048 * (sq + 1)],
                        xn2_s[sq, dc, :, :],
                        waits=(t["xn2f"].w_waits() if dc == 0 and sq == 0 else ()))
    t["xn2f"].wr(ltk)

    for chk in range(8):
        cc = slice(512 * chk, 512 * (chk + 1))
        mm = None
        for dc in range(DC):
            par = dc % 2
            sk = p.op("scalar", "activation", tmpq[par][:, 0:512],
                      xn2f[:, 4096 * dc + cc.start:4096 * dc + cc.stop], AF.Square,
                      waits=[ltk] + t["tmpq%d" % par].w_waits(), tag=True)
            t["tmpq%d" % par].wr(sk); t["xn2f"].rd(sk)
            mm = p.op("tensor", "matmul", pv2[0:1, :], ones[:, 0:1], tmpq[par][:, 0:512],
                      waits=[sk] + (t["pv2"].w_waits() if dc == 0 else []),
                      start=(dc == 0), stop=(dc == DC - 1), tag=True)
            t["tmpq%d" % par].rd(mm)
        t["pv2"].wr(mm)
        a1r = rowp[0:1, 0:512]
        tk = rms_row(a1r, pv2[0:1, :], [mm], 512)
        t["pv2"].rd(tk); t["rowp"].wr(tk)
        cmms = []
        for j in range(4):
            w = [tk] + (t["pv1"].w_waits() if j == 0 else [])
            cm = p.op("tensor", "matmul", pv1[:, j:j + 1], a1r[0:1, 128 * j:128 * (j + 1)],
                      ones[0:1, 0:1], waits=w, start=True, stop=True, tag=True)
            t["rowp"].rd(cm)
            cmms.append(cm)
        t["pv1"].wr(cmms[-1])
        ce = p.op("vector", "tensor_copy", A1c[:, 4 * chk:4 * (chk + 1)], pv1[:, 0:4],
                  waits=cmms + (t["A1c"].w_waits() if chk == 0 else []), tag=True)
        t["pv1"].rd(ce)
    t["A1c"].wr(ce)

    gred = c.sbuf("gred", [128, 1536], F32)
    gtoks = []
    for dc in range(DC):
        for sq in range(BL):
            src = xn2f[:, 4096 * dc + 2048 * sq:4096 * dc + 2048 * (sq + 1)]
            tk = p.op("vector", "tensor_tensor", gred[:, 0:1024],
                      src.rearrange("p (a two) -> p a two", two=2)[:, :, 0],
                      src.rearrange("p (a two) -> p a two", two=2)[:, :, 1], ALU.add,
                      waits=[ltk] + t["gred"].w_waits(), tag=True)
            t["xn2f"].rd(tk)
            t["gred"].wr(tk)
            off, wdt = 0, 1024
            for lv in range(5):
                nw = wdt // 2
                dsto = 1024 if lv % 2 == 0 else 0
                src2 = gred[:, off:off + wdt].rearrange("p (a two) -> p a two", two=2)
                if lv < 4:
                    tk = p.op("vector", "tensor_tensor", gred[:, dsto:dsto + nw],
                              src2[:, :, 0], src2[:, :, 1], ALU.add, waits=[tk], tag=True)
                    t["gred"].wr(tk)
                else:
                    tk = p.op("vector", "scalar_tensor_tensor",
                              gm[:, 64 * dc + 32 * sq:64 * dc + 32 * (sq + 1)],
                              src2[:, :, 0], 1.0, src2[:, :, 1], ALU.bypass, ALU.add,
                              waits=[tk] + (t["gm"].w_waits() if dc == 0 and sq == 0 else []),
                              tag=True)
                    t["gred"].rd(tk)
                off, wdt = dsto, nw
            gtoks.append(tk)
    gsc = p.op("vector", "tensor_scalar_mul", gm, gm, 1.0 / 64.0,
               waits=gtoks + t["gm"].w_waits(), tag=True)
    t["gm"].wr(gsc)

    setoks = []
    for es in range(M1):
        for mcp in range(DC):
            par = (es * DC + mcp) % 2
            tw = p.dma("gpsimd", wst[par], c.hbm["ws1"][es][mcp],
                       waits=t["wst%d" % par].w_waits(), ch=par)
            t["wst%d" % par].wr(tw)
            pz = pb(mcp % 2)
            trz = t["pb%d" % (mcp % 2)]
            mm = None
            for kc in range(DC):
                w = [gsc, tw] + (trz.w_waits() if kc == 0 else [])
                mm = p.op("tensor", "matmul", pz[:, 0:64],
                          wst[par][:, 128 * kc:128 * (kc + 1)],
                          gm[:, 64 * kc:64 * (kc + 1)],
                          waits=w, start=(kc == 0), stop=(kc == DC - 1), tag=(kc == DC - 1))
                t["gm"].rd(mm)
                t["wst%d" % par].rd(mm)
            trz.wr(mm)
            dst = seT[:, 1024 * mcp:1024 * (mcp + 1)].rearrange(
                "q (b e) -> q b e", e=M1)[:, :, es]
            ev = p.op("vector", "tensor_copy", dst, pz[:, 0:64],
                      waits=[mm] + (t["seT"].w_waits() if es == 0 and mcp == 0 else []),
                      tag=True)
            trz.rd(ev)
            setoks.append(ev)
    t["seT"].wr(setoks[-1])

    for hf in range(2):
        mm = None
        for mcp in range(DC):
            par = mcp % 2
            sk = p.op("scalar", "activation", tmpq[par][:, 0:512],
                      seT[:, 1024 * mcp + 512 * hf:1024 * mcp + 512 * (hf + 1)], AF.Square,
                      waits=setoks[-1:] + t["tmpq%d" % par].w_waits(), tag=True)
            t["seT"].rd(sk)
            t["tmpq%d" % par].wr(sk)
            mm = p.op("tensor", "matmul", pv2[0:1, :], ones[:, 0:1], tmpq[par][:, 0:512],
                      waits=[sk] + (t["pv2"].w_waits() if mcp == 0 else []),
                      start=(mcp == 0), stop=(mcp == DC - 1), tag=True)
            t["tmpq%d" % par].rd(mm)
        t["pv2"].wr(mm)
        ser = rowp[0:1, 0:512]
        tk = rms_row(ser, pv2[0:1, :], [mm], 512)
        t["pv2"].rd(tk); t["rowp"].wr(tk)
        bc = p.op("tensor", "matmul", pv1[:, :], ones[0:1, :], ser,
                  waits=[tk] + t["pv1"].w_waits(), start=True, stop=True, tag=True)
        t["rowp"].rd(bc); t["pv1"].wr(bc)
        cbc = p.op("scalar", "copy", out=tmpq[0][:, 0:512], in_=pv1[:, :],
                   waits=[bc] + t["tmpq0"].w_waits(), tag=True)
        t["pv1"].rd(cbc); t["tmpq0"].wr(cbc)
        for mcp in range(DC):
            sl = seT[:, 1024 * mcp + 512 * hf:1024 * mcp + 512 * (hf + 1)]
            t1 = p.op("vector", "tensor_tensor", tmpq[1][:, 0:512], sl, tmpq[0][:, 0:512], ALU.mult,
                      waits=[cbc] + t["tmpq1"].w_waits() + t["seT"].r_waits(), tag=True)
            t["tmpq1"].wr(t1); t["tmpq0"].rd(t1)
            t2 = p.op("vector", "tensor_scalar", sl, tmpq[1][:, 0:512], ggc[:, mcp:mcp + 1], None,
                      ALU.mult, waits=[t1, tg] + t["seT"].w_waits(), tag=True)
            t["tmpq1"].rd(t2); t["ggc"].rd(t2)
            t["seT"].wr(t2)
    se_tok = t["seT"].w

    lrow = c.sbuf("lmax", [64, 1], F32)
    lsum = c.sbuf("lsum", [64, 1], F32)
    lrcp = c.sbuf("lrcp", [64, 1], F32)
    lgs = c.sbuf("lgs", [64, 16], F32)
    ctoks = []
    for bg in range(64):
        sq, g = bg // 32, bg % 32
        gc0 = 4096 * 0 + 2048 * sq + 64 * g
        mm = None
        for kc in range(DC):
            w = [ltk, se_tok] + (t["pv0"].w_waits() if kc == 0 else [])
            mm = p.op("tensor", "matmul", pv0[0:64, 0:16],
                      xn2f[:, 4096 * kc + 2048 * sq + 64 * g:4096 * kc + 2048 * sq + 64 * (g + 1)],
                      seT[:, 1024 * kc + 16 * bg:1024 * kc + 16 * (bg + 1)],
                      waits=w, start=(kc == 0), stop=(kc == DC - 1), tag=(kc == DC - 1))
            t["xn2f"].rd(mm)
            t["seT"].rd(mm)
        t["pv0"].wr(mm)
        a1s = A1c[64 * (g % 2):64 * (g % 2) + 64,
                  (2048 * sq + 64 * g) // 128:(2048 * sq + 64 * g) // 128 + 1]
        tk = p.op("vector", "tensor_scalar", lgs, pv0[0:64, 0:16], a1s, None, ALU.mult,
                  waits=[mm, ce] + t["lgs"].w_waits(), tag=True)
        t["pv0"].rd(tk); t["A1c"].rd(tk); t["lgs"].wr(tk)
        tmx = p.op("vector", "tensor_reduce", lrow, lgs, AX.X, ALU.max, negate=True,
                   waits=[tk] + t["lmax"].w_waits(), tag=True)
        t["lmax"].wr(tmx)
        tex = p.op("scalar", "activation", lgs, lgs, AF.Exp, bias=lrow,
                   accum_out=lsum, waits=[tmx] + t["lsum"].w_waits(), tag=True)
        t["lmax"].rd(tex); t["lsum"].wr(tex); t["lgs"].wr(tex)
        trc = p.op("vector", "reciprocal", lrcp, lsum, waits=[tex] + t["lrcp"].w_waits(),
                   tag=True)
        t["lsum"].rd(trc); t["lrcp"].wr(trc)
        tcb = p.op("vector", "tensor_scalar", cmb[:, 16 * bg:16 * (bg + 1)], lgs, lrcp,
                   None, ALU.mult, waits=[trc, tex] +
                   (t["cmb"].w_waits() if bg == 0 else []), tag=True)
        t["lgs"].rd(tcb); t["lrcp"].rd(tcb)
        ctoks.append(tcb)
    t["cmb"].wr(ctoks[-1])

    sltoks = []
    for bg in range(64):
        sq, g = bg // 32, bg % 32
        ttks = []
        for dc in range(DC):
            w = [ltk] + (t["pt"].w_waits() if dc == 0 else [])
            tk = p.op("tensor", "transpose", pt[0:64, 128 * dc:128 * (dc + 1)],
                      xn2f[:, 4096 * dc + 2048 * sq + 64 * g:4096 * dc + 2048 * sq + 64 * (g + 1)],
                      ident, waits=w, tag=(dc == DC - 1))
            t["xn2f"].rd(tk if tk else ltk)
        t["pt"].wr(tk)
        cg = p.op("vector", "tensor_copy", xgt, pt[0:64, :],
                  waits=[tk] + t["xgt"].w_waits(), tag=True)
        t["pt"].rd(cg); t["xgt"].wr(cg)
        for dc in range(DC):
            w = [cg, ctoks[bg]] + t["pv1"].w_waits()
            mm = p.op("tensor", "matmul", pv1[0:128, 0:16],
                      xgt[:, 128 * dc:128 * (dc + 1)], cmb[:, 16 * bg:16 * (bg + 1)],
                      waits=w, start=True, stop=True, tag=True)
            t["xgt"].rd(mm); t["cmb"].rd(mm); t["pv1"].wr(mm)
            ev = p.op("vector", "tensor_copy",
                      slots[:, 1024 * dc + 16 * bg:1024 * dc + 16 * (bg + 1)], pv1[:, 0:16],
                      waits=[mm] + (t["slots"].w_waits() if bg == 0 and dc == 0 else []),
                      tag=True)
            t["pv1"].rd(ev)
            sltoks.append(ev)
    t["slots"].wr(sltoks[-1])

    wgb = c.sbuf("wgb2", [128, DC], BF16)
    twg = p.op("vector", "tensor_copy", wgb, c.smallc("wg", 0, DC), tag=True)
    t["wgb2"].wr(twg)
    mm = None
    for kc in range(DC):
        w = [gsc, twg] + (t["pv0"].w_waits() if kc == 0 else [])
        mm = p.op("tensor", "matmul", pv0[0:1, 0:64], wgb[:, kc:kc + 1],
                  gm[:, 64 * kc:64 * (kc + 1)], waits=w,
                  start=(kc == 0), stop=(kc == DC - 1), tag=(kc == DC - 1))
        t["gm"].rd(mm); t["wgb2"].rd(mm)
    t["pv0"].wr(mm)
    gr = grow[0:1, 0:64]
    tk = p.op("scalar", "activation", gr, pv0[0:1, 0:64], AF.Sigmoid,
              bias=c.sb["smrow"][0:1, D + E1:D + E1 + 1],
              waits=[mm] + t["grow"].w_waits(), tag=True)
    t["pv0"].rd(tk)
    tk = p.op("vector", "tensor_scalar_max", gr, gr, 0.01, waits=[tk], tag=True)
    g16 = grow[0:1, 0:1024].rearrange("p (b e) -> p b e", e=16)
    ek0 = p.op("vector", "tensor_copy", tmpq[0][0:1, 0:64], gr, waits=[tk], tag=True)
    t["tmpq0"].wr(ek0)
    etoks2 = []
    for j in range(M1):
        ek = p.op("vector", "tensor_copy", g16[:, :, j], tmpq[0][0:1, 0:64],
                  waits=[ek0] + ([] if j else []), tag=True)
        etoks2.append(ek)
    t["grow"].wr(etoks2[-1])
    t["tmpq0"].rd(etoks2[-1])
    bc1 = p.op("tensor", "matmul", pv0[:, :], ones[0:1, :], grow[0:1, 0:512],
               waits=[etoks2[-1]] + t["pv0"].w_waits(), start=True, stop=True, tag=True)
    t["grow"].rd(bc1); t["pv0"].wr(bc1)
    bc2 = p.op("tensor", "matmul", pv1[:, :], ones[0:1, :], grow[0:1, 512:1024],
               waits=[etoks2[-1]] + t["pv1"].w_waits(), start=True, stop=True, tag=True)
    t["grow"].rd(bc2); t["pv1"].wr(bc2)
    c1 = p.op("scalar", "copy", out=gates_bc[:, 0:512], in_=pv0[:, :], waits=[bc1], tag=True)
    t["pv0"].rd(c1)
    c2 = p.op("scalar", "copy", out=gates_bc[:, 512:1024], in_=pv1[:, :], waits=[bc2], tag=True)
    t["pv1"].rd(c2)
    t["gates_bc"].wr(c2)
    gb_tok = c2

    sq1toks = []
    for e in range(E1):
        h1toks = []
        for mc in range(32):
            par = mc % 2
            tw = p.dma("gpsimd", wst[par], c.hbm["w1f"][e][mc],
                       waits=t["wst%d" % par].w_waits(), ch=par)
            t["wst%d" % par].wr(tw)
            pz = pb(mc % 2)
            trz = t["pb%d" % (mc % 2)]
            mm = None
            for kc in range(DC):
                w = [sltoks[-1], tw] + (trz.w_waits() if kc == 0 else [])
                rhs = slots[:, 1024 * kc:1024 * (kc + 1)].rearrange(
                    "q (b e2 s) -> q b e2 s", e2=E1, s=S1)[:, :, e, :]
                mm = p.op("tensor", "matmul", pz[:, 0:128],
                          wst[par][:, 128 * kc:128 * (kc + 1)], rhs,
                          waits=w, start=(kc == 0), stop=(kc == DC - 1), tag=(kc == DC - 1))
                t["slots"].rd(mm)
                t["wst%d" % par].rd(mm)
            trz.wr(mm)
            ev = p.op("scalar", "activation", h1[:, 128 * mc:128 * (mc + 1)], pz[:, 0:128],
                      AF.Gelu, bias=c.smallc("b1f", 32 * e + mc),
                      waits=[mm] + (t["h1"].w_waits() if mc == 0 else []), tag=True)
            trz.rd(ev)
            h1toks.append(ev)
        t["h1"].wr(h1toks[-1])
        for dc in range(DC):
            par = dc % 2
            tw = p.dma("gpsimd", wst2[par], c.hbm["w2f"][e][dc],
                       waits=t["wst2%d" % par].w_waits(), ch=par)
            t["wst2%d" % par].wr(tw)
            po = (pv0, pv1)[dc % 2]
            tro = t["pv%d" % (dc % 2)]
            mm = None
            for kc in range(32):
                w = [h1toks[kc], tw] + (tro.w_waits() if kc == 0 else [])
                mm = p.op("tensor", "matmul", po[:, 0:128],
                          wst2[par][:, 128 * kc:128 * (kc + 1)],
                          h1[:, 128 * kc:128 * (kc + 1)],
                          waits=w, start=(kc == 0), stop=(kc == 31), tag=(kc == 31))
                t["h1"].rd(mm)
                t["wst2%d" % par].rd(mm)
            tro.wr(mm)
            te1 = p.op("scalar", "activation", tmpq[par][:, 0:128], po[:, 0:128],
                       AF.Identity, scale=egbc[:, e:e + 1], bias=b2g[:, 8 * e + dc:8 * e + dc + 1],
                       waits=[mm, tb2] + t["tmpq%d" % par].w_waits(), tag=True)
            tro.rd(te1); t["egbc"].rd(te1); t["b2g"].rd(te1)
            t["tmpq%d" % par].wr(te1)
            dst = seqf[:, 1024 * dc:1024 * (dc + 1)].rearrange(
                "q (b e2 s) -> q b e2 s", e2=E1, s=S1)[:, :, e, :]
            gsl = gates_bc.rearrange("q (b e2 s) -> q b e2 s", e2=E1, s=S1)[:, :, e, :]
            ev = p.op("vector", "tensor_tensor", dst, tmpq[par][:, 0:128], gsl, ALU.mult,
                      waits=[te1, gb_tok] + (t["seqf"].w_waits() if e == 0 and dc == 0 else []),
                      tag=True)
            t["tmpq%d" % par].rd(ev)
            t["gates_bc"].rd(ev)
            sq1toks.append(ev)
    t["seqf"].wr(sq1toks[-1])
    c.pop()
    seq_tok = t["seqf"].w
    cmb_tok = t["cmb"].w

    c.push()
    comb2 = c.sbuf("comb2", [128, 8 * 512], BF16)
    sl2 = c.sbuf("sl2", [128, DC * 1024], BF16)
    out2 = c.sbuf("out2", [128, DC * 1024], BF16)
    o2f = c.sbuf("o2f", [128, DC * 1024], BF16)
    wstB = c.views("wstB", 2, [128, 1024], BF16)
    wstB2 = c.views("wstB2", 2, [128, 4096], BF16)

    c.push()
    xs = c.sbuf("xs", [128, DC * 1024], BF16)
    cme = c.sbuf("cme", [128, DC * 256], BF16)
    se2T = c.sbuf("se2T", [128, DC * 1024], BF16)
    L2 = c.sbuf("L2", [128, 8 * 512], BF16)
    L2T = c.sbuf("L2T", [128, 8 * 512], BF16)
    disp = c.sbuf("disp", [128, 8 * 512], BF16)
    dtm = c.sbuf("dtm", [128, 8 * 512], BF16)
    xstm = c.sbuf("xstm", [128, 8 * 1024], BF16)

    a2toks = []
    for hf in range(2):
        mm = None
        for dc in range(DC):
            par = dc % 2
            sk = p.op("scalar", "activation", tmpq[par][:, 0:512],
                      seqf[:, 1024 * dc + 512 * hf:1024 * dc + 512 * (hf + 1)], AF.Square,
                      waits=[seq_tok] + t["tmpq%d" % par].w_waits(), tag=True)
            t["seqf"].rd(sk)
            t["tmpq%d" % par].wr(sk)
            mm = p.op("tensor", "matmul", pv2[0:1, :], ones[:, 0:1], tmpq[par][:, 0:512],
                      waits=[sk] + (t["pv2"].w_waits() if dc == 0 else []),
                      start=(dc == 0), stop=(dc == DC - 1), tag=True)
            t["tmpq%d" % par].rd(mm)
        t["pv2"].wr(mm)
        a2r = rowp[0:1, 0:512]
        tk = rms_row(a2r, pv2[0:1, :], [mm], 512)
        t["pv2"].rd(tk); t["rowp"].wr(tk)
        bc = p.op("tensor", "matmul", pv1[:, :], ones[0:1, :], a2r,
                  waits=[tk] + t["pv1"].w_waits(), start=True, stop=True, tag=True)
        t["rowp"].rd(bc); t["pv1"].wr(bc)
        cbc = p.op("scalar", "copy", out=tmpq[0][:, 0:512], in_=pv1[:, :],
                   waits=[bc] + t["tmpq0"].w_waits(), tag=True)
        t["pv1"].rd(cbc); t["tmpq0"].wr(cbc)
        for dc in range(DC):
            sl = slice(1024 * dc + 512 * hf, 1024 * dc + 512 * (hf + 1))
            t1 = p.op("vector", "tensor_tensor", tmpq[1][:, 0:512], seqf[:, sl],
                      tmpq[0][:, 0:512], ALU.mult,
                      waits=[cbc, seq_tok] + t["tmpq1"].w_waits(), tag=True)
            t["seqf"].rd(t1)
            t["tmpq1"].wr(t1); t["tmpq0"].rd(t1)
            t2 = p.op("vector", "tensor_scalar", xs[:, sl], tmpq[1][:, 0:512],
                      c.smallc("g2", dc), None, ALU.mult,
                      waits=[t1] + (t["xs"].w_waits() if hf == 0 and dc == 0 else []),
                      tag=True)
            t["tmpq1"].rd(t2)
            a2toks.append(t2)
    t["xs"].wr(a2toks[-1])
    xs_tok = a2toks[-1]

    cmtoks = []
    for dc in range(DC):
        par = dc % 2
        src = xs[:, 1024 * dc:1024 * (dc + 1)].rearrange("q (a two) -> q a two", two=2)
        tk = p.op("vector", "tensor_tensor", tmpq[par][:, 0:512], src[:, :, 0],
                  src[:, :, 1], ALU.add, waits=[xs_tok] + t["tmpq%d" % par].w_waits(),
                  tag=True)
        t["xs"].rd(tk)
        t["tmpq%d" % par].wr(tk)
        s2 = tmpq[par][:, 0:512].rearrange("q (a two) -> q a two", two=2)
        tk = p.op("vector", "scalar_tensor_tensor", cme[:, 256 * dc:256 * (dc + 1)],
                  s2[:, :, 0], 1.0, s2[:, :, 1], ALU.bypass, ALU.add,
                  waits=[tk] + (t["cme"].w_waits() if dc == 0 else []), tag=True)
        t["tmpq%d" % par].rd(tk)
        cmtoks.append(tk)
    csc = p.op("vector", "tensor_scalar_mul", cme, cme, 0.25, waits=cmtoks, tag=True)
    t["cme"].wr(csc)

    s2toks = []
    for e2 in range(E2):
        for mcp in range(DC):
            par = (e2 * DC + mcp) % 2
            tw = p.dma("gpsimd", wstB[par], c.hbm["ws2"][e2][mcp],
                       waits=t["wstB%d" % par].w_waits(), ch=par)
            t["wstB%d" % par].wr(tw)
            pz = pb(mcp % 2)
            trz = t["pb%d" % (mcp % 2)]
            mm = None
            for kc in range(DC):
                w = [csc, tw] + (trz.w_waits() if kc == 0 else [])
                mm = p.op("tensor", "matmul", pz[:, 0:256],
                          wstB[par][:, 128 * kc:128 * (kc + 1)],
                          cme[:, 256 * kc:256 * (kc + 1)],
                          waits=w, start=(kc == 0), stop=(kc == DC - 1), tag=(kc == DC - 1))
                t["cme"].rd(mm)
                t["wstB%d" % par].rd(mm)
            trz.wr(mm)
            evs = []
            for b in range(BL):
                ev = p.op("vector", "tensor_copy",
                          se2T[:, 1024 * mcp + 512 * b + 128 * e2:
                               1024 * mcp + 512 * b + 128 * (e2 + 1)],
                          pz[:, 128 * b:128 * (b + 1)],
                          waits=[mm] + (t["se2T"].w_waits() if e2 == 0 and mcp == 0 and b == 0 else []),
                          tag=True)
                trz.rd(ev)
                evs.append(ev)
            s2toks += evs
    t["se2T"].wr(s2toks[-1])
    for hf in range(2):
        mm = None
        for mcp in range(DC):
            par = mcp % 2
            sk = p.op("scalar", "activation", tmpq[par][:, 0:512],
                      se2T[:, 1024 * mcp + 512 * hf:1024 * mcp + 512 * (hf + 1)], AF.Square,
                      waits=[s2toks[-1]] + t["tmpq%d" % par].w_waits(), tag=True)
            t["se2T"].rd(sk)
            t["tmpq%d" % par].wr(sk)
            mm = p.op("tensor", "matmul", pv2[0:1, :], ones[:, 0:1], tmpq[par][:, 0:512],
                      waits=[sk] + (t["pv2"].w_waits() if mcp == 0 else []),
                      start=(mcp == 0), stop=(mcp == DC - 1), tag=True)
            t["tmpq%d" % par].rd(mm)
        t["pv2"].wr(mm)
        ser = rowp[0:1, 0:512]
        tk = rms_row(ser, pv2[0:1, :], [mm], 512)
        t["pv2"].rd(tk); t["rowp"].wr(tk)
        bc = p.op("tensor", "matmul", pv1[:, :], ones[0:1, :], ser,
                  waits=[tk] + t["pv1"].w_waits(), start=True, stop=True, tag=True)
        t["rowp"].rd(bc); t["pv1"].wr(bc)
        cbc = p.op("scalar", "copy", out=tmpq[0][:, 0:512], in_=pv1[:, :],
                   waits=[bc] + t["tmpq0"].w_waits(), tag=True)
        t["pv1"].rd(cbc); t["tmpq0"].wr(cbc)
        for mcp in range(DC):
            sl = se2T[:, 1024 * mcp + 512 * hf:1024 * mcp + 512 * (hf + 1)]
            t1 = p.op("vector", "tensor_tensor", tmpq[1][:, 0:512], sl,
                      tmpq[0][:, 0:512], ALU.mult,
                      waits=[cbc] + t["tmpq1"].w_waits() + t["se2T"].r_waits(), tag=True)
            t["tmpq1"].wr(t1); t["tmpq0"].rd(t1)
            t2 = p.op("vector", "tensor_scalar", sl, tmpq[1][:, 0:512],
                      c.smallc("gs2", mcp), None, ALU.mult,
                      waits=[t1] + t["se2T"].w_waits(), tag=True)
            t["tmpq1"].rd(t2)
            t["se2T"].wr(t2)
    se2_tok = t["se2T"].w

    l2toks = []
    for b in range(BL):
        for pc in range(4):
            mm = None
            for kc in range(DC):
                w = [xs_tok, se2_tok] + (t["pv0"].w_waits() if kc == 0 else [])
                mm = p.op("tensor", "matmul", pv0[:, :],
                          xs[:, 1024 * kc + 512 * b + 128 * pc:1024 * kc + 512 * b + 128 * (pc + 1)],
                          se2T[:, 1024 * kc + 512 * b:1024 * kc + 512 * (b + 1)],
                          waits=w, start=(kc == 0), stop=(kc == DC - 1), tag=(kc == DC - 1))
                t["xs"].rd(mm)
                t["se2T"].rd(mm)
            t["pv0"].wr(mm)
            i = b * 4 + pc
            ev = p.op("vector", "tensor_copy", L2[:, 512 * i:512 * (i + 1)], pv0[:, :],
                      waits=[mm] + (t["L2"].w_waits() if i == 0 else []), tag=True)
            t["pv0"].rd(ev)
            l2toks.append(ev)
    t["L2"].wr(l2toks[-1])
    cm2toks = []
    smax = c.sbuf("smax", [128, 1], F32)
    ssum = c.sbuf("ssum", [128, 1], F32)
    srcp = c.sbuf("srcp", [128, 1], F32)
    for i in range(8):
        sl = slice(512 * i, 512 * (i + 1))
        tmx = p.op("vector", "tensor_reduce", smax, L2[:, sl], AX.X, ALU.max,
                   negate=True, waits=[l2toks[i]] + t["smax"].w_waits(), tag=True)
        t["L2"].rd(tmx); t["smax"].wr(tmx)
        tex = p.op("scalar", "activation", comb2[:, sl], L2[:, sl], AF.Exp, bias=smax,
                   accum_out=ssum, waits=[tmx] + t["ssum"].w_waits() +
                   (t["comb2"].w_waits() if i == 0 else []), tag=True)
        t["smax"].rd(tex); t["ssum"].wr(tex); t["L2"].rd(tex)
        trc = p.op("vector", "reciprocal", srcp, ssum, waits=[tex] + t["srcp"].w_waits(),
                   tag=True)
        t["ssum"].rd(trc); t["srcp"].wr(trc)
        tcb = p.op("vector", "tensor_scalar", comb2[:, sl], comb2[:, sl], srcp, None,
                   ALU.mult, waits=[trc, tex], tag=True)
        t["srcp"].rd(tcb)
        cm2toks.append(tcb)
    t["comb2"].wr(cm2toks[-1])
    lt_toks = []
    for b in range(BL):
        for ec in range(4):
            for pc in range(4):
                w = [l2toks[b * 4 + pc]] + (t["pt"].w_waits() if pc == 0 else [])
                tk = p.op("tensor", "transpose", pt[:, 128 * pc:128 * (pc + 1)],
                          L2[:, 512 * (b * 4 + pc) + 128 * ec:512 * (b * 4 + pc) + 128 * (ec + 1)],
                          ident, waits=w, tag=(pc == 3))
                t["L2"].rd(tk if tk else l2toks[-1])
            t["pt"].wr(tk)
            i = b * 4 + ec
            ev = p.op("vector", "tensor_copy", L2T[:, 512 * i:512 * (i + 1)], pt[:, 0:512],
                      waits=[tk] + (t["L2T"].w_waits() if i == 0 else []), tag=True)
            t["pt"].rd(ev)
            lt_toks.append(ev)
    t["L2T"].wr(lt_toks[-1])
    dtoks = []
    for i in range(8):
        sl = slice(512 * i, 512 * (i + 1))
        tmx = p.op("vector", "tensor_reduce", smax, L2T[:, sl], AX.X, ALU.max,
                   negate=True, waits=[lt_toks[i]] + t["smax"].w_waits(), tag=True)
        t["L2T"].rd(tmx); t["smax"].wr(tmx)
        tex = p.op("scalar", "activation", disp[:, sl], L2T[:, sl], AF.Exp, bias=smax,
                   accum_out=ssum, waits=[tmx] + t["ssum"].w_waits() +
                   (t["disp"].w_waits() if i == 0 else []), tag=True)
        t["smax"].rd(tex); t["ssum"].wr(tex); t["L2T"].rd(tex)
        trc = p.op("vector", "reciprocal", srcp, ssum, waits=[tex] + t["srcp"].w_waits(),
                   tag=True)
        t["ssum"].rd(trc); t["srcp"].wr(trc)
        tcb = p.op("vector", "tensor_scalar", disp[:, sl], disp[:, sl], srcp, None,
                   ALU.mult, waits=[trc, tex], tag=True)
        t["srcp"].rd(tcb)
        dtoks.append(tcb)
    t["disp"].wr(dtoks[-1])
    dm_toks = []
    for b in range(BL):
        for pc in range(4):
            for ec in range(4):
                w = [dtoks[b * 4 + ec]] + (t["pt"].w_waits() if ec == 0 else [])
                tk = p.op("tensor", "transpose", pt[:, 128 * ec:128 * (ec + 1)],
                          disp[:, 512 * (b * 4 + ec) + 128 * pc:512 * (b * 4 + ec) + 128 * (pc + 1)],
                          ident, waits=w, tag=(ec == 3))
                t["disp"].rd(tk if tk else dtoks[-1])
            t["pt"].wr(tk)
            i = b * 4 + pc
            ev = p.op("vector", "tensor_copy", dtm[:, 512 * i:512 * (i + 1)], pt[:, 0:512],
                      waits=[tk] + (t["dtm"].w_waits() if i == 0 else []), tag=True)
            t["pt"].rd(ev)
            dm_toks.append(ev)
    t["dtm"].wr(dm_toks[-1])
    xm_toks = []
    for b in range(BL):
        for pc in range(4):
            i = b * 4 + pc
            for dc in range(DC):
                w = [xs_tok] + (t["pt"].w_waits() if dc % 8 == 0 else [])
                tk = p.op("tensor", "transpose", pt[:, 128 * (dc % 8):128 * ((dc % 8) + 1)],
                          xs[:, 1024 * dc + 512 * b + 128 * pc:1024 * dc + 512 * b + 128 * (pc + 1)],
                          ident, waits=w, tag=(dc == DC - 1))
                t["xs"].rd(tk if tk else xs_tok)
            t["pt"].wr(tk)
            ev = p.op("vector", "tensor_copy", xstm[:, 1024 * i:1024 * (i + 1)], pt[:, 0:1024],
                      waits=[tk] + (t["xstm"].w_waits() if i == 0 else []), tag=True)
            t["pt"].rd(ev)
            xm_toks.append(ev)
    t["xstm"].wr(xm_toks[-1])
    sl2toks = []
    for b in range(BL):
        for dc in range(DC):
            po = (pv0, pv1)[dc % 2]
            tro = t["pv%d" % (dc % 2)]
            mm = None
            for pc in range(4):
                i = b * 4 + pc
                w = [xm_toks[i], dm_toks[i]] + (tro.w_waits() if pc == 0 else [])
                mm = p.op("tensor", "matmul", po[:, :],
                          xstm[:, 1024 * i + 128 * dc:1024 * i + 128 * (dc + 1)],
                          dtm[:, 512 * i:512 * (i + 1)],
                          waits=w, start=(pc == 0), stop=(pc == 3), tag=(pc == 3))
                t["xstm"].rd(mm)
                t["dtm"].rd(mm)
            tro.wr(mm)
            ev = p.op("vector", "tensor_copy", sl2[:, 1024 * dc + 512 * b:1024 * dc + 512 * (b + 1)],
                      po[:, :], waits=[mm] + (t["sl2"].w_waits() if b == 0 and dc == 0 else []),
                      tag=True)
            tro.rd(ev)
            sl2toks.append(ev)
    t["sl2"].wr(sl2toks[-1])
    c.pop()
    sl2_tok = t["sl2"].w
    cb2_tok = t["comb2"].w

    c.push()
    h2 = c.sbuf("h2", [128, 32 * 256], BF16)
    o2toks = []
    for e2 in range(E2):
        h2toks = []
        for mc in range(32):
            par = mc % 2
            tw = p.dma("gpsimd", wstB[par], c.hbm["w1s"][e2][mc],
                       waits=t["wstB%d" % par].w_waits(), ch=par)
            t["wstB%d" % par].wr(tw)
            pz = pb(mc % 2)
            trz = t["pb%d" % (mc % 2)]
            mm = None
            for kc in range(DC):
                rhs = sl2[:, 1024 * kc:1024 * (kc + 1)].rearrange(
                    "q (b n) -> q b n", b=BL)[:, :, 128 * e2:128 * (e2 + 1)]
                w = [sl2_tok, tw] + (trz.w_waits() if kc == 0 else [])
                mm = p.op("tensor", "matmul", pz[:, 0:256],
                          wstB[par][:, 128 * kc:128 * (kc + 1)], rhs,
                          waits=w, start=(kc == 0), stop=(kc == DC - 1), tag=(kc == DC - 1))
                t["sl2"].rd(mm)
                t["wstB%d" % par].rd(mm)
            trz.wr(mm)
            ev = p.op("scalar", "activation", h2[:, 256 * mc:256 * (mc + 1)], pz[:, 0:256],
                      AF.Gelu, bias=c.smallc("b1s", 32 * e2 + mc),
                      waits=[mm] + (t["h2"].w_waits() if mc == 0 else []), tag=True)
            trz.rd(ev)
            h2toks.append(ev)
        t["h2"].wr(h2toks[-1])
        for dc in range(DC):
            par = dc % 2
            tw = p.dma("gpsimd", wstB2[par], c.hbm["w2s"][e2][dc],
                       waits=t["wstB2%d" % par].w_waits(), ch=par)
            t["wstB2%d" % par].wr(tw)
            po = (pv0, pv1)[dc % 2]
            tro = t["pv%d" % (dc % 2)]
            mm = None
            for kc in range(32):
                w = [h2toks[kc], tw] + (tro.w_waits() if kc == 0 else [])
                mm = p.op("tensor", "matmul", po[:, 0:256],
                          wstB2[par][:, 128 * kc:128 * (kc + 1)],
                          h2[:, 256 * kc:256 * (kc + 1)],
                          waits=w, start=(kc == 0), stop=(kc == 31), tag=(kc == 31))
                t["h2"].rd(mm)
                t["wstB2%d" % par].rd(mm)
            tro.wr(mm)
            dst = o2f[:, 1024 * dc:1024 * (dc + 1)].rearrange(
                "q (b n) -> q b n", b=BL)[:, :, 128 * e2:128 * (e2 + 1)]
            ev = p.op("scalar", "activation", dst, po[:, 0:256], AF.Identity,
                      bias=c.smallc("b2s", 8 * e2 + dc),
                      waits=[mm] + (t["o2f"].w_waits() if e2 == 0 and dc == 0 else []),
                      tag=True)
            tro.rd(ev)
            o2toks.append(ev)
    t["o2f"].wr(o2toks[-1])
    c.pop()

    c.push()
    o2tm = c.sbuf("o2tm", [128, 8 * 1024], BF16)
    c2t = c.sbuf("c2t", [128, 8 * 512], BF16)
    om_toks = []
    for b in range(BL):
        for ec in range(4):
            i = b * 4 + ec
            for dc in range(DC):
                w = [o2toks[-1]] + (t["pt"].w_waits() if dc % 8 == 0 else [])
                tk = p.op("tensor", "transpose", pt[:, 128 * (dc % 8):128 * ((dc % 8) + 1)],
                          o2f[:, 1024 * dc + 512 * b + 128 * ec:1024 * dc + 512 * b + 128 * (ec + 1)],
                          ident, waits=w, tag=(dc == DC - 1))
                t["o2f"].rd(tk if tk else o2toks[-1])
            t["pt"].wr(tk)
            ev = p.op("vector", "tensor_copy", o2tm[:, 1024 * i:1024 * (i + 1)], pt[:, 0:1024],
                      waits=[tk] + (t["o2tm"].w_waits() if i == 0 else []), tag=True)
            t["pt"].rd(ev)
            om_toks.append(ev)
    t["o2tm"].wr(om_toks[-1])
    ct_toks = []
    for b in range(BL):
        for ec in range(4):
            for pc in range(4):
                w = [cb2_tok] + (t["pt"].w_waits() if pc == 0 else [])
                tk = p.op("tensor", "transpose", pt[:, 128 * pc:128 * (pc + 1)],
                          comb2[:, 512 * (b * 4 + pc) + 128 * ec:512 * (b * 4 + pc) + 128 * (ec + 1)],
                          ident, waits=w, tag=(pc == 3))
                t["comb2"].rd(tk if tk else cb2_tok)
            t["pt"].wr(tk)
            i = b * 4 + ec
            ev = p.op("vector", "tensor_copy", c2t[:, 512 * i:512 * (i + 1)], pt[:, 0:512],
                      waits=[tk] + (t["c2t"].w_waits() if i == 0 else []), tag=True)
            t["pt"].rd(ev)
            ct_toks.append(ev)
    t["c2t"].wr(ct_toks[-1])
    ot_toks = []
    for b in range(BL):
        for dc in range(DC):
            po = (pv0, pv1)[dc % 2]
            tro = t["pv%d" % (dc % 2)]
            mm = None
            for ec in range(4):
                i = b * 4 + ec
                w = [om_toks[i], ct_toks[i]] + (tro.w_waits() if ec == 0 else [])
                mm = p.op("tensor", "matmul", po[:, :],
                          o2tm[:, 1024 * i + 128 * dc:1024 * i + 128 * (dc + 1)],
                          c2t[:, 512 * i:512 * (i + 1)],
                          waits=w, start=(ec == 0), stop=(ec == 3), tag=(ec == 3))
                t["o2tm"].rd(mm)
                t["c2t"].rd(mm)
            tro.wr(mm)
            ev = p.op("vector", "tensor_copy",
                      out2[:, 1024 * dc + 512 * b:1024 * dc + 512 * (b + 1)], po[:, :],
                      waits=[mm] + (t["out2"].w_waits() if b == 0 and dc == 0 else []),
                      tag=True)
            tro.rd(ev)
            ot_toks.append(ev)
    t["out2"].wr(ot_toks[-1])
    c.pop()
    out2_tok = t["out2"].w

    c.push()
    ffn_fm = c.sbuf("ffn_fm", [128, DC * 4096], BF16)
    cT = c.views("cT", 2, [16, 64], BF16)
    o2g = c.views("o2g", 2, [16, 1024], BF16)
    fsb = c.views("fsb", 2, [64, 1024], BF16)
    ff_toks = []
    for bg in range(64):
        b, g = bg // 32, bg % 32
        par = bg % 2
        tk = p.op("tensor", "transpose", pt[0:16, 0:64], cmb[:, 16 * bg:16 * (bg + 1)],
                  ident[0:64, 0:64], waits=[cmb_tok] + t["pt"].w_waits(), tag=True)
        t["cmb"].rd(tk); t["pt"].wr(tk)
        ck = p.op("vector", "tensor_copy", cT[par], pt[0:16, 0:64],
                  waits=[tk] + t["cT%d" % par].w_waits(), tag=True)
        t["pt"].rd(ck); t["cT%d" % par].wr(ck)
        gtk = None
        for dc in range(DC):
            w = [out2_tok] + (t["pt"].w_waits() if dc == 0 else [])
            gtk = p.op("tensor", "transpose", pt[0:16, 128 * (dc % 8):128 * ((dc % 8) + 1)],
                       out2[:, 1024 * dc + 512 * b + 16 * g:1024 * dc + 512 * b + 16 * (g + 1)],
                       ident, waits=w, tag=(dc == DC - 1))
            t["out2"].rd(gtk if gtk else out2_tok)
        t["pt"].wr(gtk)
        gk = p.op("vector", "tensor_copy", o2g[par], pt[0:16, 0:1024],
                  waits=[gtk] + t["o2g%d" % par].w_waits(), tag=True)
        t["pt"].rd(gk); t["o2g%d" % par].wr(gk)
        mms = []
        for hf in range(2):
            po = (pv0, pv1)[hf]
            tro = t["pv%d" % hf]
            mm = p.op("tensor", "matmul", po[0:64, :], cT[par],
                      o2g[par][:, 512 * hf:512 * (hf + 1)],
                      waits=[ck, gk] + tro.w_waits(), start=True, stop=True, tag=True)
            t["cT%d" % par].rd(mm)
            t["o2g%d" % par].rd(mm)
            tro.wr(mm)
            mms.append(mm)
        ev = p.op("vector", "tensor_copy", fsb[par][:, 0:512], pv0[0:64, :],
                  waits=[mms[0]] + t["fsb%d" % par].w_waits(), tag=True)
        t["pv0"].rd(ev)
        ev2 = p.op("vector", "tensor_copy", fsb[par][:, 512:1024], pv1[0:64, :],
                   waits=[mms[1], ev], tag=True)
        t["pv1"].rd(ev2)
        t["fsb%d" % par].wr(ev2)
        for dc in range(DC):
            tk = p.op("tensor", "transpose", pt[:, 0:64],
                      fsb[par][:, 128 * dc:128 * (dc + 1)], ident[0:64, 0:64],
                      waits=[ev2] + t["pt"].w_waits(), tag=True)
            t["fsb%d" % par].rd(tk); t["pt"].wr(tk)
            fk = p.op("vector", "tensor_copy",
                      ffn_fm[:, 4096 * dc + 2048 * b + 64 * g:4096 * dc + 2048 * b + 64 * (g + 1)],
                      pt[:, 0:64],
                      waits=[tk] + (t["ffn_fm"].w_waits() if bg == 0 and dc == 0 else []),
                      tag=True)
            t["pt"].rd(fk)
            ff_toks.append(fk)
    t["ffn_fm"].wr(ff_toks[-1])

    acct = c.views("acct", 2, [128, 512], F32)
    yt = c.views("yt", 2, [128, 512], F32)
    for dc in range(DC):
        for i in range(8):
            par = i % 2
            cc = slice(512 * i, 512 * (i + 1))
            b, tk0 = (512 * i) // 2048, (512 * i) % 2048
            al = p.dma("sync", acct[par], acc_s[b, dc, :, tk0:tk0 + 512],
                       waits=t["acct%d" % par].w_waits(), ch=par)
            t["acct%d" % par].wr(al)
            ta = p.op("vector", "tensor_tensor", yt[par], acct[par],
                      ffn_fm[:, 4096 * dc + cc.start:4096 * dc + cc.stop], ALU.add,
                      waits=[al, ff_toks[-1]] + t["yt%d" % par].w_waits(), tag=True)
            t["acct%d" % par].rd(ta)
            t["ffn_fm"].rd(ta)
            t["yt%d" % par].wr(ta)
            dtk = p.dma("sync", y[b, dc, :, tk0:tk0 + 512], yt[par], waits=[ta])
            t["yt%d" % par].rd(dtk)
    c.pop()
    c.pop()
    c.pop()


# version 05
# speedup vs baseline: 1.1544x; 1.1544x over previous
import numpy as np
import ml_dtypes
from contextlib import ExitStack
import concourse.bass as bass
import concourse.mybir as mybir
from concourse.bass_utils import run_bass_kernel_spmd

F32 = mybir.dt.float32
BF16 = mybir.dt.bfloat16
I32 = mybir.dt.int32
AF = mybir.ActivationFunctionType
ALU = mybir.AluOpType
AX = mybir.AxisListType

B, S, D = 16, 2048, 1024
NCORES = 8
BL = B // NCORES
TOK = BL * S
NH, DH = 16, 64
NPAIR = NH // 2
FH = 3072
EXP = 4096
KG, E1, S1, E2 = 64, 8, 2, 4
G = S // KG
M1 = E1 * S1
LL = G * M1
N2 = LL // E2
HID = 4096
DC = D // 128
ENGINES = ("sync", "gpsimd", "scalar", "vector", "tensor")
NEG = -1.0e30


def _lhsT(w):
    K, M = w.shape
    return np.ascontiguousarray(
        w.reshape(K // 128, 128, M // 128, 128).transpose(2, 1, 0, 3))


def _fm(v):
    return np.ascontiguousarray(np.asarray(v).reshape(-1, 128).T)


class _Smalls:
    def __init__(self):
        self.cols = []
        self.map = {}
        self.n = 0

    def add(self, name, arr):
        arr = np.asarray(arr, np.float32).reshape(128, -1)
        self.map[name] = self.n
        self.n += arr.shape[1]
        self.cols.append(arr)

    def finalize(self):
        return np.ascontiguousarray(np.concatenate(self.cols, axis=1))


def prep_inputs(inp):
    d = {}
    wq = inp["L0_attn_Wq"].reshape(NPAIR, 2, D, DH)
    wq = wq.transpose(0, 2, 1, 3).reshape(NPAIR, D, 128)
    d["wq"] = np.stack([_lhsT(w)[0] for w in wq])
    wk = inp["L0_attn_Wk"].reshape(NPAIR, 2, D, DH).transpose(0, 2, 1, 3).reshape(NPAIR, D, 128)
    d["wk"] = np.stack([_lhsT(w)[0] for w in wk])
    d["wv"] = np.ascontiguousarray(inp["L0_attn_Wv"].transpose(1, 0, 2).reshape(D, D))
    d["wo0"] = _lhsT(inp["L0_attn_Wo"])
    d["wg0"] = _lhsT(np.ascontiguousarray(inp["L0_ffn_w_gate"].T))
    d["wu0"] = _lhsT(np.ascontiguousarray(inp["L0_ffn_w_up"].T))
    d["wd0"] = _lhsT(np.ascontiguousarray(inp["L0_ffn_w_down"].T))
    d["w1"] = _lhsT(inp["L1_attn_W1"])
    d["w2"] = _lhsT(inp["L1_attn_W2"])
    d["wz"] = _lhsT(inp["L1_attn_Wz"])
    d["wh"] = _lhsT(inp["L1_attn_Wh"])
    d["wo1"] = _lhsT(inp["L1_attn_Wo"])
    ws1 = inp["L1_ffn_W_slot1"].reshape(D, M1, D).transpose(1, 0, 2)
    d["ws1"] = np.stack([_lhsT(w) for w in ws1])
    d["w1f"] = np.stack([_lhsT(w) for w in inp["L1_ffn_W1_first"]])
    d["w2f"] = np.stack([_lhsT(w) for w in inp["L1_ffn_W2_first"]])
    ws2 = inp["L1_ffn_W_slot2"].reshape(D, E2, D).transpose(1, 0, 2)
    d["ws2"] = np.stack([_lhsT(w) for w in ws2])
    d["w1s"] = np.stack([_lhsT(w) for w in inp["L1_ffn_W1_second"]])
    d["w2s"] = np.stack([_lhsT(w) for w in inp["L1_ffn_W2_second"]])

    sm = _Smalls()
    sm.add("bq", inp["L0_attn_bq"].reshape(NPAIR, 128).T)
    sm.add("bk", inp["L0_attn_bk"].reshape(NPAIR, 128).T)
    sm.add("bo0", _fm(inp["L0_attn_bo"]))
    sm.add("b1", _fm(inp["L1_attn_b1"]))
    sm.add("b2", _fm(inp["L1_attn_b2"]))
    sm.add("bz", _fm(inp["L1_attn_bz"]))
    sm.add("bh", _fm(inp["L1_attn_bh"]))
    sm.add("bo1", _fm(inp["L1_attn_bo"]))
    sm.add("g1", _fm(inp["L1_ffn_gamma1"]))
    sm.add("gs1", _fm(inp["L1_ffn_gamma_slot1"]))
    sm.add("g2", _fm(inp["L1_ffn_gamma2"]))
    sm.add("gs2", _fm(inp["L1_ffn_gamma_slot2"]))
    sm.add("b1f", np.concatenate([_fm(v) for v in inp["L1_ffn_b1_first"]], 1))
    sm.add("b2f", np.concatenate([_fm(v) for v in inp["L1_ffn_b2_first"]], 1))
    sm.add("b1s", np.concatenate([_fm(v) for v in inp["L1_ffn_b1_second"]], 1))
    sm.add("b2s", np.concatenate([_fm(v) for v in inp["L1_ffn_b2_second"]], 1))
    sm.add("wg", _fm(inp["L1_ffn_Wg"][:, 0]))
    d["smalls"] = sm.finalize()
    d["smmap"] = sm.map
    row = np.zeros((1, 1040), np.float32)
    row[0, :D] = inp["L0_attn_bv"].reshape(D)
    row[0, D:D + E1] = inp["L1_ffn_expert_gates"]
    row[0, D + E1] = inp["L1_ffn_bg"][0]
    d["smrow"] = row
    return d



class Trk:
    __slots__ = ("w", "r")

    def __init__(self):
        self.w = None
        self.r = {}

    def rd(self, tok):
        if tok is not None:
            s, v = tok
            if self.r.get(s, 0) < v:
                self.r[s] = v

    def wr(self, tok):
        if tok is not None:
            self.w = tok
            self.r = {}

    def w_waits(self):
        out = [(s, v) for s, v in self.r.items()]
        if self.w is not None:
            out.append(self.w)
        return out

    def r_waits(self):
        return [self.w] if self.w is not None else []


class Prog:
    def __init__(self, nc):
        self.nc = nc
        self.rec = {e: [] for e in ENGINES}
        self.semh = {}
        self.cnt = {}
        self.wm = {e: {} for e in ENGINES}
        for e in ENGINES:
            self._sem("s_" + e)
        self.bounds = {}
        self.fence = {}
        for q in ("sync", "gpsimd", "scalar"):
            for sfx in ("", "B", "C"):
                self._sem("d" + sfx + "_" + q)
                self.bounds["d" + sfx + "_" + q] = [0]
                self.fence["d" + sfx + "_" + q] = None
        self.tag_since_wait = {e: False for e in ENGINES}
        self.last_tag = {e: 0 for e in ENGINES}
        self.ninstr = 0

    def _sem(self, name):
        self.semh[name] = self.nc.alloc_semaphore(name)
        self.cnt[name] = 0

    def wait(self, eng, tok):
        if tok is None:
            return
        s, v = tok
        if s in self.bounds and not (eng in ("sync", "gpsimd", "scalar") and s.endswith("_" + eng)):
            bl = self.bounds[s]
            if bl[-1] < v:
                cur = self.cnt[s]
                bl.append(cur)
                self.fence[s] = cur
                v = cur
            else:
                import bisect
                v = bl[bisect.bisect_left(bl, v)]
        if self.wm[eng].get(s, 0) >= v:
            return
        self.wm[eng][s] = v
        self.rec[eng].append(("w", s, v))
        self.tag_since_wait[eng] = False

    def op(self, eng, meth, *args, waits=(), tag=False, **kw):
        for t in waits:
            self.wait(eng, t)
        tok = None
        sem = None
        if tag:
            if self.tag_since_wait[eng]:
                sname = "s_" + eng
                lv = self.last_tag[eng]
                self.rec[eng].append(("w", sname, lv))
                if self.wm[eng].get(sname, 0) < lv:
                    self.wm[eng][sname] = lv
            sname = "s_" + eng
            self.cnt[sname] += 1
            tok = (sname, self.cnt[sname])
            sem = sname
            self.tag_since_wait[eng] = True
            self.last_tag[eng] = self.cnt[sname]
        self.rec[eng].append(("i", meth, args, kw, sem, 1))
        self.ninstr += 1
        return tok

    def dma(self, q, out, in_, waits=(), ch=0, **kw):
        sname = "d" + ("", "B", "C")[ch] + "_" + q
        if self.fence[sname] is not None:
            fv = self.fence[sname]
            self.fence[sname] = None
            if self.wm[q].get(sname, 0) < fv:
                self.wm[q][sname] = fv
                self.rec[q].append(("w", sname, fv))
        for t in waits:
            self.wait(q, t)
        self.cnt[sname] += 16
        self.rec[q].append(("i", "dma_start", (), dict(out=out, in_=in_, **kw), sname, 16))
        self.ninstr += 1
        return (sname, self.cnt[sname])

    def barrier(self):
        for q in ("sync", "gpsimd", "scalar"):
            for sfx in ("", "B", "C"):
                sname = "d" + sfx + "_" + q
                cur = self.cnt[sname]
                if self.bounds[sname][-1] < cur:
                    self.bounds[sname].append(cur)
                self.fence[sname] = None
                self.wait(q, (sname, cur))
        toks = [self.op(e, "nop", tag=True) for e in ENGINES]
        for e in ENGINES:
            for t in toks:
                self.wait(e, t)

    def finish(self):
        for q in ("sync", "gpsimd", "scalar"):
            for sfx in ("", "B", "C"):
                sname = "d" + sfx + "_" + q
                cur = self.cnt[sname]
                if self.bounds[sname][-1] < cur:
                    self.bounds[sname].append(cur)
                self.wait("sync", (sname, cur))

    def replay(self):
        with self.nc.Block() as blk:
            for e in ENGINES:
                getattr(blk, e)(lambda eng, e=e: self._run(eng, e))

    def _run(self, eng, ename):
        semh = self.semh
        for item in self.rec[ename]:
            if item[0] == "w":
                eng.wait_ge(semh[item[1]], item[2])
            else:
                _, meth, args, kw, sem, inc = item
                try:
                    inst = getattr(eng, meth)(*args, **kw)
                except Exception as ex:
                    def nm(a):
                        try:
                            return "%s%s" % (a.tensor.name, list(a.shape))
                        except Exception:
                            return str(a)[:40]
                    raise RuntimeError(
                        "replay failed: engine=%s meth=%s args=%s kw=%s: %s"
                        % (ename, meth, [nm(a) for a in args],
                           {k: nm(v) for k, v in kw.items()}, ex)) from ex
                if sem is not None:
                    inst.then_inc(semh[sem], inc)


class Ctx:
    def __init__(self, nc, p, stack):
        self.nc = nc
        self.p = p
        self.hbm = {}
        self.sb = {}
        self.trk = {}
        self.sm = None
        self.gstack = stack
        self.stack = stack
        self._frames = []

    def push(self):
        fr = ExitStack()
        self._frames.append(fr)
        self.stack = fr

    def pop(self):
        fr = self._frames.pop()
        fr.close()
        self.stack = self._frames[-1] if self._frames else self.gstack
        self.p.barrier()

    def sbuf(self, name, shape, dtype):
        self._uniq = getattr(self, "_uniq", 0) + 1
        h = self.stack.enter_context(
            self.nc.sbuf_tensor("%s_u%d" % (name, self._uniq), shape, dtype))
        ap = h[:]
        self.sb[name] = ap
        self.trk[name] = Trk()
        return ap

    def views(self, name, n, shape, dtype):
        ap = self.sbuf(name, [shape[0], n * shape[1]], dtype)
        out = []
        for i in range(n):
            out.append(ap[:, i * shape[1]:(i + 1) * shape[1]])
            self.trk[name + str(i)] = Trk()
        return out

    def psum(self, name, shape, dtype):
        h = self.gstack.enter_context(self.nc.psum_tensor(name, shape, dtype))
        ap = h[:]
        self.sb[name] = ap
        self.trk[name] = Trk()
        return ap

    def cp(self, name, prow=False):
        j = self.cpmap[name]
        ap = self.sb["cpool"]
        return ap[0:1, j:j + 1] if prow else ap[:, j:j + 1]

    def smallc(self, name, off=0, n=1):
        col = self.sm[name] + off
        return self.sb["smalls"][:, col:col + n]


CONSTS = {"eps5": 1e-5, "s8": 0.125, "dq": 1.0 / 127.0, "neg1": -1.0, "one": 1.0}


def emit_consts(c):
    p = c.p
    cpool = c.sbuf("cpool", [128, len(CONSTS)], F32)
    c.cpmap = {}
    for j, (nme, val) in enumerate(CONSTS.items()):
        c.cpmap[nme] = j
        p.op("gpsimd", "memset", cpool[:, j:j + 1], float(val), tag=True)
    ones_f = c.sbuf("ones_f", [128, 128], F32)
    ident_b = c.sbuf("ident_b", [128, 128], BF16)
    mask = c.sbuf("mask", [128, 128], F32)
    smalls = c.sbuf("smalls", [128, c.hbm["smalls"].shape[1]], F32)
    smrow = c.sbuf("smrow", [1, 1040], F32)
    t1 = p.op("gpsimd", "memset", ones_f, 1.0, tag=True)
    t3 = p.op("gpsimd", "memset", ident_b, 0.0, tag=True)
    t4 = p.op("gpsimd", "memset", mask, 0.0, tag=True)
    p.op("gpsimd", "affine_select", waits=[t3], tag=True, out=ident_b,
         in_=ident_b, compare_op=ALU.not_equal, fill=1.0, base=0,
         pattern=[[-1, 128]], channel_multiplier=1)
    p.op("gpsimd", "affine_select", waits=[t4], tag=True, out=mask,
         in_=mask, compare_op=ALU.is_ge, fill=NEG, base=0,
         pattern=[[-1, 128]], channel_multiplier=1)
    p.dma("sync", smalls, c.hbm["smalls"])
    p.dma("sync", smrow, c.hbm["smrow"])
    p.barrier()


def row_recip_sqrt(c, out_row, in_row, eps, scale=1.0, waits=(), floor=None):
    p = c.p
    n = out_row.shape[-1]
    tmp = c.sb["rowtmp"][0:1, 0:n]
    assert eps == 1e-5
    tk = p.op("scalar", "activation", tmp, in_row, AF.Sqrt, bias=c.cp("eps5", prow=True),
              waits=list(waits) + c.trk["rowtmp"].w_waits(), tag=True)
    c.trk["rowtmp"].wr(tk)
    if floor is not None:
        tk = p.op("vector", "tensor_scalar_max", tmp, tmp, float(floor), waits=[tk], tag=True)
        c.trk["rowtmp"].wr(tk)
    tk = p.op("vector", "reciprocal", out_row, tmp, waits=[tk], tag=True)
    c.trk["rowtmp"].rd(tk)
    if scale != 1.0:
        tk = p.op("vector", "tensor_scalar_mul", out_row, out_row, float(scale),
                  waits=[tk], tag=True)
    return tk



DBG_L0 = False
L0_SKIP = ()


def emit_l0(c):
    p = c.p
    t = c.trk
    x = c.hbm["x"]
    xn0_s = c.hbm["xn0_s"]
    x1_s = c.hbm["x1_s"]
    c.push()

    pbig = c.sb["pbig"]
    pv0, pv1, pv2 = c.sb["pv0"], c.sb["pv1"], c.sb["pv2"]
    pt = c.sb["pt"]

    xch = c.views("xch", 2, [128, 512], F32)
    arow = c.sbuf("arow", [1, 512], F32)
    brow = c.sbuf("brow", [1, 512], F32)
    c.sbuf("rowtmp", [1, 1024], F32)
    for nme in ("mrow", "vrow"):
        c.sbuf(nme, [1, 512], F32)
    abc = c.sbuf("abc", [128, 512], F32)
    bbc = c.sbuf("bbc", [128, 512], F32)
    xn = c.sbuf("xn", [128, DC * 512], BF16)
    _shr = 16 if "smallbuf" in L0_SKIP else 1
    wq_sb = c.sbuf("wq_sb", [128, NPAIR * 1024 // _shr], BF16)
    wk_sb = c.sbuf("wk_sb", [128, NPAIR * 1024 // _shr], BF16)
    wsh = c.sbuf("wsh", [128, DC * 1024 // _shr], BF16)
    q_sb = c.sbuf("q_sb", [128, NPAIR * 2048 // _shr], BF16)
    k_sb = c.sbuf("k_sb", [128, NPAIR * 2048 // _shr], BF16)
    v_sb = c.sbuf("v_sb", [128, 16 * 1024 // _shr], BF16)
    for pr in range(NPAIR):
        t["q%d" % pr] = Trk(); t["k%d" % pr] = Trk(); t["o%d" % pr] = Trk()
    for tc in range(16):
        t["v%d" % tc] = Trk()
    psb = c.views("psb", 1, [128, 2048], BF16)
    ptsb = c.views("ptsb", 1, [128, 2048], BF16)
    lsb = c.views("lsb", 1, [128, 1], F32)
    nm4 = c.sbuf("nm4", [128, 4], F32)
    l4 = c.sbuf("l4", [128, 4], F32)
    nmax = c.views("nmax", 1, [128, 1], F32)
    ebias = c.views("ebias", 1, [128, 1], F32)
    rsb = c.views("rsb", 1, [128, 1], F32)
    tmpf = c.views("tmpf", 2, [128, 512], F32)
    bv_bc = c.sbuf("bv_bc", [128, D], F32)
    t["ptg0"] = Trk(); t["ptg1"] = Trk()

    ones = c.sb["ones_f"]
    tbv = None
    for hf in range(2 if "bvbc" not in L0_SKIP else 0):
        pvh, trh = ((pv0, t["pv0"]), (pv1, t["pv1"]))[hf]
        bm = p.op("tensor", "matmul", pvh[:, :], ones[0:1, :],
                  c.sb["smrow"][0:1, 512 * hf:512 * (hf + 1)],
                  waits=trh.w_waits(), start=True, stop=True, tag=True)
        trh.wr(bm)
        tbv = p.op("scalar", "copy", out=bv_bc[:, 512 * hf:512 * (hf + 1)], in_=pvh[:, :],
                   waits=[bm], tag=True)
        trh.rd(tbv)
    t["bv_bc"].wr(tbv)

    def load_wsh(src_list):
        tok = None
        for i, src in enumerate(src_list):
            tok = p.dma("gpsimd", wsh[:, 1024 * i:1024 * (i + 1)], src,
                        waits=t["wsh"].w_waits() if i == 0 else ())
        t["wsh"].wr(tok)
        return tok

    for pr in range(NPAIR if "wload" not in L0_SKIP else 0):
        tok = p.dma("gpsimd", wq_sb[:, 1024 * pr:1024 * (pr + 1)], c.hbm["wq"][pr])
        t["wq_sb"].wr(tok)
        tok = p.dma("gpsimd", wk_sb[:, 1024 * pr:1024 * (pr + 1)], c.hbm["wk"][pr])
        t["wk_sb"].wr(tok)
    wq_tok = t["wq_sb"].w
    wk_tok = t["wk_sb"].w

    for s in range(BL):
        wv_tok = None if "wload" in L0_SKIP else load_wsh(
            [c.hbm["wv"][128 * dc:128 * (dc + 1), :] for dc in range(DC)])
        for ch in range(4):
            cc = slice(512 * ch, 512 * (ch + 1))
            if "stats" in L0_SKIP:
                xtoks = []
                for dc in range(DC):
                    par = dc % 2
                    ltk = p.dma("sync", xch[par], x[s, dc, :, cc],
                                waits=t["xch%d" % par].w_waits(), ch=par)
                    t["xch%d" % par].wr(ltk)
                    xnv = xn[:, 512 * dc:512 * (dc + 1)]
                    tk2 = p.op("vector", "tensor_copy", xnv, xch[par],
                               waits=[ltk] + (t["xn"].w_waits() if dc == 0 else []), tag=True)
                    t["xch%d" % par].rd(tk2)
                    xtoks.append(tk2)
                    dtk = p.dma("sync", xn0_s[s, dc, :, cc], xnv, waits=[tk2])
                    t["xn"].rd(dtk)
                t["xn"].wr(xtoks[-1])
            else:
                mm1 = mm2 = None
                for dc in range(DC):
                    par = dc % 2
                    tk = p.dma("sync", xch[par], x[s, dc, :, cc],
                               waits=t["xch%d" % par].w_waits(), ch=par)
                    t["xch%d" % par].wr(tk)
                    stk = p.op("scalar", "activation", tmpf[1], xch[par],
                               AF.Square, waits=[tk] + t["tmpf1"].w_waits(), tag=True)
                    t["tmpf1"].wr(stk); t["xch%d" % par].rd(stk)
                    mm1 = p.op("tensor", "matmul", pv0[0:1, :], ones[:, 0:1], xch[par],
                               waits=[tk] + (t["pv0"].w_waits() if dc == 0 else []),
                               start=(dc == 0), stop=(dc == DC - 1), tag=True)
                    mm2 = p.op("tensor", "matmul", pv1[0:1, :], ones[:, 0:1], tmpf[1],
                               waits=[stk] + (t["pv1"].w_waits() if dc == 0 else []),
                               start=(dc == 0), stop=(dc == DC - 1), tag=True)
                    t["tmpf1"].rd(mm2)
                    t["xch%d" % par].rd(mm1)
                t["pv0"].wr(mm1)
                t["pv1"].wr(mm2)
                mrow = c.sb["mrow"][0:1, :]
                vrow = c.sb["vrow"][0:1, :]
                m2r = c.sb["rowtmp"][0:1, 512:1024]
                tk_m = p.op("vector", "tensor_scalar_mul", mrow, pv0[0:1, :], 1.0 / D,
                            waits=[mm1] + t["mrow"].w_waits(), tag=True)
                t["pv0"].rd(tk_m); t["mrow"].wr(tk_m)
                tk_v = p.op("vector", "tensor_scalar_mul", vrow, pv1[0:1, :], 1.0 / D,
                            waits=[mm2] + t["vrow"].w_waits(), tag=True)
                t["pv1"].rd(tk_v); t["vrow"].wr(tk_v)
                tk_m2 = p.op("vector", "tensor_mul", m2r, mrow, mrow,
                             waits=[tk_m] + t["rowtmp"].w_waits(), tag=True)
                t["rowtmp"].wr(tk_m2)
                tk_v2 = p.op("vector", "tensor_sub", vrow, vrow, m2r,
                             waits=[tk_m2, tk_v], tag=True)
                t["vrow"].wr(tk_v2); t["rowtmp"].rd(tk_v2)
                av = arow[0:1, :]
                bv = brow[0:1, :]
                tk_a = row_recip_sqrt(c, av, vrow, 1e-5, waits=[tk_v2])
                t["vrow"].rd(tk_a); t["arow"].wr(tk_a)
                tk_b = p.op("vector", "scalar_tensor_tensor", bv, mrow, -1.0, av,
                            ALU.mult, ALU.mult, waits=[tk_a, tk_m] + t["brow"].w_waits(),
                            tag=True)
                t["mrow"].rd(tk_b); t["arow"].rd(tk_b); t["brow"].wr(tk_b)
                bca = p.op("tensor", "matmul", pv0[:, :], ones[0:1, :], av,
                           waits=[tk_a] + t["pv0"].w_waits(), start=True, stop=True, tag=True)
                t["arow"].rd(bca); t["pv0"].wr(bca)
                bcb = p.op("tensor", "matmul", pv1[:, :], ones[0:1, :], bv,
                           waits=[tk_b] + t["pv1"].w_waits(), start=True, stop=True, tag=True)
                t["brow"].rd(bcb); t["pv1"].wr(bcb)
                ca = p.op("scalar", "copy", waits=[bca] + t["abc"].w_waits(), tag=True,
                          out=abc, in_=pv0[:, :])
                t["abc"].wr(ca); t["pv0"].rd(ca)
                cb = p.op("scalar", "copy", waits=[bcb] + t["bbc"].w_waits(), tag=True,
                          out=bbc, in_=pv1[:, :])
                t["bbc"].wr(cb); t["pv1"].rd(cb)
                xtoks = []
                dtks = []
                for dc in range(DC):
                    par = dc % 2
                    ltk = p.dma("sync", xch[par], x[s, dc, :, cc],
                                waits=t["xch%d" % par].w_waits(), ch=par)
                    t["xch%d" % par].wr(ltk)
                    xv = xch[par]
                    xnv = xn[:, 512 * dc:512 * (dc + 1)]
                    tk1 = p.op("vector", "tensor_mul", tmpf[0], xv, abc,
                               waits=[ca, ltk] + t["tmpf0"].w_waits(), tag=True)
                    t["tmpf0"].wr(tk1); t["abc"].rd(tk1); t["xch%d" % par].rd(tk1)
                    tk2 = p.op("vector", "tensor_tensor", xnv, tmpf[0], bbc, ALU.add,
                               waits=[tk1, cb] + (t["xn"].w_waits() if dc == 0 else []),
                               tag=True)
                    t["tmpf0"].rd(tk2); t["bbc"].rd(tk2)
                    xtoks.append(tk2)
                    dtk = p.dma("sync", xn0_s[s, dc, :, cc], xnv, waits=[tk2])
                    dtks.append(dtk)
                t["xn"].wr(xtoks[-1])
                for dtk in dtks:
                    t["xn"].rd(dtk)
            for pr in range(NPAIR if "qk" not in L0_SKIP else 0):
                ps_q, trq = (pv0, t["pv0"]) if pr % 2 == 0 else (pv1, t["pv1"])
                mmq = None
                for kc in range(DC):
                    w = [xtoks[kc], wq_tok] + (trq.w_waits() if kc == 0 else [])
                    mmq = p.op("tensor", "matmul", ps_q[:, :],
                               wq_sb[:, 1024 * pr + 128 * kc:1024 * pr + 128 * (kc + 1)],
                               xn[:, 512 * kc:512 * (kc + 1)], waits=w,
                               start=(kc == 0), stop=(kc == DC - 1), tag=(kc == DC - 1))
                    t["xn"].rd(mmq)
                trq.wr(mmq)
                eq = p.op("scalar", "activation",
                          q_sb[:, 2048 * pr + cc.start:2048 * pr + cc.stop], ps_q[:, :],
                          AF.Identity, bias=c.smallc("bq", pr),
                          waits=[mmq] + t["q%d" % pr].w_waits(), tag=True)
                trq.rd(eq); t["q%d" % pr].wr(eq)
                mmk = None
                for kc in range(DC):
                    w = [xtoks[kc], wk_tok] + (trq.w_waits() if kc == 0 else [])
                    mmk = p.op("tensor", "matmul", ps_q[:, :],
                               wk_sb[:, 1024 * pr + 128 * kc:1024 * pr + 128 * (kc + 1)],
                               xn[:, 512 * kc:512 * (kc + 1)], waits=w,
                               start=(kc == 0), stop=(kc == DC - 1), tag=(kc == DC - 1))
                    t["xn"].rd(mmk)
                trq.wr(mmk)
                ek = p.op("scalar", "activation",
                          k_sb[:, 2048 * pr + cc.start:2048 * pr + cc.stop], ps_q[:, :],
                          AF.Identity, bias=c.smallc("bk", pr),
                          waits=[mmk] + t["k%d" % pr].w_waits(), tag=True)
                trq.rd(ek); t["k%d" % pr].wr(ek)
            for t4 in range(4 if "v" not in L0_SKIP else 0):
                tc = 4 * ch + t4
                for half in range(2):
                    i = (t4 * 2 + half) % 4
                    pv = pbig[:, 512 * i:512 * (i + 1)]
                    trv = t["pb%d" % i]
                    mmv = None
                    for kc in range(DC):
                        w = [xtoks[kc], wv_tok] + (trv.w_waits() if kc == 0 else [])
                        mmv = p.op("tensor", "matmul", pv,
                                   xn[:, 512 * kc + 128 * t4:512 * kc + 128 * (t4 + 1)],
                                   wsh[:, 1024 * kc + 512 * half:1024 * kc + 512 * (half + 1)],
                                   waits=w, start=(kc == 0), stop=(kc == DC - 1),
                                   tag=(kc == DC - 1))
                        t["xn"].rd(mmv); t["wsh"].rd(mmv)
                    trv.wr(mmv)
                    ev = p.op("vector", "tensor_tensor",
                              v_sb[:, 1024 * tc + 512 * half:1024 * tc + 512 * (half + 1)],
                              pv, bv_bc[:, 512 * half:512 * (half + 1)], ALU.add,
                              waits=[mmv, tbv] + t["v%d" % tc].w_waits(), tag=True)
                    trv.rd(ev); t["v%d" % tc].wr(ev)

        if DBG_L0 and s == 0:
            for pr in range(NPAIR):
                dtk = p.dma("sync", c.hbm["dbg_q"][:, 2048 * pr:2048 * (pr + 1)],
                            q_sb[:, 2048 * pr:2048 * (pr + 1)], waits=[t["q%d" % pr].w])
                t["q%d" % pr].rd(dtk)
                dtk = p.dma("sync", c.hbm["dbg_k"][:, 2048 * pr:2048 * (pr + 1)],
                            k_sb[:, 2048 * pr:2048 * (pr + 1)], waits=[t["k%d" % pr].w])
                t["k%d" % pr].rd(dtk)
            for tc in range(16):
                dtk = p.dma("sync", c.hbm["dbg_v"][:, 1024 * tc:1024 * (tc + 1)],
                            v_sb[:, 1024 * tc:1024 * (tc + 1)], waits=[t["v%d" % tc].w])
                t["v%d" % tc].rd(dtk)
        prev = None
        NSC = 16
        if "attn" in L0_SKIP:
            NSC = 0
        elif "attn1" in L0_SKIP:
            NSC = 1
        elif "attn4" in L0_SKIP:
            NSC = 4
        elif "attn8" in L0_SKIP:
            NSC = 8
        for sc in range(NSC):
            for h in range(NH):
                pr, h2 = h // 2, h % 2
                par = 0
                nt = sc + 1
                t_len = nt * 128
                qrows = slice(64 * h2, 64 * (h2 + 1))
                smtoks = []
                for jb in range((t_len + 511) // 512):
                    wdt = min(512, t_len - 512 * jb)
                    wts = [t["q%d" % pr].w, t["k%d" % pr].w] + t["pb%d" % jb].w_waits()
                    mm = p.op("tensor", "matmul", pbig[:, 512 * jb:512 * jb + wdt],
                              q_sb[qrows, 2048 * pr + 128 * sc:2048 * pr + 128 * (sc + 1)],
                              k_sb[qrows, 2048 * pr + 512 * jb:2048 * pr + 512 * jb + wdt],
                              waits=wts, start=True, stop=True, tag=True)
                    t["pb%d" % jb].wr(mm)
                    t["q%d" % pr].rd(mm)
                    t["k%d" % pr].rd(mm)
                    smtoks.append(mm)
                if prev is not None:
                    _l0_pt_pv(c, *prev)
                    prev = None
                tk_mask = p.op("vector", "tensor_tensor", pbig[:, t_len - 128:t_len],
                               pbig[:, t_len - 128:t_len], c.sb["mask"], ALU.add,
                               waits=[smtoks[-1]], tag=True)
                t["pb%d" % ((t_len - 1) // 512)].wr(tk_mask)
                njb = (t_len + 511) // 512
                mptoks = []
                for jb in range(njb):
                    wdt = min(512, t_len - 512 * jb)
                    w = [smtoks[jb], tk_mask]
                    if jb == 0:
                        w += t["nm4"].w_waits()
                    tk = p.op("vector", "tensor_reduce", nm4[:, jb:jb + 1],
                              pbig[:, 512 * jb:512 * jb + wdt], AX.X, ALU.max,
                              waits=w, tag=True)
                    t["pb%d" % jb].rd(tk)
                    mptoks.append(tk)
                t["nm4"].wr(mptoks[-1])
                tk_max = p.op("vector", "tensor_reduce", nmax[par], nm4[:, 0:njb],
                              AX.X, ALU.max, negate=True,
                              waits=mptoks + t["nmax" + str(par)].w_waits(), tag=True)
                t["nm4"].rd(tk_max)
                t["nmax" + str(par)].wr(tk_max)
                tk_eb = p.op("vector", "tensor_scalar_mul", ebias[par], nmax[par], 0.125,
                             waits=[tk_max] + t["ebias" + str(par)].w_waits(), tag=True)
                t["ebias" + str(par)].wr(tk_eb); t["nmax" + str(par)].rd(tk_eb)
                eptoks = []
                for jb in range(njb):
                    wdt = min(512, t_len - 512 * jb)
                    w = [tk_eb] + (t["psb%d" % par].w_waits() if jb == 0 else [])
                    if jb == 0:
                        w += t["l4"].w_waits()
                    tk = p.op("scalar", "activation", psb[par][:, 512 * jb:512 * jb + wdt],
                              pbig[:, 512 * jb:512 * jb + wdt], AF.Exp, scale=c.cp("s8"),
                              bias=ebias[par], accum_out=l4[:, jb:jb + 1],
                              waits=w, tag=True)
                    t["pb%d" % jb].rd(tk)
                    eptoks.append(tk)
                tk_exp = eptoks[-1]
                t["ebias" + str(par)].rd(tk_exp)
                t["l4"].wr(tk_exp)
                tk_ls = p.op("vector", "tensor_reduce", lsb[par], l4[:, 0:njb], AX.X,
                             ALU.add, waits=eptoks + t["lsb%d" % par].w_waits(), tag=True)
                t["l4"].rd(tk_ls)
                t["lsb%d" % par].wr(tk_ls)
                tk_r = p.op("vector", "reciprocal", rsb[par], lsb[par],
                            waits=[tk_ls] + t["rsb%d" % par].w_waits(), tag=True)
                t["lsb%d" % par].rd(tk_r); t["rsb%d" % par].wr(tk_r)
                tk_pn = p.op("vector", "tensor_scalar_mul", psb[par][:, 0:t_len],
                             psb[par][:, 0:t_len], rsb[par],
                             waits=eptoks + [tk_r], tag=True)
                t["rsb%d" % par].rd(tk_pn)
                t["psb%d" % par].wr(tk_pn)
                prev = (sc, h, par, nt)
        if prev is not None:
            _l0_pt_pv(c, *prev)

        if DBG_L0 and s == 0:
            for pr in range(NPAIR):
                dtk = p.dma("sync", c.hbm["dbg_o"][:, 2048 * pr:2048 * (pr + 1)],
                            q_sb[:, 2048 * pr:2048 * (pr + 1)], waits=[t["o%d" % pr].w])
                t["o%d" % pr].rd(dtk)
        wo_tok = None if "wload" in L0_SKIP else load_wsh([c.hbm["wo0"][mc] for mc in range(DC)])
        if "outproj" in L0_SKIP and "dummyout" in L0_SKIP:
            zk = p.op("vector", "memset", tmpf[0], 0.0,
                      waits=t["tmpf0"].w_waits(), tag=True)
            t["tmpf0"].wr(zk)
            for mc in range(DC):
                for ch4 in range(4):
                    dtk = p.dma("sync", x1_s[s, mc, :, 512 * ch4:512 * (ch4 + 1)],
                                tmpf[0], waits=[zk])
                    t["tmpf0"].rd(dtk)
        for ch in range(4 if "outproj" not in L0_SKIP else 0):
            cc = slice(512 * ch, 512 * (ch + 1))
            for mc in range(DC):
                xpar = mc % 2
                ltk = p.dma("sync", xch[xpar], x[s, mc, :, cc],
                            waits=t["xch%d" % xpar].w_waits())
                t["xch%d" % xpar].wr(ltk)
                po, tro = ((pv0, t["pv0"]), (pv1, t["pv1"]), (pv2, t["pv2"]))[mc % 3]
                mmo = None
                for kc in range(DC):
                    w = [t["o%d" % kc].w, wo_tok] + (tro.w_waits() if kc == 0 else [])
                    mmo = p.op("tensor", "matmul", po[:, :],
                               wsh[:, 1024 * mc + 128 * kc:1024 * mc + 128 * (kc + 1)],
                               q_sb[:, 2048 * kc + cc.start:2048 * kc + cc.stop],
                               waits=w, start=(kc == 0), stop=(kc == DC - 1),
                               tag=(kc == DC - 1))
                    t["o%d" % kc].rd(mmo)
                    t["wsh"].rd(mmo)
                tro.wr(mmo)
                par = mc % 2
                tb = p.op("scalar", "activation", tmpf[par], po[:, :], AF.Identity,
                          bias=c.smallc("bo0", mc),
                          waits=[mmo] + t["tmpf%d" % par].w_waits(), tag=True)
                tro.rd(tb); t["tmpf%d" % par].wr(tb)
                ta = p.op("vector", "tensor_tensor", tmpf[par], tmpf[par],
                          xch[xpar], ALU.add,
                          waits=[tb, ltk], tag=True)
                t["xch%d" % xpar].rd(ta); t["tmpf%d" % par].wr(ta)
                dtk = p.dma("sync", x1_s[s, mc, :, cc], tmpf[par], waits=[ta])
                t["tmpf%d" % par].rd(dtk)
    c.pop()


def _l0_pt_pv(c, sc, h, par, nt):
    p = c.p
    t = c.trk
    pr, h2 = h // 2, h % 2
    pt = c.sb["pt"]
    pv2 = c.sb["pv2"]
    ptv = c.sb["ptsb"][:, 2048 * par:2048 * (par + 1)] if False else None
    ptv = c.sb["ptsb"][:, 2048 * par:2048 * (par + 1)]
    ptrk = t["ptsb%d" % par]
    pn_tok = t["psb%d" % par].w
    psbv = c.sb["psb"][:, 2048 * par:2048 * (par + 1)]
    groups = []
    for lo in range(0, nt, 4):
        hi = min(lo + 4, nt) - 1
        gtrk = t["pt"]
        tk = None
        for tb in range(lo, hi + 1):
            slot = (tb % 8)
            wts = [pn_tok]
            if tb == lo:
                wts += gtrk.w_waits()
            tk = p.op("tensor", "transpose", pt[:, 128 * slot:128 * (slot + 1)],
                      psbv[:, 128 * tb:128 * (tb + 1)], c.sb["ident_b"],
                      waits=wts, tag=(tb == hi))
        gtrk.wr(tk)
        t["psb%d" % par].rd(tk)
        wdt = (hi - lo + 1) * 128
        ck = p.op("vector", "tensor_copy",
                  ptv[:, 128 * lo:128 * lo + wdt],
                  pt[:, 128 * (lo % 8):128 * (lo % 8) + wdt],
                  waits=[tk] + (ptrk.w_waits() if lo == 0 else []), tag=True)
        gtrk.rd(ck)
        groups.append((lo, hi, ck))
    ptrk.wr(groups[-1][2])
    out = pv2[slice(64 * h2, 64 * (h2 + 1)), 0:128]
    first = True
    mm = None
    for lo, hi, ck in groups:
        for tb in range(lo, hi + 1):
            wts = [ck, t["v%d" % tb].w]
            if first:
                wts += t["pv2"].w_waits()
            mm = p.op("tensor", "matmul", out,
                      c.sb["v_sb"][:, 1024 * tb + 64 * h:1024 * tb + 64 * (h + 1)],
                      ptv[:, 128 * tb:128 * (tb + 1)],
                      waits=wts, start=first, stop=(tb == nt - 1), tag=(tb == nt - 1))
            t["v%d" % tb].rd(mm)
            ptrk.rd(mm)
            first = False
    t["pv2"].wr(mm)
    eo = p.op("scalar", "activation",
              c.sb["q_sb"][slice(64 * h2, 64 * (h2 + 1)),
                           2048 * pr + 128 * sc:2048 * pr + 128 * (sc + 1)],
              out, AF.Copy, waits=[mm] + t["q%d" % pr].w_waits(), tag=True)
    t["pv2"].rd(eo)
    t["o%d" % pr].wr(eo)


def alloc_psum_pool(c):
    c.psum("pbig", [128, 2048], F32)
    for i in range(4):
        c.trk["pb%d" % i] = Trk()
    c.psum("pv0", [128, 512], F32)
    c.psum("pv1", [128, 512], F32)
    c.psum("pv2", [128, 512], F32)
    c.psum("pt", [128, 1024], BF16)



def emit_ffn0(c):
    p = c.p
    t = c.trk
    xn0_s = c.hbm["xn0_s"]
    x1_s = c.hbm["x1_s"]
    xnew_s = c.hbm["xnew_s"]
    xn2_s = c.hbm["xn2_s"]
    c.push()

    pbig = c.sb["pbig"]
    pv0, pv1, pv2 = c.sb["pv0"], c.sb["pv1"], c.sb["pv2"]

    wgb = c.sbuf("wgb", [128, 24 * 1024], BF16)
    wub = c.sbuf("wub", [128, 24 * 1024], BF16)
    wdb = c.sbuf("wdb", [128, 8 * 3072], BF16)
    c.push()
    stg = c.views("stg", 2, [128, 3072], F32)
    DQ = 1.0 / 127.0
    jobs = ([("wg0", wgb, mc, 1024) for mc in range(24)]
            + [("wu0", wub, mc, 1024) for mc in range(24)]
            + [("wd0", wdb, mc, 3072) for mc in range(8)])
    for j, (src, dst, mc, wdt) in enumerate(jobs):
        par = j % 2
        sv = stg[par][:, 0:wdt]
        dtk = p.dma("gpsimd", sv, c.hbm[src][mc], waits=t["stg%d" % par].w_waits(), ch=par)
        t["stg%d" % par].wr(dtk)
        ck = p.op("scalar", "mul", dst[:, wdt * mc:wdt * (mc + 1)], sv, c.cp("dq"),
                  waits=[dtk], tag=True)
        t["stg%d" % par].rd(ck)
        t[{id(wgb): "wgb", id(wub): "wub", id(wdb): "wdb"}[id(dst)]].wr(ck)
    c.pop()
    wg_tok, wu_tok, wd_tok = t["wgb"].w, t["wub"].w, t["wdb"].w

    xnch = c.sbuf("xnch", [128, DC * 256], BF16)
    h_sb = c.sbuf("h_sb", [128, 24 * 256], BF16)
    x1ch = c.views("x1ch", 2, [128, 256], F32)
    xnt = c.views("xnt", 2, [128, 256], F32)
    xnbf = c.sbuf("xnbf", [128, DC * 256], BF16)
    tmpf = c.views("tmpf", 2, [128, 256], F32)
    abc = c.sbuf("abc", [128, 256], F32)
    bbc = c.sbuf("bbc", [128, 256], F32)
    arow = c.sbuf("arow", [1, 256], F32)
    brow = c.sbuf("brow", [1, 256], F32)
    mrow = c.sbuf("mrow", [1, 256], F32)
    vrow = c.sbuf("vrow", [1, 256], F32)
    c.sbuf("rowtmp", [1, 512], F32)

    for s in range(BL):
        for ch in range(8):
            cc = slice(256 * ch, 256 * (ch + 1))
            ltoks = []
            for dc in range(DC):
                tk = p.dma("sync", xnch[:, 256 * dc:256 * (dc + 1)], xn0_s[s, dc, :, cc],
                           waits=(t["xnch"].w_waits() if dc == 0 else ()))
                ltoks.append(tk)
            t["xnch"].wr(ltoks[-1])
            for mc in range(24):
                pg = pbig[:, 512 * (mc % 2):512 * (mc % 2) + 256]
                trg = t["pb%d" % (mc % 2)]
                pu = pbig[:, 1024 + 256 * (mc % 2):1024 + 256 * (mc % 2) + 256]
                tru = t["pb2"]
                mmg = mmu = None
                for kc in range(DC):
                    w = [ltoks[kc], wg_tok] + (trg.w_waits() if kc == 0 else [])
                    mmg = p.op("tensor", "matmul", pg,
                               wgb[:, 1024 * mc + 128 * kc:1024 * mc + 128 * (kc + 1)],
                               xnch[:, 256 * kc:256 * (kc + 1)], waits=w,
                               start=(kc == 0), stop=(kc == DC - 1), tag=(kc == DC - 1))
                    t["xnch"].rd(mmg)
                    w = [ltoks[kc], wu_tok] + (tru.w_waits() if kc == 0 else [])
                    mmu = p.op("tensor", "matmul", pu,
                               wub[:, 1024 * mc + 128 * kc:1024 * mc + 128 * (kc + 1)],
                               xnch[:, 256 * kc:256 * (kc + 1)], waits=w,
                               start=(kc == 0), stop=(kc == DC - 1), tag=(kc == DC - 1))
                    t["xnch"].rd(mmu)
                trg.wr(mmg)
                tru.wr(mmu)
                par = mc % 2
                ts1 = p.op("scalar", "activation", tmpf[par], pg, AF.Silu,
                           waits=[mmg] + t["tmpf%d" % par].w_waits(), tag=True)
                trg.rd(ts1); t["tmpf%d" % par].wr(ts1)
                th = p.op("vector", "tensor_tensor", h_sb[:, 256 * mc:256 * (mc + 1)],
                          tmpf[par], pu, ALU.mult,
                          waits=[ts1, mmu] + (t["h_sb"].w_waits() if mc == 0 else []),
                          tag=True)
                t["tmpf%d" % par].rd(th); tru.rd(th)
            t["h_sb"].wr(th)
            stat_first = True
            smm = None
            for mc2 in range(DC):
                par = mc2 % 2
                pd = (pv0, pv1)[par]
                trd = t["pv%d" % par]
                mmd = None
                for kc in range(24):
                    w = [th, wd_tok] + (trd.w_waits() if kc == 0 else [])
                    mmd = p.op("tensor", "matmul", pd[:, 0:256],
                               wdb[:, 3072 * mc2 + 128 * kc:3072 * mc2 + 128 * (kc + 1)],
                               h_sb[:, 256 * kc:256 * (kc + 1)], waits=w,
                               start=(kc == 0), stop=(kc == 23), tag=(kc == 23))
                    t["h_sb"].rd(mmd)
                trd.wr(mmd)
                xl = p.dma("sync", x1ch[par], x1_s[s, mc2, :, cc],
                           waits=t["x1ch%d" % par].w_waits(), ch=par)
                t["x1ch%d" % par].wr(xl)
                ta = p.op("vector", "tensor_tensor", xnt[par], pd[:, 0:256], x1ch[par],
                          ALU.add, waits=[mmd, xl] + t["xnt%d" % par].w_waits(), tag=True)
                trd.rd(ta); t["x1ch%d" % par].rd(ta); t["xnt%d" % par].wr(ta)
                dnew = p.dma("sync", xnew_s[s, mc2, :, cc], xnt[par], waits=[ta])
                t["xnt%d" % par].rd(dnew)
                cbf = p.op("vector", "tensor_copy", xnbf[:, 256 * mc2:256 * (mc2 + 1)],
                           xnt[par], waits=[ta] + (t["xnbf"].w_waits() if mc2 == 0 else []),
                           tag=True)
                t["xnt%d" % par].rd(cbf)
                sk = p.op("scalar", "activation", tmpf[par], xnt[par], AF.Square,
                          waits=[ta] + t["tmpf%d" % par].w_waits(), tag=True)
                t["xnt%d" % par].rd(sk); t["tmpf%d" % par].wr(sk)
                w = [ta] + (t["pb3"].w_waits() if stat_first else [])
                smm = p.op("tensor", "matmul", pbig[:, 1536:1537][0:1, :].tensor[0:1, 1536:1792]
                           if False else pbig[0:1, 1536:1792], c.sb["ones_f"][:, 0:1],
                           xnt[par], waits=w, start=stat_first, stop=False, tag=True)
                t["xnt%d" % par].rd(smm)
                stat_first = False
                smm = p.op("tensor", "matmul", pbig[0:1, 1792:2048], c.sb["ones_f"][:, 0:1],
                           tmpf[par], start=False, stop=(mc2 == DC - 1),
                           waits=[sk], tag=True)
                t["tmpf%d" % par].rd(smm)
            t["pb3"].wr(smm)
            t["xnbf"].wr(cbf)
            tk_m = p.op("vector", "tensor_scalar_mul", mrow[0:1, :], pbig[0:1, 1536:1792],
                        1.0 / D, waits=[smm] + t["mrow"].w_waits(), tag=True)
            t["pb3"].rd(tk_m); t["mrow"].wr(tk_m)
            tk_v = p.op("vector", "tensor_scalar_mul", vrow[0:1, :], pbig[0:1, 1792:2048],
                        1.0 / D, waits=[smm] + t["vrow"].w_waits(), tag=True)
            t["pb3"].rd(tk_v); t["vrow"].wr(tk_v)
            m2r = c.sb["rowtmp"][0:1, 256:512]
            tk_m2 = p.op("vector", "tensor_mul", m2r, mrow[0:1, :], mrow[0:1, :],
                         waits=[tk_m] + t["rowtmp"].w_waits(), tag=True)
            t["rowtmp"].wr(tk_m2)
            tk_v2 = p.op("vector", "tensor_sub", vrow[0:1, :], vrow[0:1, :], m2r,
                         waits=[tk_m2, tk_v], tag=True)
            t["vrow"].wr(tk_v2); t["rowtmp"].rd(tk_v2)
            tk_a = row_recip_sqrt(c, arow[0:1, :], vrow[0:1, :], 1e-5, waits=[tk_v2])
            t["vrow"].rd(tk_a); t["arow"].wr(tk_a)
            tk_b = p.op("vector", "scalar_tensor_tensor", brow[0:1, :], mrow[0:1, :],
                        -1.0, arow[0:1, :], ALU.mult, ALU.mult,
                        waits=[tk_a, tk_m] + t["brow"].w_waits(), tag=True)
            t["mrow"].rd(tk_b); t["arow"].rd(tk_b); t["brow"].wr(tk_b)
            bca = p.op("tensor", "matmul", pv2[:, 0:256], c.sb["ones_f"][0:1, :],
                       arow[0:1, :], waits=[tk_a] + t["pv2"].w_waits(),
                       start=True, stop=True, tag=True)
            t["arow"].rd(bca); t["pv2"].wr(bca)
            bcb = p.op("tensor", "matmul", pbig[:, 1536:1792], c.sb["ones_f"][0:1, :],
                       brow[0:1, :], waits=[tk_b] + t["pb3"].w_waits(),
                       start=True, stop=True, tag=True)
            t["brow"].rd(bcb); t["pb3"].wr(bcb)
            ca = p.op("scalar", "copy", out=abc, in_=pv2[:, 0:256],
                      waits=[bca] + t["abc"].w_waits(), tag=True)
            t["pv2"].rd(ca); t["abc"].wr(ca)
            cb = p.op("scalar", "copy", out=bbc, in_=pbig[:, 1536:1792],
                      waits=[bcb] + t["bbc"].w_waits(), tag=True)
            t["pb3"].rd(cb); t["bbc"].wr(cb)
            dtks2 = []
            for mc2 in range(DC):
                par = mc2 % 2
                tk1 = p.op("vector", "tensor_mul", tmpf[par],
                           xnbf[:, 256 * mc2:256 * (mc2 + 1)], abc,
                           waits=[ca, cbf] + t["tmpf%d" % par].w_waits(), tag=True)
                t["abc"].rd(tk1); t["xnbf"].rd(tk1); t["tmpf%d" % par].wr(tk1)
                tk2 = p.op("vector", "tensor_tensor", xnch[:, 256 * mc2:256 * (mc2 + 1)],
                           tmpf[par], bbc, ALU.add,
                           waits=[tk1, cb] + (t["xnch"].w_waits() if mc2 == 0 else []),
                           tag=True)
                t["tmpf%d" % par].rd(tk2); t["bbc"].rd(tk2)
                dtk = p.dma("sync", xn2_s[s, mc2, :, cc],
                            xnch[:, 256 * mc2:256 * (mc2 + 1)], waits=[tk2])
                dtks2.append(dtk)
            t["xnch"].wr(tk2)
            for dtk in dtks2:
                t["xnch"].rd(dtk)
    c.pop()



_PHASES = {}


def _register():
    _PHASES["l0"] = emit_l0
    _PHASES["ffn0"] = emit_ffn0
    for nme in ("l1ab", "l1c", "l1ffn"):
        if nme not in _PHASES and ("emit_" + nme) in globals():
            _PHASES[nme] = globals()["emit_" + nme]


WEIGHT_TENS = {
    "wq": F32, "wk": F32, "wv": F32, "wo0": F32,
    "wg0": I32, "wu0": I32, "wd0": I32,
    "w1": F32, "w2": F32, "wz": F32, "wh": F32, "wo1": F32,
    "ws1": F32, "w1f": F32, "w2f": F32, "ws2": F32, "w1s": F32, "w2s": F32,
    "smalls": F32, "smrow": F32,
}

SCRATCH = {
    "xn0_s": ([BL, DC, 128, S], BF16, "l0", ("ffn0",)),
    "x1_s": ([BL, DC, 128, S], F32, "l0", ("ffn0",)),
    "xnew_s": ([BL, DC, 128, S], F32, "ffn0", ("l1c",)),
    "xn2_s": ([BL, DC, 128, S], BF16, "ffn0", ("l1ab", "l1ffn")),
    "xskip_s": ([32, 128, BL, S], BF16, "l1ab", ("l1c",)),
    "h_s": ([32, 128, BL, S], BF16, "l1ab", ("l1c",)),
    "acc_s": ([BL, DC, 128, S], F32, "l1c", ("l1ffn",)),
}

PHASE_USED_W = {
    "l0": ("wq", "wk", "wv", "wo0"),
    "ffn0": ("wg0", "wu0", "wd0"),
    "l1ab": ("w1", "w2", "wz", "wh"),
    "l1c": ("wo1",),
    "l1ffn": ("ws1", "w1f", "w2f", "ws2", "w1s", "w2s"),
}

ALL_PHASES = ("l0", "ffn0", "l1ab", "l1c", "l1ffn")


def build_group(group, prep, final=False):
    _register()
    nc = bass.Bass("TRN2", target_bir_lowering=False, debug=False)
    p = Prog(nc)
    gstack = ExitStack()
    c = Ctx(nc, p, gstack)
    c.sm = prep["smmap"]
    in_names, out_names = [], []

    def need_x(g):
        return "l0" in g

    with gstack:
        if need_x(group):
            c.hbm["x"] = nc.dram_tensor("h_x", [BL, DC, 128, S], F32, kind="ExternalInput").ap()
            in_names.append("x")
        for w in ("smalls", "smrow"):
            arr = prep[w]
            c.hbm[w] = nc.dram_tensor("h_" + w, list(arr.shape), F32, kind="ExternalInput").ap()
            in_names.append(w)
        for ph in group:
            for w in PHASE_USED_W[ph]:
                arr = prep[w]
                c.hbm[w] = nc.dram_tensor("h_" + w, list(arr.shape), WEIGHT_TENS[w],
                                          kind="ExternalInput").ap()
                in_names.append(w)
        for nme, (shp, dt, prod, cons) in SCRATCH.items():
            inside_p = prod in group
            used = [x for x in cons if x in group]
            later = [x for x in cons if x not in group and
                     ALL_PHASES.index(x) > max(ALL_PHASES.index(g) for g in group)]
            if inside_p and not later:
                kind = "Internal"
            elif inside_p and later:
                kind = "ExternalOutput"
            elif used:
                kind = "ExternalInput"
            else:
                continue
            c.hbm[nme] = nc.dram_tensor("h_" + nme, shp, dt, kind=kind).ap()
            if kind == "ExternalInput":
                in_names.append(nme)
            elif kind == "ExternalOutput":
                out_names.append(nme)
        if DBG_L0 and "l0" in group:
            for nme in ("dbg_q", "dbg_k", "dbg_v", "dbg_o"):
                c.hbm[nme] = nc.dram_tensor("h_" + nme, [128, 16384], BF16,
                                            kind="ExternalOutput").ap()
                out_names.append(nme)
        if final or group[-1] == "l1ffn":
            c.hbm["y"] = nc.dram_tensor("h_y", [BL, DC, 128, S], F32,
                                        kind="ExternalOutput").ap()
            out_names.append("y")
        alloc_psum_pool(c)
        emit_consts(c)
        for ph in group:
            _PHASES[ph](c)
        p.finish()
        p.replay()
    return nc, in_names, out_names


def _core_x(inputs, ci):
    xs = inputs["L0_attn_x"][BL * ci:BL * (ci + 1)]
    xs = np.ascontiguousarray(np.transpose(xs, (0, 2, 1)))
    return xs.reshape(BL, DC, 128, S)


GROUPS = [("l0", "ffn0", "l1ab", "l1c", "l1ffn")]


def kernel(**inputs):
    prep = prep_inputs(inputs)
    state = [{} for _ in range(NCORES)]
    for ci in range(NCORES):
        state[ci]["x"] = _core_x(inputs, ci)
    out = None
    for gi, group in enumerate(GROUPS):
        nc, in_names, out_names = build_group(group, prep)
        in_maps = []
        for ci in range(NCORES):
            m = {}
            for nme in in_names:
                if nme == "x":
                    m["h_x"] = state[ci]["x"]
                elif nme in prep:
                    m["h_" + nme] = prep[nme]
                else:
                    m["h_" + nme] = state[ci][nme]
            in_maps.append(m)
        res = run_bass_kernel_spmd(nc, in_maps, core_ids=list(range(NCORES)))
        for ci in range(NCORES):
            for nme in out_names:
                state[ci][nme] = np.asarray(res.results[ci]["h_" + nme])
    ys = []
    for ci in range(NCORES):
        y = state[ci]["y"].reshape(BL, D, S)
        ys.append(np.transpose(y, (0, 2, 1)))
    return np.ascontiguousarray(np.concatenate(ys, 0)).astype(np.float32)



def emit_l1ab(c):
    p = c.p
    t = c.trk
    xn2_s = c.hbm["xn2_s"]
    xskip_s = c.hbm["xskip_s"]
    h_s = c.hbm["h_s"]
    c.push()

    pbig = c.sb["pbig"]
    pv0, pv1, pv2 = c.sb["pv0"], c.sb["pv1"], c.sb["pv2"]

    x_in = c.sbuf("x_in", [128, 32 * 2048], BF16)
    for m in range(32):
        t["xin%d" % m] = Trk()
    carry = c.sbuf("carry", [128, 64], F32)
    nbz = c.sbuf("nbz", [128, 32], F32)
    tnb = p.op("vector", "tensor_scalar_mul", nbz, c.smallc("bz", 0, 32), -1.0, tag=True)
    t["nbz"].wr(tnb)

    for ch in range(2):
        tks = slice(1024 * ch, 1024 * (ch + 1))
        c.push()
        xn2ch = c.sbuf("xn2ch", [128, DC * 2048], BF16)
        w1t = c.views("w1t", 2, [128, 1024], BF16)
        w2t = c.views("w2t", 2, [128, 1024], BF16)
        xsk = c.views("xsk", 2, [128, 2048], BF16)
        ltoks = []
        for dc in range(DC):
            for sq in range(BL):
                tk = p.dma("sync", xn2ch[:, 2048 * dc + 1024 * sq:2048 * dc + 1024 * (sq + 1)],
                           xn2_s[sq, dc, :, tks],
                           waits=(t["xn2ch"].w_waits() if dc == 0 and sq == 0 else ()))
            ltoks.append(tk)
        t["xn2ch"].wr(tk)
        lt = ltoks[-1]
        for mp in range(32):
            par = mp % 2
            tw1 = p.dma("gpsimd", w1t[par], c.hbm["w1"][mp], waits=t["w1t%d" % par].w_waits(), ch=par)
            t["w1t%d" % par].wr(tw1)
            tw2 = p.dma("gpsimd", w2t[par], c.hbm["w2"][mp], waits=t["w2t%d" % par].w_waits(), ch=par)
            t["w2t%d" % par].wr(tw2)
            gmm = [None] * 4
            umm = [None] * 4
            spar = mp % 2
            for kc in range(DC):
                for tt in range(4):
                    pg = pbig[:, 512 * tt:512 * (tt + 1)]
                    trg = t["pb%d" % tt]
                    w = [lt, tw1] + (trg.w_waits() if kc == 0 else [])
                    gmm[tt] = p.op("tensor", "matmul", pg,
                                   w1t[par][:, 128 * kc:128 * (kc + 1)],
                                   xn2ch[:, 2048 * kc + 512 * tt:2048 * kc + 512 * (tt + 1)],
                                   waits=w, start=(kc == 0), stop=(kc == DC - 1),
                                   tag=(kc == DC - 1))
                    t["xn2ch"].rd(gmm[tt])
                    t["w1t%d" % par].rd(gmm[tt])
            for tt in range(4):
                pg = pbig[:, 512 * tt:512 * (tt + 1)]
                trg = t["pb%d" % tt]
                trg.wr(gmm[tt])
                ev = p.op("scalar", "activation",
                          x_in[:, 2048 * mp + 512 * tt:2048 * mp + 512 * (tt + 1)], pg,
                          AF.Silu, bias=c.smallc("b1", mp),
                          waits=[gmm[tt]] + (t["xin%d" % mp].w_waits() if tt == 0 else []),
                          tag=True)
                trg.rd(ev)
                if tt == 3:
                    t["xin%d" % mp].wr(ev)
            for tt in range(4):
                pu = pv0 if tt % 2 == 0 else pv1
                tru = t["pv0"] if tt % 2 == 0 else t["pv1"]
                mm2 = None
                for kc in range(DC):
                    w = [lt, tw2] + (tru.w_waits() if kc == 0 else [])
                    mm2 = p.op("tensor", "matmul", pu[:, :],
                               w2t[par][:, 128 * kc:128 * (kc + 1)],
                               xn2ch[:, 2048 * kc + 512 * tt:2048 * kc + 512 * (tt + 1)],
                               waits=w, start=(kc == 0), stop=(kc == DC - 1),
                               tag=(kc == DC - 1))
                    t["xn2ch"].rd(mm2)
                    t["w2t%d" % par].rd(mm2)
                tru.wr(mm2)
                ev2 = p.op("scalar", "activation", xsk[spar][:, 512 * tt:512 * (tt + 1)],
                           pu[:, :], AF.Silu, bias=c.smallc("b2", mp),
                           waits=[mm2] + (t["xsk%d" % spar].w_waits() if tt == 0 else []),
                           tag=True)
                tru.rd(ev2)
            t["xsk%d" % spar].wr(ev2)
            for sq in range(BL):
                dtk = p.dma("sync", xskip_s[mp, :, sq, tks],
                            xsk[spar][:, 1024 * sq:1024 * (sq + 1)], waits=[ev2])
                t["xsk%d" % spar].rd(dtk)
        c.pop()
        c.push()
        wzt = c.views("wzt", 2, [128, 4096], BF16)
        wht = c.views("wht", 2, [128, 4096], BF16)
        a_sb = c.sbuf("a_sb", [128, 2048], F32)
        z_sb = c.sbuf("z_sb", [128, 2048], F32)
        b_sb = c.sbuf("b_sb", [128, 2048], F32)
        h_bf = c.sbuf("h_bf", [128, 2048], BF16)
        for m in range(32):
            par = m % 2
            twz = p.dma("gpsimd", wzt[par], c.hbm["wz"][m], waits=t["wzt%d" % par].w_waits(), ch=par)
            t["wzt%d" % par].wr(twz)
            twh = p.dma("gpsimd", wht[par], c.hbm["wh"][m], waits=t["wht%d" % par].w_waits(), ch=par)
            t["wht%d" % par].wr(twh)
            atoks = []
            zmm = [None] * 4
            for kc in range(32):
                for tt in range(4):
                    pz = pbig[:, 512 * tt:512 * (tt + 1)]
                    trz = t["pb%d" % tt]
                    w = [t["xin%d" % kc].w, twz] + (trz.w_waits() if kc == 0 else [])
                    zmm[tt] = p.op("tensor", "matmul", pz,
                                   wzt[par][:, 128 * kc:128 * (kc + 1)],
                                   x_in[:, 2048 * kc + 512 * tt:2048 * kc + 512 * (tt + 1)],
                                   waits=w, start=(kc == 0), stop=(kc == 31),
                                   tag=(kc == 31))
                    t["xin%d" % kc].rd(zmm[tt])
                    t["wzt%d" % par].rd(zmm[tt])
            for tt in range(4):
                pz = pbig[:, 512 * tt:512 * (tt + 1)]
                trz = t["pb%d" % tt]
                mm = zmm[tt]
                trz.wr(mm)
                ea = p.op("scalar", "activation", a_sb[:, 512 * tt:512 * (tt + 1)], pz,
                          AF.Sigmoid, scale=c.cp("neg1"), bias=nbz[:, m:m + 1],
                          waits=[mm, tnb] + (t["a_sb"].w_waits() if tt == 0 else []),
                          tag=True)
                trz.rd(ea)
                ez = p.op("scalar", "activation", z_sb[:, 512 * tt:512 * (tt + 1)], pz,
                          AF.Sigmoid, bias=c.smallc("bz", m),
                          waits=[mm] + (t["z_sb"].w_waits() if tt == 0 else []), tag=True)
                trz.rd(ez)
                atoks.append((ea, ez))
            t["a_sb"].wr(atoks[-1][0])
            t["z_sb"].wr(atoks[-1][1])
            btoks = []
            hmm4 = [None] * 4
            for kc in range(32):
                for tt in range(4):
                    ph = pbig[:, 512 * tt:512 * (tt + 1)]
                    trh = t["pb%d" % tt]
                    w = [t["xin%d" % kc].w, twh] + (trh.w_waits() if kc == 0 else [])
                    hmm4[tt] = p.op("tensor", "matmul", ph,
                                    wht[par][:, 128 * kc:128 * (kc + 1)],
                                    x_in[:, 2048 * kc + 512 * tt:2048 * kc + 512 * (tt + 1)],
                                    waits=w, start=(kc == 0), stop=(kc == 31),
                                    tag=(kc == 31))
                    t["wht%d" % par].rd(hmm4[tt])
                    t["xin%d" % kc].rd(hmm4[tt])
            for tt in range(4):
                ph = pbig[:, 512 * tt:512 * (tt + 1)]
                trh = t["pb%d" % tt]
                mm = hmm4[tt]
                trh.wr(mm)
                eb = p.op("vector", "scalar_tensor_tensor",
                          b_sb[:, 512 * tt:512 * (tt + 1)], ph[:, :],
                          c.smallc("bh", m), z_sb[:, 512 * tt:512 * (tt + 1)],
                          ALU.add, ALU.mult,
                          waits=[mm, atoks[tt][1]] + (t["b_sb"].w_waits() if tt == 0 else []),
                          tag=True)
                trh.rd(eb)
                t["z_sb"].rd(eb)
                btoks.append(eb)
            t["b_sb"].wr(btoks[-1])
            stoks = []
            for sq in range(BL):
                cin = carry[:, 2 * m + sq:2 * m + sq + 1]
                init = 0.0 if ch == 0 else cin
                w = [atoks[-1][0], btoks[-1]] + ([t["carry"].w] if ch > 0 else [])
                w += t["h_bf"].w_waits() if sq == 0 else []
                sk = p.op("vector", "tensor_tensor_scan",
                          h_bf[:, 1024 * sq:1024 * (sq + 1)],
                          a_sb[:, 1024 * sq:1024 * (sq + 1)],
                          b_sb[:, 1024 * sq:1024 * (sq + 1)],
                          init, ALU.mult, ALU.add, waits=w, tag=True)
                t["a_sb"].rd(sk)
                t["b_sb"].rd(sk)
                if ch > 0:
                    t["carry"].rd(sk)
                stoks.append(sk)
            t["h_bf"].wr(stoks[-1])
            if ch == 0:
                for sq in range(BL):
                    ck = p.op("vector", "tensor_copy",
                              carry[:, 2 * m + sq:2 * m + sq + 1],
                              h_bf[:, 1024 * (sq + 1) - 1:1024 * (sq + 1)],
                              waits=[stoks[sq]], tag=True)
                    t["h_bf"].rd(ck)
                    t["carry"].wr(ck)
            for sq in range(BL):
                dtk = p.dma("sync", h_s[m, :, sq, tks],
                            h_bf[:, 1024 * sq:1024 * (sq + 1)], waits=[stoks[-1]])
                t["h_bf"].rd(dtk)
        c.pop()
    c.pop()



def emit_l1c(c):
    p = c.p
    t = c.trk
    xskip_s = c.hbm["xskip_s"]
    h_s = c.hbm["h_s"]
    xnew_s = c.hbm["xnew_s"]
    acc_s = c.hbm["acc_s"]
    c.push()
    pv0, pv1, pv2 = c.sb["pv0"], c.sb["pv1"], c.sb["pv2"]

    wo1t = c.sbuf("wo1t", [128, DC * 4096], BF16)
    for mc in range(DC):
        tok = p.dma("gpsimd", wo1t[:, 4096 * mc:4096 * (mc + 1)], c.hbm["wo1"][mc])
    t["wo1t"].wr(tok)
    wo_tok = tok
    ht = c.sbuf("ht", [128, 32 * 512], BF16)
    xst = c.sbuf("xst", [128, 32 * 512], BF16)
    xnt = c.views("xnt", 2, [128, 512], F32)
    tmpe = c.views("tmpe", 2, [128, 512], F32)

    for g8 in range(8):
        sq, tk0 = g8 // 4, 512 * (g8 % 4)
        tcol = slice(tk0, tk0 + 512)
        htoks = []
        for m in range(32):
            tk1 = p.dma("sync", ht[:, 512 * m:512 * (m + 1)], h_s[m, :, sq, tcol],
                        waits=(t["ht"].w_waits() if m == 0 else ()))
            tk2 = p.dma("sync", xst[:, 512 * m:512 * (m + 1)], xskip_s[m, :, sq, tcol],
                        waits=(t["xst"].w_waits() if m == 0 else ()))
            htoks.append((tk1, tk2))
        t["xst"].wr(tk2)
        stoks = []
        for m in range(32):
            sk = p.op("vector", "tensor_tensor", ht[:, 512 * m:512 * (m + 1)],
                      ht[:, 512 * m:512 * (m + 1)], xst[:, 512 * m:512 * (m + 1)],
                      ALU.add, waits=[htoks[m][0], htoks[m][1]] +
                      (t["ht"].w_waits() if m == 0 else []), tag=True)
            t["xst"].rd(sk)
            stoks.append(sk)
        t["ht"].wr(stoks[-1])
        for mc in range(DC):
            po, tro = ((pv0, t["pv0"]), (pv1, t["pv1"]), (pv2, t["pv2"]))[mc % 3]
            mm = None
            for kc in range(32):
                w = [stoks[kc], wo_tok] + (tro.w_waits() if kc == 0 else [])
                mm = p.op("tensor", "matmul", po[:, :],
                          wo1t[:, 4096 * mc + 128 * kc:4096 * mc + 128 * (kc + 1)],
                          ht[:, 512 * kc:512 * (kc + 1)],
                          waits=w, start=(kc == 0), stop=(kc == 31), tag=(kc == 31))
                t["ht"].rd(mm)
                t["wo1t"].rd(mm)
            tro.wr(mm)
            par = mc % 2
            te = p.op("scalar", "activation", tmpe[par], po[:, :], AF.Identity,
                      bias=c.smallc("bo1", mc),
                      waits=[mm] + t["tmpe%d" % par].w_waits(), tag=True)
            tro.rd(te)
            t["tmpe%d" % par].wr(te)
            xl = p.dma("sync", xnt[par], xnew_s[sq, mc, :, tcol],
                       waits=t["xnt%d" % par].w_waits(), ch=par)
            t["xnt%d" % par].wr(xl)
            ta = p.op("vector", "tensor_tensor", tmpe[par], tmpe[par], xnt[par],
                      ALU.add, waits=[te, xl], tag=True)
            t["xnt%d" % par].rd(ta)
            t["tmpe%d" % par].wr(ta)
            dtk = p.dma("sync", acc_s[sq, mc, :, tcol], tmpe[par], waits=[ta])
            t["tmpe%d" % par].rd(dtk)
    c.pop()



def emit_l1ffn(c):
    p = c.p
    t = c.trk
    xn2_s = c.hbm["xn2_s"]
    acc_s = c.hbm["acc_s"]
    y = c.hbm["y"]
    c.push()
    pbig = c.sb["pbig"]
    pv0, pv1, pv2 = c.sb["pv0"], c.sb["pv1"], c.sb["pv2"]
    pt = c.sb["pt"]
    ident = c.sb["ident_b"]
    ones = c.sb["ones_f"]

    def pb(i):
        return pbig[:, 512 * i:512 * (i + 1)]

    cmb = c.sbuf("cmb", [64, 1024], BF16)
    seqf = c.sbuf("seqf", [128, DC * 1024], BF16)
    egbc = c.sbuf("egbc", [128, E1], F32)
    ggc = c.sbuf("ggc", [128, DC], F32)
    b2g = c.sbuf("b2g", [128, 64], F32)
    rowp = c.sbuf("rowp", [1, 1024], F32)
    c.sbuf("rowtmp", [1, 1024], F32)
    tmpq = c.views("tmpq", 2, [128, 1024], F32)

    bm = p.op("tensor", "matmul", pv0[:, 0:E1], ones[0:1, :],
              c.sb["smrow"][0:1, D:D + E1], waits=t["pv0"].w_waits(),
              start=True, stop=True, tag=True)
    t["pv0"].wr(bm)
    te = p.op("scalar", "copy", out=egbc, in_=pv0[:, 0:E1], waits=[bm], tag=True)
    t["pv0"].rd(te); t["egbc"].wr(te)
    tg = p.op("vector", "tensor_mul", ggc, c.smallc("gs1", 0, DC), c.smallc("g1", 0, DC),
              waits=[], tag=True)
    t["ggc"].wr(tg)
    tb2 = None
    for e in range(E1):
        tb2 = p.op("vector", "tensor_scalar", b2g[:, 8 * e:8 * (e + 1)],
                   c.smallc("b2f", 8 * e, 8), egbc[:, e:e + 1], None, ALU.mult,
                   waits=[te] + (t["b2g"].w_waits() if e == 0 else []), tag=True)
        t["egbc"].rd(tb2)
    t["b2g"].wr(tb2)

    def rms_row(out_row, ps_row, waits, n):
        tmp = c.sb["rowtmp"][0:1, 0:n]
        tk = p.op("scalar", "activation", tmp, ps_row, AF.Sqrt,
                  waits=list(waits) + c.trk["rowtmp"].w_waits(), tag=True)
        c.trk["rowtmp"].wr(tk)
        tk = p.op("vector", "tensor_scalar_max", tmp, tmp, 1e-8, waits=[tk], tag=True)
        c.trk["rowtmp"].wr(tk)
        tk = p.op("vector", "reciprocal", out_row, tmp, waits=[tk], tag=True)
        c.trk["rowtmp"].rd(tk)
        tk = p.op("vector", "tensor_scalar_mul", out_row, out_row, 32.0, waits=[tk], tag=True)
        return tk

    c.push()
    xn2f = c.sbuf("xn2f", [128, DC * 4096], BF16)
    seT = c.sbuf("seT", [128, DC * 1024], BF16)
    gm = c.sbuf("gm", [128, DC * 64], BF16)
    slots = c.sbuf("slots", [128, DC * 1024], BF16)
    h1 = c.sbuf("h1", [128, 32 * 128], BF16)
    A1c = c.sbuf("A1c", [128, 32], F32)
    gates_bc = c.sbuf("gates_bc", [128, 1024], F32)
    grow = c.sbuf("grow", [1, 1024], F32)
    xgt = c.sbuf("xgt", [64, 1024], BF16)
    wst = c.views("wst", 2, [128, 4096], BF16)
    wst2 = c.views("wst2", 2, [128, 4096], BF16)

    ltk = None
    for dc in range(DC):
        for sq in range(BL):
            ltk = p.dma("sync", xn2f[:, 4096 * dc + 2048 * sq:4096 * dc + 2048 * (sq + 1)],
                        xn2_s[sq, dc, :, :],
                        waits=(t["xn2f"].w_waits() if dc == 0 and sq == 0 else ()))
    t["xn2f"].wr(ltk)

    for chk in range(8):
        cc = slice(512 * chk, 512 * (chk + 1))
        mm = None
        for dc in range(DC):
            par = dc % 2
            sk = p.op("scalar", "activation", tmpq[par][:, 0:512],
                      xn2f[:, 4096 * dc + cc.start:4096 * dc + cc.stop], AF.Square,
                      waits=[ltk] + t["tmpq%d" % par].w_waits(), tag=True)
            t["tmpq%d" % par].wr(sk); t["xn2f"].rd(sk)
            mm = p.op("tensor", "matmul", pv2[0:1, :], ones[:, 0:1], tmpq[par][:, 0:512],
                      waits=[sk] + (t["pv2"].w_waits() if dc == 0 else []),
                      start=(dc == 0), stop=(dc == DC - 1), tag=True)
            t["tmpq%d" % par].rd(mm)
        t["pv2"].wr(mm)
        a1r = rowp[0:1, 0:512]
        tk = rms_row(a1r, pv2[0:1, :], [mm], 512)
        t["pv2"].rd(tk); t["rowp"].wr(tk)
        cmms = []
        for j in range(4):
            w = [tk] + (t["pv1"].w_waits() if j == 0 else [])
            cm = p.op("tensor", "matmul", pv1[:, j:j + 1], a1r[0:1, 128 * j:128 * (j + 1)],
                      ones[0:1, 0:1], waits=w, start=True, stop=True, tag=True)
            t["rowp"].rd(cm)
            cmms.append(cm)
        t["pv1"].wr(cmms[-1])
        ce = p.op("vector", "tensor_copy", A1c[:, 4 * chk:4 * (chk + 1)], pv1[:, 0:4],
                  waits=cmms + (t["A1c"].w_waits() if chk == 0 else []), tag=True)
        t["pv1"].rd(ce)
    t["A1c"].wr(ce)

    gred = c.sbuf("gred", [128, 1536], F32)
    gtoks = []
    for dc in range(DC):
        for sq in range(BL):
            src = xn2f[:, 4096 * dc + 2048 * sq:4096 * dc + 2048 * (sq + 1)]
            tk = p.op("vector", "tensor_tensor", gred[:, 0:1024],
                      src.rearrange("p (a two) -> p a two", two=2)[:, :, 0],
                      src.rearrange("p (a two) -> p a two", two=2)[:, :, 1], ALU.add,
                      waits=[ltk] + t["gred"].w_waits(), tag=True)
            t["xn2f"].rd(tk)
            t["gred"].wr(tk)
            off, wdt = 0, 1024
            for lv in range(5):
                nw = wdt // 2
                dsto = 1024 if lv % 2 == 0 else 0
                src2 = gred[:, off:off + wdt].rearrange("p (a two) -> p a two", two=2)
                if lv < 4:
                    tk = p.op("vector", "tensor_tensor", gred[:, dsto:dsto + nw],
                              src2[:, :, 0], src2[:, :, 1], ALU.add, waits=[tk], tag=True)
                    t["gred"].wr(tk)
                else:
                    tk = p.op("vector", "scalar_tensor_tensor",
                              gm[:, 64 * dc + 32 * sq:64 * dc + 32 * (sq + 1)],
                              src2[:, :, 0], 1.0, src2[:, :, 1], ALU.bypass, ALU.add,
                              waits=[tk] + (t["gm"].w_waits() if dc == 0 and sq == 0 else []),
                              tag=True)
                    t["gred"].rd(tk)
                off, wdt = dsto, nw
            gtoks.append(tk)
    gsc = p.op("vector", "tensor_scalar_mul", gm, gm, 1.0 / 64.0,
               waits=gtoks + t["gm"].w_waits(), tag=True)
    t["gm"].wr(gsc)

    setoks = []
    for es in range(M1):
        for mcp in range(DC):
            par = ((es * 2) + mcp // 4) % 2
            if mcp % 4 == 0:
                tw = p.dma("gpsimd", wst[par], c.hbm["ws1"][es][mcp:mcp + 4],
                           waits=t["wst%d" % par].w_waits(), ch=par)
                t["wst%d" % par].wr(tw)
            tw = t["wst%d" % par].w
            pz = pb(mcp % 2)
            trz = t["pb%d" % (mcp % 2)]
            mm = None
            for kc in range(DC):
                w = [gsc, tw] + (trz.w_waits() if kc == 0 else [])
                mm = p.op("tensor", "matmul", pz[:, 0:64],
                          wst[par][:, 1024 * (mcp % 4) + 128 * kc:1024 * (mcp % 4) + 128 * (kc + 1)],
                          gm[:, 64 * kc:64 * (kc + 1)],
                          waits=w, start=(kc == 0), stop=(kc == DC - 1), tag=(kc == DC - 1))
                t["gm"].rd(mm)
                t["wst%d" % par].rd(mm)
            trz.wr(mm)
            dst = seT[:, 1024 * mcp:1024 * (mcp + 1)].rearrange(
                "q (b e) -> q b e", e=M1)[:, :, es]
            ev = p.op("vector", "tensor_copy", dst, pz[:, 0:64],
                      waits=[mm] + (t["seT"].w_waits() if es == 0 and mcp == 0 else []),
                      tag=True)
            trz.rd(ev)
            setoks.append(ev)
    t["seT"].wr(setoks[-1])

    for hf in range(2):
        mm = None
        for mcp in range(DC):
            par = mcp % 2
            sk = p.op("scalar", "activation", tmpq[par][:, 0:512],
                      seT[:, 1024 * mcp + 512 * hf:1024 * mcp + 512 * (hf + 1)], AF.Square,
                      waits=setoks[-1:] + t["tmpq%d" % par].w_waits(), tag=True)
            t["seT"].rd(sk)
            t["tmpq%d" % par].wr(sk)
            mm = p.op("tensor", "matmul", pv2[0:1, :], ones[:, 0:1], tmpq[par][:, 0:512],
                      waits=[sk] + (t["pv2"].w_waits() if mcp == 0 else []),
                      start=(mcp == 0), stop=(mcp == DC - 1), tag=True)
            t["tmpq%d" % par].rd(mm)
        t["pv2"].wr(mm)
        ser = rowp[0:1, 0:512]
        tk = rms_row(ser, pv2[0:1, :], [mm], 512)
        t["pv2"].rd(tk); t["rowp"].wr(tk)
        bc = p.op("tensor", "matmul", pv1[:, :], ones[0:1, :], ser,
                  waits=[tk] + t["pv1"].w_waits(), start=True, stop=True, tag=True)
        t["rowp"].rd(bc); t["pv1"].wr(bc)
        cbc = p.op("scalar", "copy", out=tmpq[0][:, 0:512], in_=pv1[:, :],
                   waits=[bc] + t["tmpq0"].w_waits(), tag=True)
        t["pv1"].rd(cbc); t["tmpq0"].wr(cbc)
        for mcp in range(DC):
            sl = seT[:, 1024 * mcp + 512 * hf:1024 * mcp + 512 * (hf + 1)]
            t1 = p.op("vector", "tensor_tensor", tmpq[1][:, 0:512], sl, tmpq[0][:, 0:512], ALU.mult,
                      waits=[cbc] + t["tmpq1"].w_waits() + t["seT"].r_waits(), tag=True)
            t["tmpq1"].wr(t1); t["tmpq0"].rd(t1)
            t2 = p.op("vector", "tensor_scalar", sl, tmpq[1][:, 0:512], ggc[:, mcp:mcp + 1], None,
                      ALU.mult, waits=[t1, tg] + t["seT"].w_waits(), tag=True)
            t["tmpq1"].rd(t2); t["ggc"].rd(t2)
            t["seT"].wr(t2)
    se_tok = t["seT"].w

    lrow = c.sbuf("lmax", [64, 1], F32)
    lsum = c.sbuf("lsum", [64, 1], F32)
    lrcp = c.sbuf("lrcp", [64, 1], F32)
    lgs = c.sbuf("lgs", [64, 16], F32)
    ctoks = []
    for bg in range(64):
        sq, g = bg // 32, bg % 32
        gc0 = 4096 * 0 + 2048 * sq + 64 * g
        mm = None
        for kc in range(DC):
            w = [ltk, se_tok] + (t["pv0"].w_waits() if kc == 0 else [])
            mm = p.op("tensor", "matmul", pv0[0:64, 0:16],
                      xn2f[:, 4096 * kc + 2048 * sq + 64 * g:4096 * kc + 2048 * sq + 64 * (g + 1)],
                      seT[:, 1024 * kc + 16 * bg:1024 * kc + 16 * (bg + 1)],
                      waits=w, start=(kc == 0), stop=(kc == DC - 1), tag=(kc == DC - 1))
            t["xn2f"].rd(mm)
            t["seT"].rd(mm)
        t["pv0"].wr(mm)
        a1s = A1c[64 * (g % 2):64 * (g % 2) + 64,
                  (2048 * sq + 64 * g) // 128:(2048 * sq + 64 * g) // 128 + 1]
        tk = p.op("vector", "tensor_scalar", lgs, pv0[0:64, 0:16], a1s, None, ALU.mult,
                  waits=[mm, ce] + t["lgs"].w_waits(), tag=True)
        t["pv0"].rd(tk); t["A1c"].rd(tk); t["lgs"].wr(tk)
        tmx = p.op("vector", "tensor_reduce", lrow, lgs, AX.X, ALU.max, negate=True,
                   waits=[tk] + t["lmax"].w_waits(), tag=True)
        t["lmax"].wr(tmx)
        tex = p.op("scalar", "activation", lgs, lgs, AF.Exp, bias=lrow,
                   accum_out=lsum, waits=[tmx] + t["lsum"].w_waits(), tag=True)
        t["lmax"].rd(tex); t["lsum"].wr(tex); t["lgs"].wr(tex)
        trc = p.op("vector", "reciprocal", lrcp, lsum, waits=[tex] + t["lrcp"].w_waits(),
                   tag=True)
        t["lsum"].rd(trc); t["lrcp"].wr(trc)
        tcb = p.op("vector", "tensor_scalar", cmb[:, 16 * bg:16 * (bg + 1)], lgs, lrcp,
                   None, ALU.mult, waits=[trc, tex] +
                   (t["cmb"].w_waits() if bg == 0 else []), tag=True)
        t["lgs"].rd(tcb); t["lrcp"].rd(tcb)
        ctoks.append(tcb)
    t["cmb"].wr(ctoks[-1])

    sltoks = []
    for bg in range(64):
        sq, g = bg // 32, bg % 32
        ttks = []
        for dc in range(DC):
            w = [ltk] + (t["pt"].w_waits() if dc == 0 else [])
            tk = p.op("tensor", "transpose", pt[0:64, 128 * dc:128 * (dc + 1)],
                      xn2f[:, 4096 * dc + 2048 * sq + 64 * g:4096 * dc + 2048 * sq + 64 * (g + 1)],
                      ident, waits=w, tag=(dc == DC - 1))
            t["xn2f"].rd(tk if tk else ltk)
        t["pt"].wr(tk)
        cg = p.op("vector", "tensor_copy", xgt, pt[0:64, :],
                  waits=[tk] + t["xgt"].w_waits(), tag=True)
        t["pt"].rd(cg); t["xgt"].wr(cg)
        for dc in range(DC):
            w = [cg, ctoks[bg]] + t["pv1"].w_waits()
            mm = p.op("tensor", "matmul", pv1[0:128, 0:16],
                      xgt[:, 128 * dc:128 * (dc + 1)], cmb[:, 16 * bg:16 * (bg + 1)],
                      waits=w, start=True, stop=True, tag=True)
            t["xgt"].rd(mm); t["cmb"].rd(mm); t["pv1"].wr(mm)
            ev = p.op("vector", "tensor_copy",
                      slots[:, 1024 * dc + 16 * bg:1024 * dc + 16 * (bg + 1)], pv1[:, 0:16],
                      waits=[mm] + (t["slots"].w_waits() if bg == 0 and dc == 0 else []),
                      tag=True)
            t["pv1"].rd(ev)
            sltoks.append(ev)
    t["slots"].wr(sltoks[-1])

    wgb = c.sbuf("wgb2", [128, DC], BF16)
    twg = p.op("vector", "tensor_copy", wgb, c.smallc("wg", 0, DC), tag=True)
    t["wgb2"].wr(twg)
    mm = None
    for kc in range(DC):
        w = [gsc, twg] + (t["pv0"].w_waits() if kc == 0 else [])
        mm = p.op("tensor", "matmul", pv0[0:1, 0:64], wgb[:, kc:kc + 1],
                  gm[:, 64 * kc:64 * (kc + 1)], waits=w,
                  start=(kc == 0), stop=(kc == DC - 1), tag=(kc == DC - 1))
        t["gm"].rd(mm); t["wgb2"].rd(mm)
    t["pv0"].wr(mm)
    gr = grow[0:1, 0:64]
    tk = p.op("scalar", "activation", gr, pv0[0:1, 0:64], AF.Sigmoid,
              bias=c.sb["smrow"][0:1, D + E1:D + E1 + 1],
              waits=[mm] + t["grow"].w_waits(), tag=True)
    t["pv0"].rd(tk)
    tk = p.op("vector", "tensor_scalar_max", gr, gr, 0.01, waits=[tk], tag=True)
    g16 = grow[0:1, 0:1024].rearrange("p (b e) -> p b e", e=16)
    ek0 = p.op("vector", "tensor_copy", tmpq[0][0:1, 0:64], gr, waits=[tk], tag=True)
    t["tmpq0"].wr(ek0)
    etoks2 = []
    for j in range(M1):
        ek = p.op("vector", "tensor_copy", g16[:, :, j], tmpq[0][0:1, 0:64],
                  waits=[ek0] + ([] if j else []), tag=True)
        etoks2.append(ek)
    t["grow"].wr(etoks2[-1])
    t["tmpq0"].rd(etoks2[-1])
    bc1 = p.op("tensor", "matmul", pv0[:, :], ones[0:1, :], grow[0:1, 0:512],
               waits=[etoks2[-1]] + t["pv0"].w_waits(), start=True, stop=True, tag=True)
    t["grow"].rd(bc1); t["pv0"].wr(bc1)
    bc2 = p.op("tensor", "matmul", pv1[:, :], ones[0:1, :], grow[0:1, 512:1024],
               waits=[etoks2[-1]] + t["pv1"].w_waits(), start=True, stop=True, tag=True)
    t["grow"].rd(bc2); t["pv1"].wr(bc2)
    c1 = p.op("scalar", "copy", out=gates_bc[:, 0:512], in_=pv0[:, :], waits=[bc1], tag=True)
    t["pv0"].rd(c1)
    c2 = p.op("scalar", "copy", out=gates_bc[:, 512:1024], in_=pv1[:, :], waits=[bc2], tag=True)
    t["pv1"].rd(c2)
    t["gates_bc"].wr(c2)
    gb_tok = c2

    sq1toks = []
    for e in range(E1):
        h1toks = []
        for mc in range(32):
            par = (mc // 4) % 2
            if mc % 4 == 0:
                tw = p.dma("gpsimd", wst[par], c.hbm["w1f"][e][mc:mc + 4],
                           waits=t["wst%d" % par].w_waits(), ch=par)
                t["wst%d" % par].wr(tw)
            tw = t["wst%d" % par].w
            pz = pb(mc % 2)
            trz = t["pb%d" % (mc % 2)]
            mm = None
            for kc in range(DC):
                w = [sltoks[-1], tw] + (trz.w_waits() if kc == 0 else [])
                rhs = slots[:, 1024 * kc:1024 * (kc + 1)].rearrange(
                    "q (b e2 s) -> q b e2 s", e2=E1, s=S1)[:, :, e, :]
                mm = p.op("tensor", "matmul", pz[:, 0:128],
                          wst[par][:, 1024 * (mc % 4) + 128 * kc:1024 * (mc % 4) + 128 * (kc + 1)],
                          rhs,
                          waits=w, start=(kc == 0), stop=(kc == DC - 1), tag=(kc == DC - 1))
                t["slots"].rd(mm)
                t["wst%d" % par].rd(mm)
            trz.wr(mm)
            ev = p.op("scalar", "activation", h1[:, 128 * mc:128 * (mc + 1)], pz[:, 0:128],
                      AF.Gelu, bias=c.smallc("b1f", 32 * e + mc),
                      waits=[mm] + (t["h1"].w_waits() if mc == 0 else []), tag=True)
            trz.rd(ev)
            h1toks.append(ev)
        t["h1"].wr(h1toks[-1])
        for dc in range(DC):
            par = dc % 2
            tw = p.dma("gpsimd", wst2[par], c.hbm["w2f"][e][dc],
                       waits=t["wst2%d" % par].w_waits(), ch=par)
            t["wst2%d" % par].wr(tw)
            po = (pv0, pv1)[dc % 2]
            tro = t["pv%d" % (dc % 2)]
            mm = None
            for kc in range(32):
                w = [h1toks[kc], tw] + (tro.w_waits() if kc == 0 else [])
                mm = p.op("tensor", "matmul", po[:, 0:128],
                          wst2[par][:, 128 * kc:128 * (kc + 1)],
                          h1[:, 128 * kc:128 * (kc + 1)],
                          waits=w, start=(kc == 0), stop=(kc == 31), tag=(kc == 31))
                t["h1"].rd(mm)
                t["wst2%d" % par].rd(mm)
            tro.wr(mm)
            te1 = p.op("scalar", "activation", tmpq[par][:, 0:128], po[:, 0:128],
                       AF.Identity, scale=egbc[:, e:e + 1], bias=b2g[:, 8 * e + dc:8 * e + dc + 1],
                       waits=[mm, tb2] + t["tmpq%d" % par].w_waits(), tag=True)
            tro.rd(te1); t["egbc"].rd(te1); t["b2g"].rd(te1)
            t["tmpq%d" % par].wr(te1)
            dst = seqf[:, 1024 * dc:1024 * (dc + 1)].rearrange(
                "q (b e2 s) -> q b e2 s", e2=E1, s=S1)[:, :, e, :]
            gsl = gates_bc.rearrange("q (b e2 s) -> q b e2 s", e2=E1, s=S1)[:, :, e, :]
            ev = p.op("vector", "tensor_tensor", dst, tmpq[par][:, 0:128], gsl, ALU.mult,
                      waits=[te1, gb_tok] + (t["seqf"].w_waits() if e == 0 and dc == 0 else []),
                      tag=True)
            t["tmpq%d" % par].rd(ev)
            t["gates_bc"].rd(ev)
            sq1toks.append(ev)
    t["seqf"].wr(sq1toks[-1])
    c.pop()
    seq_tok = t["seqf"].w
    cmb_tok = t["cmb"].w

    c.push()
    comb2 = c.sbuf("comb2", [128, 8 * 512], BF16)
    sl2 = c.sbuf("sl2", [128, DC * 1024], BF16)
    out2 = c.sbuf("out2", [128, DC * 1024], BF16)
    o2f = c.sbuf("o2f", [128, DC * 1024], BF16)
    wstB = c.views("wstB", 2, [128, 4096], BF16)
    wstB2 = c.views("wstB2", 2, [128, 4096], BF16)

    c.push()
    xs = c.sbuf("xs", [128, DC * 1024], BF16)
    cme = c.sbuf("cme", [128, DC * 256], BF16)
    se2T = c.sbuf("se2T", [128, DC * 1024], BF16)
    L2 = c.sbuf("L2", [128, 8 * 512], BF16)
    L2T = c.sbuf("L2T", [128, 8 * 512], BF16)
    disp = c.sbuf("disp", [128, 8 * 512], BF16)
    xstm = c.sbuf("xstm", [128, 8 * 1024], BF16)

    a2toks = []
    for hf in range(2):
        mm = None
        for dc in range(DC):
            par = dc % 2
            sk = p.op("scalar", "activation", tmpq[par][:, 0:512],
                      seqf[:, 1024 * dc + 512 * hf:1024 * dc + 512 * (hf + 1)], AF.Square,
                      waits=[seq_tok] + t["tmpq%d" % par].w_waits(), tag=True)
            t["seqf"].rd(sk)
            t["tmpq%d" % par].wr(sk)
            mm = p.op("tensor", "matmul", pv2[0:1, :], ones[:, 0:1], tmpq[par][:, 0:512],
                      waits=[sk] + (t["pv2"].w_waits() if dc == 0 else []),
                      start=(dc == 0), stop=(dc == DC - 1), tag=True)
            t["tmpq%d" % par].rd(mm)
        t["pv2"].wr(mm)
        a2r = rowp[0:1, 0:512]
        tk = rms_row(a2r, pv2[0:1, :], [mm], 512)
        t["pv2"].rd(tk); t["rowp"].wr(tk)
        bc = p.op("tensor", "matmul", pv1[:, :], ones[0:1, :], a2r,
                  waits=[tk] + t["pv1"].w_waits(), start=True, stop=True, tag=True)
        t["rowp"].rd(bc); t["pv1"].wr(bc)
        cbc = p.op("scalar", "copy", out=tmpq[0][:, 0:512], in_=pv1[:, :],
                   waits=[bc] + t["tmpq0"].w_waits(), tag=True)
        t["pv1"].rd(cbc); t["tmpq0"].wr(cbc)
        for dc in range(DC):
            sl = slice(1024 * dc + 512 * hf, 1024 * dc + 512 * (hf + 1))
            t1 = p.op("vector", "tensor_tensor", tmpq[1][:, 0:512], seqf[:, sl],
                      tmpq[0][:, 0:512], ALU.mult,
                      waits=[cbc, seq_tok] + t["tmpq1"].w_waits(), tag=True)
            t["seqf"].rd(t1)
            t["tmpq1"].wr(t1); t["tmpq0"].rd(t1)
            t2 = p.op("vector", "tensor_scalar", xs[:, sl], tmpq[1][:, 0:512],
                      c.smallc("g2", dc), None, ALU.mult,
                      waits=[t1] + (t["xs"].w_waits() if hf == 0 and dc == 0 else []),
                      tag=True)
            t["tmpq1"].rd(t2)
            a2toks.append(t2)
    t["xs"].wr(a2toks[-1])
    xs_tok = a2toks[-1]

    cmtoks = []
    for dc in range(DC):
        par = dc % 2
        src = xs[:, 1024 * dc:1024 * (dc + 1)].rearrange("q (a two) -> q a two", two=2)
        tk = p.op("vector", "tensor_tensor", tmpq[par][:, 0:512], src[:, :, 0],
                  src[:, :, 1], ALU.add, waits=[xs_tok] + t["tmpq%d" % par].w_waits(),
                  tag=True)
        t["xs"].rd(tk)
        t["tmpq%d" % par].wr(tk)
        s2 = tmpq[par][:, 0:512].rearrange("q (a two) -> q a two", two=2)
        tk = p.op("vector", "scalar_tensor_tensor", cme[:, 256 * dc:256 * (dc + 1)],
                  s2[:, :, 0], 1.0, s2[:, :, 1], ALU.bypass, ALU.add,
                  waits=[tk] + (t["cme"].w_waits() if dc == 0 else []), tag=True)
        t["tmpq%d" % par].rd(tk)
        cmtoks.append(tk)
    csc = p.op("vector", "tensor_scalar_mul", cme, cme, 0.25, waits=cmtoks, tag=True)
    t["cme"].wr(csc)

    s2toks = []
    for e2 in range(E2):
        for mcp in range(DC):
            par = ((e2 * 2) + mcp // 4) % 2
            if mcp % 4 == 0:
                tw = p.dma("gpsimd", wstB[par], c.hbm["ws2"][e2][mcp:mcp + 4],
                           waits=t["wstB%d" % par].w_waits(), ch=par)
                t["wstB%d" % par].wr(tw)
            tw = t["wstB%d" % par].w
            pz = pb(mcp % 2)
            trz = t["pb%d" % (mcp % 2)]
            mm = None
            for kc in range(DC):
                w = [csc, tw] + (trz.w_waits() if kc == 0 else [])
                mm = p.op("tensor", "matmul", pz[:, 0:256],
                          wstB[par][:, 1024 * (mcp % 4) + 128 * kc:1024 * (mcp % 4) + 128 * (kc + 1)],
                          cme[:, 256 * kc:256 * (kc + 1)],
                          waits=w, start=(kc == 0), stop=(kc == DC - 1), tag=(kc == DC - 1))
                t["cme"].rd(mm)
                t["wstB%d" % par].rd(mm)
            trz.wr(mm)
            evs = []
            for b in range(BL):
                ev = p.op("vector", "tensor_copy",
                          se2T[:, 1024 * mcp + 512 * b + 128 * e2:
                               1024 * mcp + 512 * b + 128 * (e2 + 1)],
                          pz[:, 128 * b:128 * (b + 1)],
                          waits=[mm] + (t["se2T"].w_waits() if e2 == 0 and mcp == 0 and b == 0 else []),
                          tag=True)
                trz.rd(ev)
                evs.append(ev)
            s2toks += evs
    t["se2T"].wr(s2toks[-1])
    for hf in range(2):
        mm = None
        for mcp in range(DC):
            par = mcp % 2
            sk = p.op("scalar", "activation", tmpq[par][:, 0:512],
                      se2T[:, 1024 * mcp + 512 * hf:1024 * mcp + 512 * (hf + 1)], AF.Square,
                      waits=[s2toks[-1]] + t["tmpq%d" % par].w_waits(), tag=True)
            t["se2T"].rd(sk)
            t["tmpq%d" % par].wr(sk)
            mm = p.op("tensor", "matmul", pv2[0:1, :], ones[:, 0:1], tmpq[par][:, 0:512],
                      waits=[sk] + (t["pv2"].w_waits() if mcp == 0 else []),
                      start=(mcp == 0), stop=(mcp == DC - 1), tag=True)
            t["tmpq%d" % par].rd(mm)
        t["pv2"].wr(mm)
        ser = rowp[0:1, 0:512]
        tk = rms_row(ser, pv2[0:1, :], [mm], 512)
        t["pv2"].rd(tk); t["rowp"].wr(tk)
        bc = p.op("tensor", "matmul", pv1[:, :], ones[0:1, :], ser,
                  waits=[tk] + t["pv1"].w_waits(), start=True, stop=True, tag=True)
        t["rowp"].rd(bc); t["pv1"].wr(bc)
        cbc = p.op("scalar", "copy", out=tmpq[0][:, 0:512], in_=pv1[:, :],
                   waits=[bc] + t["tmpq0"].w_waits(), tag=True)
        t["pv1"].rd(cbc); t["tmpq0"].wr(cbc)
        for mcp in range(DC):
            sl = se2T[:, 1024 * mcp + 512 * hf:1024 * mcp + 512 * (hf + 1)]
            t1 = p.op("vector", "tensor_tensor", tmpq[1][:, 0:512], sl,
                      tmpq[0][:, 0:512], ALU.mult,
                      waits=[cbc] + t["tmpq1"].w_waits() + t["se2T"].r_waits(), tag=True)
            t["tmpq1"].wr(t1); t["tmpq0"].rd(t1)
            t2 = p.op("vector", "tensor_scalar", sl, tmpq[1][:, 0:512],
                      c.smallc("gs2", mcp), None, ALU.mult,
                      waits=[t1] + t["se2T"].w_waits(), tag=True)
            t["tmpq1"].rd(t2)
            t["se2T"].wr(t2)
    se2_tok = t["se2T"].w

    l2toks = []
    for b in range(BL):
        for pc in range(4):
            mm = None
            for kc in range(DC):
                w = [xs_tok, se2_tok] + (t["pv0"].w_waits() if kc == 0 else [])
                mm = p.op("tensor", "matmul", pv0[:, :],
                          xs[:, 1024 * kc + 512 * b + 128 * pc:1024 * kc + 512 * b + 128 * (pc + 1)],
                          se2T[:, 1024 * kc + 512 * b:1024 * kc + 512 * (b + 1)],
                          waits=w, start=(kc == 0), stop=(kc == DC - 1), tag=(kc == DC - 1))
                t["xs"].rd(mm)
                t["se2T"].rd(mm)
            t["pv0"].wr(mm)
            i = b * 4 + pc
            ev = p.op("vector", "tensor_copy", L2[:, 512 * i:512 * (i + 1)], pv0[:, :],
                      waits=[mm] + (t["L2"].w_waits() if i == 0 else []), tag=True)
            t["pv0"].rd(ev)
            l2toks.append(ev)
    t["L2"].wr(l2toks[-1])
    cm2toks = []
    smax = c.sbuf("smax", [128, 1], F32)
    ssum = c.sbuf("ssum", [128, 1], F32)
    srcp = c.sbuf("srcp", [128, 1], F32)
    for i in range(8):
        sl = slice(512 * i, 512 * (i + 1))
        tmx = p.op("vector", "tensor_reduce", smax, L2[:, sl], AX.X, ALU.max,
                   negate=True, waits=[l2toks[i]] + t["smax"].w_waits(), tag=True)
        t["L2"].rd(tmx); t["smax"].wr(tmx)
        tex = p.op("scalar", "activation", comb2[:, sl], L2[:, sl], AF.Exp, bias=smax,
                   accum_out=ssum, waits=[tmx] + t["ssum"].w_waits() +
                   (t["comb2"].w_waits() if i == 0 else []), tag=True)
        t["smax"].rd(tex); t["ssum"].wr(tex); t["L2"].rd(tex)
        trc = p.op("vector", "reciprocal", srcp, ssum, waits=[tex] + t["srcp"].w_waits(),
                   tag=True)
        t["ssum"].rd(trc); t["srcp"].wr(trc)
        tcb = p.op("vector", "tensor_scalar", comb2[:, sl], comb2[:, sl], srcp, None,
                   ALU.mult, waits=[trc, tex], tag=True)
        t["srcp"].rd(tcb)
        cm2toks.append(tcb)
    t["comb2"].wr(cm2toks[-1])
    lt_toks = []
    for b in range(BL):
        for ec in range(4):
            for pc in range(4):
                w = [l2toks[b * 4 + pc]] + (t["pt"].w_waits() if pc == 0 else [])
                tk = p.op("tensor", "transpose", pt[:, 128 * pc:128 * (pc + 1)],
                          L2[:, 512 * (b * 4 + pc) + 128 * ec:512 * (b * 4 + pc) + 128 * (ec + 1)],
                          ident, waits=w, tag=(pc == 3))
                t["L2"].rd(tk if tk else l2toks[-1])
            t["pt"].wr(tk)
            i = b * 4 + ec
            ev = p.op("vector", "tensor_copy", L2T[:, 512 * i:512 * (i + 1)], pt[:, 0:512],
                      waits=[tk] + (t["L2T"].w_waits() if i == 0 else []), tag=True)
            t["pt"].rd(ev)
            lt_toks.append(ev)
    t["L2T"].wr(lt_toks[-1])
    dtoks = []
    for i in range(8):
        sl = slice(512 * i, 512 * (i + 1))
        tmx = p.op("vector", "tensor_reduce", smax, L2T[:, sl], AX.X, ALU.max,
                   negate=True, waits=[lt_toks[i]] + t["smax"].w_waits(), tag=True)
        t["L2T"].rd(tmx); t["smax"].wr(tmx)
        tex = p.op("scalar", "activation", disp[:, sl], L2T[:, sl], AF.Exp, bias=smax,
                   accum_out=ssum, waits=[tmx] + t["ssum"].w_waits() +
                   (t["disp"].w_waits() if i == 0 else []), tag=True)
        t["smax"].rd(tex); t["ssum"].wr(tex); t["L2T"].rd(tex)
        trc = p.op("vector", "reciprocal", srcp, ssum, waits=[tex] + t["srcp"].w_waits(),
                   tag=True)
        t["ssum"].rd(trc); t["srcp"].wr(trc)
        tcb = p.op("vector", "tensor_scalar", disp[:, sl], disp[:, sl], srcp, None,
                   ALU.mult, waits=[trc, tex], tag=True)
        t["srcp"].rd(tcb)
        dtoks.append(tcb)
    t["disp"].wr(dtoks[-1])
    dtm = L2
    t["dtm"] = Trk()
    t["dtm"].w = t["L2"].w
    t["dtm"].r = dict(t["L2"].r)
    dm_toks = []
    for b in range(BL):
        for pc in range(4):
            for ec in range(4):
                w = [dtoks[b * 4 + ec]] + (t["pt"].w_waits() if ec == 0 else [])
                tk = p.op("tensor", "transpose", pt[:, 128 * ec:128 * (ec + 1)],
                          disp[:, 512 * (b * 4 + ec) + 128 * pc:512 * (b * 4 + ec) + 128 * (pc + 1)],
                          ident, waits=w, tag=(ec == 3))
                t["disp"].rd(tk if tk else dtoks[-1])
            t["pt"].wr(tk)
            i = b * 4 + pc
            ev = p.op("vector", "tensor_copy", dtm[:, 512 * i:512 * (i + 1)], pt[:, 0:512],
                      waits=[tk] + (t["dtm"].w_waits() if i == 0 else []), tag=True)
            t["pt"].rd(ev)
            dm_toks.append(ev)
    t["dtm"].wr(dm_toks[-1])
    xm_toks = []
    for b in range(BL):
        for pc in range(4):
            i = b * 4 + pc
            for dc in range(DC):
                w = [xs_tok] + (t["pt"].w_waits() if dc % 8 == 0 else [])
                tk = p.op("tensor", "transpose", pt[:, 128 * (dc % 8):128 * ((dc % 8) + 1)],
                          xs[:, 1024 * dc + 512 * b + 128 * pc:1024 * dc + 512 * b + 128 * (pc + 1)],
                          ident, waits=w, tag=(dc == DC - 1))
                t["xs"].rd(tk if tk else xs_tok)
            t["pt"].wr(tk)
            ev = p.op("vector", "tensor_copy", xstm[:, 1024 * i:1024 * (i + 1)], pt[:, 0:1024],
                      waits=[tk] + (t["xstm"].w_waits() if i == 0 else []), tag=True)
            t["pt"].rd(ev)
            xm_toks.append(ev)
    t["xstm"].wr(xm_toks[-1])
    sl2toks = []
    for b in range(BL):
        for dc in range(DC):
            po = (pv0, pv1)[dc % 2]
            tro = t["pv%d" % (dc % 2)]
            mm = None
            for pc in range(4):
                i = b * 4 + pc
                w = [xm_toks[i], dm_toks[i]] + (tro.w_waits() if pc == 0 else [])
                mm = p.op("tensor", "matmul", po[:, :],
                          xstm[:, 1024 * i + 128 * dc:1024 * i + 128 * (dc + 1)],
                          dtm[:, 512 * i:512 * (i + 1)],
                          waits=w, start=(pc == 0), stop=(pc == 3), tag=(pc == 3))
                t["xstm"].rd(mm)
                t["dtm"].rd(mm)
            tro.wr(mm)
            ev = p.op("vector", "tensor_copy", sl2[:, 1024 * dc + 512 * b:1024 * dc + 512 * (b + 1)],
                      po[:, :], waits=[mm] + (t["sl2"].w_waits() if b == 0 and dc == 0 else []),
                      tag=True)
            tro.rd(ev)
            sl2toks.append(ev)
    t["sl2"].wr(sl2toks[-1])
    c.pop()
    sl2_tok = t["sl2"].w
    cb2_tok = t["comb2"].w

    c.push()
    h2 = c.sbuf("h2", [128, 32 * 256], BF16)
    o2toks = []
    for e2 in range(E2):
        h2toks = []
        for mc in range(32):
            par = (mc // 4) % 2
            if mc % 4 == 0:
                tw = p.dma("gpsimd", wstB[par], c.hbm["w1s"][e2][mc:mc + 4],
                           waits=t["wstB%d" % par].w_waits(), ch=par)
                t["wstB%d" % par].wr(tw)
            tw = t["wstB%d" % par].w
            pz = pb(mc % 2)
            trz = t["pb%d" % (mc % 2)]
            mm = None
            for kc in range(DC):
                rhs = sl2[:, 1024 * kc:1024 * (kc + 1)].rearrange(
                    "q (b n) -> q b n", b=BL)[:, :, 128 * e2:128 * (e2 + 1)]
                w = [sl2_tok, tw] + (trz.w_waits() if kc == 0 else [])
                mm = p.op("tensor", "matmul", pz[:, 0:256],
                          wstB[par][:, 1024 * (mc % 4) + 128 * kc:1024 * (mc % 4) + 128 * (kc + 1)],
                          rhs,
                          waits=w, start=(kc == 0), stop=(kc == DC - 1), tag=(kc == DC - 1))
                t["sl2"].rd(mm)
                t["wstB%d" % par].rd(mm)
            trz.wr(mm)
            ev = p.op("scalar", "activation", h2[:, 256 * mc:256 * (mc + 1)], pz[:, 0:256],
                      AF.Gelu, bias=c.smallc("b1s", 32 * e2 + mc),
                      waits=[mm] + (t["h2"].w_waits() if mc == 0 else []), tag=True)
            trz.rd(ev)
            h2toks.append(ev)
        t["h2"].wr(h2toks[-1])
        for dc in range(DC):
            par = dc % 2
            tw = p.dma("gpsimd", wstB2[par], c.hbm["w2s"][e2][dc],
                       waits=t["wstB2%d" % par].w_waits(), ch=par)
            t["wstB2%d" % par].wr(tw)
            po = (pv0, pv1)[dc % 2]
            tro = t["pv%d" % (dc % 2)]
            mm = None
            for kc in range(32):
                w = [h2toks[kc], tw] + (tro.w_waits() if kc == 0 else [])
                mm = p.op("tensor", "matmul", po[:, 0:256],
                          wstB2[par][:, 128 * kc:128 * (kc + 1)],
                          h2[:, 256 * kc:256 * (kc + 1)],
                          waits=w, start=(kc == 0), stop=(kc == 31), tag=(kc == 31))
                t["h2"].rd(mm)
                t["wstB2%d" % par].rd(mm)
            tro.wr(mm)
            dst = o2f[:, 1024 * dc:1024 * (dc + 1)].rearrange(
                "q (b n) -> q b n", b=BL)[:, :, 128 * e2:128 * (e2 + 1)]
            ev = p.op("scalar", "activation", dst, po[:, 0:256], AF.Identity,
                      bias=c.smallc("b2s", 8 * e2 + dc),
                      waits=[mm] + (t["o2f"].w_waits() if e2 == 0 and dc == 0 else []),
                      tag=True)
            tro.rd(ev)
            o2toks.append(ev)
    t["o2f"].wr(o2toks[-1])
    c.pop()

    c.push()
    o2tm = c.sbuf("o2tm", [128, 8 * 1024], BF16)
    c2t = c.sbuf("c2t", [128, 8 * 512], BF16)
    om_toks = []
    for b in range(BL):
        for ec in range(4):
            i = b * 4 + ec
            for dc in range(DC):
                w = [o2toks[-1]] + (t["pt"].w_waits() if dc % 8 == 0 else [])
                tk = p.op("tensor", "transpose", pt[:, 128 * (dc % 8):128 * ((dc % 8) + 1)],
                          o2f[:, 1024 * dc + 512 * b + 128 * ec:1024 * dc + 512 * b + 128 * (ec + 1)],
                          ident, waits=w, tag=(dc == DC - 1))
                t["o2f"].rd(tk if tk else o2toks[-1])
            t["pt"].wr(tk)
            ev = p.op("vector", "tensor_copy", o2tm[:, 1024 * i:1024 * (i + 1)], pt[:, 0:1024],
                      waits=[tk] + (t["o2tm"].w_waits() if i == 0 else []), tag=True)
            t["pt"].rd(ev)
            om_toks.append(ev)
    t["o2tm"].wr(om_toks[-1])
    ct_toks = []
    for b in range(BL):
        for ec in range(4):
            for pc in range(4):
                w = [cb2_tok] + (t["pt"].w_waits() if pc == 0 else [])
                tk = p.op("tensor", "transpose", pt[:, 128 * pc:128 * (pc + 1)],
                          comb2[:, 512 * (b * 4 + pc) + 128 * ec:512 * (b * 4 + pc) + 128 * (ec + 1)],
                          ident, waits=w, tag=(pc == 3))
                t["comb2"].rd(tk if tk else cb2_tok)
            t["pt"].wr(tk)
            i = b * 4 + ec
            ev = p.op("vector", "tensor_copy", c2t[:, 512 * i:512 * (i + 1)], pt[:, 0:512],
                      waits=[tk] + (t["c2t"].w_waits() if i == 0 else []), tag=True)
            t["pt"].rd(ev)
            ct_toks.append(ev)
    t["c2t"].wr(ct_toks[-1])
    ot_toks = []
    for b in range(BL):
        for dc in range(DC):
            po = (pv0, pv1)[dc % 2]
            tro = t["pv%d" % (dc % 2)]
            mm = None
            for ec in range(4):
                i = b * 4 + ec
                w = [om_toks[i], ct_toks[i]] + (tro.w_waits() if ec == 0 else [])
                mm = p.op("tensor", "matmul", po[:, :],
                          o2tm[:, 1024 * i + 128 * dc:1024 * i + 128 * (dc + 1)],
                          c2t[:, 512 * i:512 * (i + 1)],
                          waits=w, start=(ec == 0), stop=(ec == 3), tag=(ec == 3))
                t["o2tm"].rd(mm)
                t["c2t"].rd(mm)
            tro.wr(mm)
            ev = p.op("vector", "tensor_copy",
                      out2[:, 1024 * dc + 512 * b:1024 * dc + 512 * (b + 1)], po[:, :],
                      waits=[mm] + (t["out2"].w_waits() if b == 0 and dc == 0 else []),
                      tag=True)
            tro.rd(ev)
            ot_toks.append(ev)
    t["out2"].wr(ot_toks[-1])
    c.pop()
    out2_tok = t["out2"].w

    c.push()
    ffn_fm = c.sbuf("ffn_fm", [128, DC * 4096], BF16)
    cT = c.views("cT", 2, [16, 64], BF16)
    o2g = c.views("o2g", 2, [16, 1024], BF16)
    fsb = c.views("fsb", 2, [64, 1024], BF16)
    ff_toks = []
    for bg in range(64):
        b, g = bg // 32, bg % 32
        par = bg % 2
        tk = p.op("tensor", "transpose", pt[0:16, 0:64], cmb[:, 16 * bg:16 * (bg + 1)],
                  ident[0:64, 0:64], waits=[cmb_tok] + t["pt"].w_waits(), tag=True)
        t["cmb"].rd(tk); t["pt"].wr(tk)
        ck = p.op("vector", "tensor_copy", cT[par], pt[0:16, 0:64],
                  waits=[tk] + t["cT%d" % par].w_waits(), tag=True)
        t["pt"].rd(ck); t["cT%d" % par].wr(ck)
        gtk = None
        for dc in range(DC):
            w = [out2_tok] + (t["pt"].w_waits() if dc == 0 else [])
            gtk = p.op("tensor", "transpose", pt[0:16, 128 * (dc % 8):128 * ((dc % 8) + 1)],
                       out2[:, 1024 * dc + 512 * b + 16 * g:1024 * dc + 512 * b + 16 * (g + 1)],
                       ident, waits=w, tag=(dc == DC - 1))
            t["out2"].rd(gtk if gtk else out2_tok)
        t["pt"].wr(gtk)
        gk = p.op("vector", "tensor_copy", o2g[par], pt[0:16, 0:1024],
                  waits=[gtk] + t["o2g%d" % par].w_waits(), tag=True)
        t["pt"].rd(gk); t["o2g%d" % par].wr(gk)
        mms = []
        for hf in range(2):
            po = (pv0, pv1)[hf]
            tro = t["pv%d" % hf]
            mm = p.op("tensor", "matmul", po[0:64, :], cT[par],
                      o2g[par][:, 512 * hf:512 * (hf + 1)],
                      waits=[ck, gk] + tro.w_waits(), start=True, stop=True, tag=True)
            t["cT%d" % par].rd(mm)
            t["o2g%d" % par].rd(mm)
            tro.wr(mm)
            mms.append(mm)
        ev = p.op("vector", "tensor_copy", fsb[par][:, 0:512], pv0[0:64, :],
                  waits=[mms[0]] + t["fsb%d" % par].w_waits(), tag=True)
        t["pv0"].rd(ev)
        ev2 = p.op("vector", "tensor_copy", fsb[par][:, 512:1024], pv1[0:64, :],
                   waits=[mms[1], ev], tag=True)
        t["pv1"].rd(ev2)
        t["fsb%d" % par].wr(ev2)
        for dc in range(DC):
            tk = p.op("tensor", "transpose", pt[:, 0:64],
                      fsb[par][:, 128 * dc:128 * (dc + 1)], ident[0:64, 0:64],
                      waits=[ev2] + t["pt"].w_waits(), tag=True)
            t["fsb%d" % par].rd(tk); t["pt"].wr(tk)
            fk = p.op("vector", "tensor_copy",
                      ffn_fm[:, 4096 * dc + 2048 * b + 64 * g:4096 * dc + 2048 * b + 64 * (g + 1)],
                      pt[:, 0:64],
                      waits=[tk] + (t["ffn_fm"].w_waits() if bg == 0 and dc == 0 else []),
                      tag=True)
            t["pt"].rd(fk)
            ff_toks.append(fk)
    t["ffn_fm"].wr(ff_toks[-1])

    acct = [tmpq[0][:, 0:512], tmpq[1][:, 0:512]]
    yt = [tmpq[0][:, 512:1024], tmpq[1][:, 512:1024]]
    for i2 in range(2):
        t["acct%d" % i2] = Trk()
        t["acct%d" % i2].w = t["tmpq%d" % i2].w
        t["acct%d" % i2].r = dict(t["tmpq%d" % i2].r)
        t["yt%d" % i2] = Trk()
        t["yt%d" % i2].w = t["tmpq%d" % i2].w
    for dc in range(DC):
        for i in range(8):
            par = i % 2
            cc = slice(512 * i, 512 * (i + 1))
            b, tk0 = (512 * i) // 2048, (512 * i) % 2048
            al = p.dma("sync", acct[par], acc_s[b, dc, :, tk0:tk0 + 512],
                       waits=t["acct%d" % par].w_waits(), ch=par)
            t["acct%d" % par].wr(al)
            ta = p.op("vector", "tensor_tensor", yt[par], acct[par],
                      ffn_fm[:, 4096 * dc + cc.start:4096 * dc + cc.stop], ALU.add,
                      waits=[al, ff_toks[-1]] + t["yt%d" % par].w_waits(), tag=True)
            t["acct%d" % par].rd(ta)
            t["ffn_fm"].rd(ta)
            t["yt%d" % par].wr(ta)
            dtk = p.dma("sync", y[b, dc, :, tk0:tk0 + 512], yt[par], waits=[ta])
            t["yt%d" % par].rd(dtk)
    c.pop()
    c.pop()
    c.pop()


# version 06
# speedup vs baseline: 1.2386x; 1.0730x over previous
import numpy as np
import ml_dtypes
from contextlib import ExitStack
import concourse.bass as bass
import concourse.mybir as mybir
from concourse.bass_utils import run_bass_kernel_spmd

F32 = mybir.dt.float32
BF16 = mybir.dt.bfloat16
I32 = mybir.dt.int32
AF = mybir.ActivationFunctionType
ALU = mybir.AluOpType
AX = mybir.AxisListType

B, S, D = 16, 2048, 1024
NCORES = 8
BL = B // NCORES
TOK = BL * S
NH, DH = 16, 64
NPAIR = NH // 2
FH = 3072
EXP = 4096
KG, E1, S1, E2 = 64, 8, 2, 4
G = S // KG
M1 = E1 * S1
LL = G * M1
N2 = LL // E2
HID = 4096
DC = D // 128
ENGINES = ("sync", "gpsimd", "scalar", "vector", "tensor")
NEG = -1.0e30


def _lhsT(w):
    K, M = w.shape
    return np.ascontiguousarray(
        w.reshape(K // 128, 128, M // 128, 128).transpose(2, 1, 0, 3))


def _fm(v):
    return np.ascontiguousarray(np.asarray(v).reshape(-1, 128).T)


class _Smalls:
    def __init__(self):
        self.cols = []
        self.map = {}
        self.n = 0

    def add(self, name, arr):
        arr = np.asarray(arr, np.float32).reshape(128, -1)
        self.map[name] = self.n
        self.n += arr.shape[1]
        self.cols.append(arr)

    def finalize(self):
        return np.ascontiguousarray(np.concatenate(self.cols, axis=1))


def prep_inputs(inp):
    d = {}
    wq = inp["L0_attn_Wq"].reshape(NPAIR, 2, D, DH)
    wq = wq.transpose(0, 2, 1, 3).reshape(NPAIR, D, 128)
    d["wq"] = np.stack([_lhsT(w)[0] for w in wq])
    wk = inp["L0_attn_Wk"].reshape(NPAIR, 2, D, DH).transpose(0, 2, 1, 3).reshape(NPAIR, D, 128)
    d["wk"] = np.stack([_lhsT(w)[0] for w in wk])
    d["wv"] = np.ascontiguousarray(inp["L0_attn_Wv"].transpose(1, 0, 2).reshape(D, D))
    d["wo0"] = _lhsT(inp["L0_attn_Wo"])
    d["wg0"] = _lhsT(np.ascontiguousarray(inp["L0_ffn_w_gate"].T))
    d["wu0"] = _lhsT(np.ascontiguousarray(inp["L0_ffn_w_up"].T))
    d["wd0"] = _lhsT(np.ascontiguousarray(inp["L0_ffn_w_down"].T))
    d["w1"] = _lhsT(inp["L1_attn_W1"])
    d["w2"] = _lhsT(inp["L1_attn_W2"])
    d["wz"] = _lhsT(inp["L1_attn_Wz"])
    d["wh"] = _lhsT(inp["L1_attn_Wh"])
    d["wo1"] = _lhsT(inp["L1_attn_Wo"])
    ws1 = inp["L1_ffn_W_slot1"].reshape(D, M1, D).transpose(1, 0, 2)
    d["ws1"] = np.stack([_lhsT(w) for w in ws1])
    d["w1f"] = np.stack([_lhsT(w) for w in inp["L1_ffn_W1_first"]])
    d["w2f"] = np.stack([_lhsT(w) for w in inp["L1_ffn_W2_first"]])
    ws2 = inp["L1_ffn_W_slot2"].reshape(D, E2, D).transpose(1, 0, 2)
    d["ws2"] = np.stack([_lhsT(w) for w in ws2])
    d["w1s"] = np.stack([_lhsT(w) for w in inp["L1_ffn_W1_second"]])
    d["w2s"] = np.stack([_lhsT(w) for w in inp["L1_ffn_W2_second"]])

    sm = _Smalls()
    sm.add("bq", inp["L0_attn_bq"].reshape(NPAIR, 128).T)
    sm.add("bk", inp["L0_attn_bk"].reshape(NPAIR, 128).T)
    sm.add("bo0", _fm(inp["L0_attn_bo"]))
    sm.add("b1", _fm(inp["L1_attn_b1"]))
    sm.add("b2", _fm(inp["L1_attn_b2"]))
    sm.add("bz", _fm(inp["L1_attn_bz"]))
    sm.add("bh", _fm(inp["L1_attn_bh"]))
    sm.add("bo1", _fm(inp["L1_attn_bo"]))
    sm.add("g1", _fm(inp["L1_ffn_gamma1"]))
    sm.add("gs1", _fm(inp["L1_ffn_gamma_slot1"]))
    sm.add("g2", _fm(inp["L1_ffn_gamma2"]))
    sm.add("gs2", _fm(inp["L1_ffn_gamma_slot2"]))
    sm.add("b1f", np.concatenate([_fm(v) for v in inp["L1_ffn_b1_first"]], 1))
    sm.add("b2f", np.concatenate([_fm(v) for v in inp["L1_ffn_b2_first"]], 1))
    sm.add("b1s", np.concatenate([_fm(v) for v in inp["L1_ffn_b1_second"]], 1))
    sm.add("b2s", np.concatenate([_fm(v) for v in inp["L1_ffn_b2_second"]], 1))
    sm.add("wg", _fm(inp["L1_ffn_Wg"][:, 0]))
    d["smalls"] = sm.finalize()
    d["smmap"] = sm.map
    row = np.zeros((1, 1040), np.float32)
    row[0, :D] = inp["L0_attn_bv"].reshape(D)
    row[0, D:D + E1] = inp["L1_ffn_expert_gates"]
    row[0, D + E1] = inp["L1_ffn_bg"][0]
    d["smrow"] = row
    return d



class Trk:
    __slots__ = ("w", "r")

    def __init__(self):
        self.w = None
        self.r = {}

    def rd(self, tok):
        if tok is not None:
            s, v = tok
            if self.r.get(s, 0) < v:
                self.r[s] = v

    def wr(self, tok):
        if tok is not None:
            self.w = tok
            self.r = {}

    def w_waits(self):
        out = [(s, v) for s, v in self.r.items()]
        if self.w is not None:
            out.append(self.w)
        return out

    def r_waits(self):
        return [self.w] if self.w is not None else []


class Prog:
    def __init__(self, nc):
        self.nc = nc
        self.rec = {e: [] for e in ENGINES}
        self.semh = {}
        self.cnt = {}
        self.wm = {e: {} for e in ENGINES}
        for e in ENGINES:
            self._sem("s_" + e)
        self.bounds = {}
        self.fence = {}
        for q in ("sync", "gpsimd", "scalar"):
            for sfx in ("", "B", "C"):
                self._sem("d" + sfx + "_" + q)
                self.bounds["d" + sfx + "_" + q] = [0]
                self.fence["d" + sfx + "_" + q] = None
        self.tag_since_wait = {e: False for e in ENGINES}
        self.last_tag = {e: 0 for e in ENGINES}
        self.ninstr = 0

    def _sem(self, name):
        self.semh[name] = self.nc.alloc_semaphore(name)
        self.cnt[name] = 0

    def wait(self, eng, tok):
        if tok is None:
            return
        s, v = tok
        if s in self.bounds and not (eng in ("sync", "gpsimd", "scalar") and s.endswith("_" + eng)):
            bl = self.bounds[s]
            if bl[-1] < v:
                cur = self.cnt[s]
                bl.append(cur)
                self.fence[s] = cur
                v = cur
            else:
                import bisect
                v = bl[bisect.bisect_left(bl, v)]
        if self.wm[eng].get(s, 0) >= v:
            return
        self.wm[eng][s] = v
        self.rec[eng].append(("w", s, v))
        self.tag_since_wait[eng] = False

    def op(self, eng, meth, *args, waits=(), tag=False, **kw):
        for t in waits:
            self.wait(eng, t)
        tok = None
        sem = None
        if tag:
            if self.tag_since_wait[eng]:
                sname = "s_" + eng
                lv = self.last_tag[eng]
                self.rec[eng].append(("w", sname, lv))
                if self.wm[eng].get(sname, 0) < lv:
                    self.wm[eng][sname] = lv
            sname = "s_" + eng
            self.cnt[sname] += 1
            tok = (sname, self.cnt[sname])
            sem = sname
            self.tag_since_wait[eng] = True
            self.last_tag[eng] = self.cnt[sname]
        self.rec[eng].append(("i", meth, args, kw, sem, 1))
        self.ninstr += 1
        return tok

    def dma(self, q, out, in_, waits=(), ch=0, **kw):
        sname = "d" + ("", "B", "C")[ch] + "_" + q
        if self.fence[sname] is not None:
            fv = self.fence[sname]
            self.fence[sname] = None
            if self.wm[q].get(sname, 0) < fv:
                self.wm[q][sname] = fv
                self.rec[q].append(("w", sname, fv))
        for t in waits:
            self.wait(q, t)
        self.cnt[sname] += 16
        self.rec[q].append(("i", "dma_start", (), dict(out=out, in_=in_, **kw), sname, 16))
        self.ninstr += 1
        return (sname, self.cnt[sname])

    def barrier(self):
        for q in ("sync", "gpsimd", "scalar"):
            for sfx in ("", "B", "C"):
                sname = "d" + sfx + "_" + q
                cur = self.cnt[sname]
                if self.bounds[sname][-1] < cur:
                    self.bounds[sname].append(cur)
                self.fence[sname] = None
                self.wait(q, (sname, cur))
        toks = [self.op(e, "nop", tag=True) for e in ENGINES]
        for e in ENGINES:
            for t in toks:
                self.wait(e, t)

    def finish(self):
        for q in ("sync", "gpsimd", "scalar"):
            for sfx in ("", "B", "C"):
                sname = "d" + sfx + "_" + q
                cur = self.cnt[sname]
                if self.bounds[sname][-1] < cur:
                    self.bounds[sname].append(cur)
                self.wait("sync", (sname, cur))

    def replay(self):
        with self.nc.Block() as blk:
            for e in ENGINES:
                getattr(blk, e)(lambda eng, e=e: self._run(eng, e))

    def _run(self, eng, ename):
        semh = self.semh
        for item in self.rec[ename]:
            if item[0] == "w":
                eng.wait_ge(semh[item[1]], item[2])
            else:
                _, meth, args, kw, sem, inc = item
                try:
                    inst = getattr(eng, meth)(*args, **kw)
                except Exception as ex:
                    def nm(a):
                        try:
                            return "%s%s" % (a.tensor.name, list(a.shape))
                        except Exception:
                            return str(a)[:40]
                    raise RuntimeError(
                        "replay failed: engine=%s meth=%s args=%s kw=%s: %s"
                        % (ename, meth, [nm(a) for a in args],
                           {k: nm(v) for k, v in kw.items()}, ex)) from ex
                if sem is not None:
                    inst.then_inc(semh[sem], inc)


class Ctx:
    def __init__(self, nc, p, stack):
        self.nc = nc
        self.p = p
        self.hbm = {}
        self.sb = {}
        self.trk = {}
        self.sm = None
        self.gstack = stack
        self.stack = stack
        self._frames = []

    def push(self):
        fr = ExitStack()
        self._frames.append(fr)
        self.stack = fr

    def pop(self):
        fr = self._frames.pop()
        fr.close()
        self.stack = self._frames[-1] if self._frames else self.gstack
        self.p.barrier()

    def sbuf(self, name, shape, dtype):
        self._uniq = getattr(self, "_uniq", 0) + 1
        h = self.stack.enter_context(
            self.nc.sbuf_tensor("%s_u%d" % (name, self._uniq), shape, dtype))
        ap = h[:]
        self.sb[name] = ap
        self.trk[name] = Trk()
        return ap

    def views(self, name, n, shape, dtype):
        ap = self.sbuf(name, [shape[0], n * shape[1]], dtype)
        out = []
        for i in range(n):
            out.append(ap[:, i * shape[1]:(i + 1) * shape[1]])
            self.trk[name + str(i)] = Trk()
        return out

    def psum(self, name, shape, dtype):
        self._uniq = getattr(self, "_uniq", 0) + 1
        h = self.stack.enter_context(
            self.nc.psum_tensor("%s_u%d" % (name, self._uniq), shape, dtype))
        ap = h[:]
        self.sb[name] = ap
        self.trk[name] = Trk()
        return ap

    def cp(self, name, prow=False):
        j = self.cpmap[name]
        ap = self.sb["cpool"]
        return ap[0:1, j:j + 1] if prow else ap[:, j:j + 1]

    def smallc(self, name, off=0, n=1):
        col = self.sm[name] + off
        return self.sb["smalls"][:, col:col + n]


CONSTS = {"eps5": 1e-5, "s8": 0.125, "dq": 1.0 / 127.0, "neg1": -1.0, "one": 1.0}


def emit_consts(c):
    p = c.p
    cpool = c.sbuf("cpool", [128, len(CONSTS)], F32)
    c.cpmap = {}
    for j, (nme, val) in enumerate(CONSTS.items()):
        c.cpmap[nme] = j
        p.op("gpsimd", "memset", cpool[:, j:j + 1], float(val), tag=True)
    ones_f = c.sbuf("ones_f", [128, 128], F32)
    ident_b = c.sbuf("ident_b", [128, 128], BF16)
    mask = c.sbuf("mask", [128, 128], F32)
    smalls = c.sbuf("smalls", [128, c.hbm["smalls"].shape[1]], F32)
    smrow = c.sbuf("smrow", [1, 1040], F32)
    t1 = p.op("gpsimd", "memset", ones_f, 1.0, tag=True)
    t3 = p.op("gpsimd", "memset", ident_b, 0.0, tag=True)
    t4 = p.op("gpsimd", "memset", mask, 0.0, tag=True)
    p.op("gpsimd", "affine_select", waits=[t3], tag=True, out=ident_b,
         in_=ident_b, compare_op=ALU.not_equal, fill=1.0, base=0,
         pattern=[[-1, 128]], channel_multiplier=1)
    p.op("gpsimd", "affine_select", waits=[t4], tag=True, out=mask,
         in_=mask, compare_op=ALU.is_ge, fill=NEG, base=0,
         pattern=[[-1, 128]], channel_multiplier=1)
    p.dma("sync", smalls, c.hbm["smalls"])
    p.dma("sync", smrow, c.hbm["smrow"])
    p.barrier()


def row_recip_sqrt(c, out_row, in_row, eps, scale=1.0, waits=(), floor=None):
    p = c.p
    n = out_row.shape[-1]
    tmp = c.sb["rowtmp"][0:1, 0:n]
    assert eps == 1e-5
    tk = p.op("scalar", "activation", tmp, in_row, AF.Sqrt, bias=c.cp("eps5", prow=True),
              waits=list(waits) + c.trk["rowtmp"].w_waits(), tag=True)
    c.trk["rowtmp"].wr(tk)
    if floor is not None:
        tk = p.op("vector", "tensor_scalar_max", tmp, tmp, float(floor), waits=[tk], tag=True)
        c.trk["rowtmp"].wr(tk)
    tk = p.op("vector", "reciprocal", out_row, tmp, waits=[tk], tag=True)
    c.trk["rowtmp"].rd(tk)
    if scale != 1.0:
        tk = p.op("vector", "tensor_scalar_mul", out_row, out_row, float(scale),
                  waits=[tk], tag=True)
    return tk



DBG_L0 = False
L0_SKIP = ()


def emit_l0(c):
    p = c.p
    t = c.trk
    x = c.hbm["x"]
    xn0_s = c.hbm["xn0_s"]
    x1_s = c.hbm["x1_s"]
    c.push()
    alloc_psum_l0(c)

    pbig = c.sb["pbig"]
    pv0, pv1 = c.sb["pv0"], c.sb["pv1"]
    pt = c.sb["pt"]

    xch = c.views("xch", 2, [128, 512], F32)
    arow = c.sbuf("arow", [1, 512], F32)
    brow = c.sbuf("brow", [1, 512], F32)
    c.sbuf("rowtmp", [1, 1024], F32)
    for nme in ("mrow", "vrow"):
        c.sbuf(nme, [1, 512], F32)
    abc = c.sbuf("abc", [128, 512], F32)
    bbc = c.sbuf("bbc", [128, 512], F32)
    xn = c.sbuf("xn", [128, DC * 512], BF16)
    _shr = 16 if "smallbuf" in L0_SKIP else 1
    wq_sb = c.sbuf("wq_sb", [128, NPAIR * 1024 // _shr], BF16)
    wk_sb = c.sbuf("wk_sb", [128, NPAIR * 1024 // _shr], BF16)
    wsh = c.sbuf("wsh", [128, DC * 1024 // _shr], BF16)
    q_sb = c.sbuf("q_sb", [128, NPAIR * 2048 // _shr], BF16)
    k_sb = c.sbuf("k_sb", [128, NPAIR * 2048 // _shr], BF16)
    v_sb = c.sbuf("v_sb", [128, 16 * 1024 // _shr], BF16)
    for pr in range(NPAIR):
        t["q%d" % pr] = Trk(); t["k%d" % pr] = Trk(); t["o%d" % pr] = Trk()
    for tc in range(16):
        t["v%d" % tc] = Trk()
    psb = c.views("psb", 2, [128, 2048], BF16)
    ptsb = c.views("ptsb", 2, [128, 2048], BF16)
    lsb = c.views("lsb", 2, [128, 1], F32)
    nm4 = c.sbuf("nm4", [128, 4], F32)
    l4 = c.sbuf("l4", [128, 4], F32)
    nmax = c.views("nmax", 2, [128, 1], F32)
    ebias = c.views("ebias", 2, [128, 1], F32)
    rsb = c.views("rsb", 2, [128, 1], F32)
    tmpf = c.views("tmpf", 2, [128, 512], F32)
    bv_bc = c.sbuf("bv_bc", [128, D], F32)
    t["ptg0"] = Trk(); t["ptg1"] = Trk()

    ones = c.sb["ones_f"]
    tbv = None
    for hf in range(2 if "bvbc" not in L0_SKIP else 0):
        pvh, trh = ((pv0, t["pv0"]), (pv1, t["pv1"]))[hf]
        bm = p.op("tensor", "matmul", pvh[:, :], ones[0:1, :],
                  c.sb["smrow"][0:1, 512 * hf:512 * (hf + 1)],
                  waits=trh.w_waits(), start=True, stop=True, tag=True)
        trh.wr(bm)
        tbv = p.op("scalar", "copy", out=bv_bc[:, 512 * hf:512 * (hf + 1)], in_=pvh[:, :],
                   waits=[bm], tag=True)
        trh.rd(tbv)
    t["bv_bc"].wr(tbv)

    def load_wsh(src_list):
        tok = None
        for i, src in enumerate(src_list):
            tok = p.dma("gpsimd", wsh[:, 1024 * i:1024 * (i + 1)], src,
                        waits=t["wsh"].w_waits() if i == 0 else ())
        t["wsh"].wr(tok)
        return tok

    for pr in range(NPAIR if "wload" not in L0_SKIP else 0):
        tok = p.dma("gpsimd", wq_sb[:, 1024 * pr:1024 * (pr + 1)], c.hbm["wq"][pr])
        t["wq_sb"].wr(tok)
        tok = p.dma("gpsimd", wk_sb[:, 1024 * pr:1024 * (pr + 1)], c.hbm["wk"][pr])
        t["wk_sb"].wr(tok)
    wq_tok = t["wq_sb"].w
    wk_tok = t["wk_sb"].w

    for s in range(BL):
        wv_tok = None if "wload" in L0_SKIP else load_wsh(
            [c.hbm["wv"][128 * dc:128 * (dc + 1), :] for dc in range(DC)])
        for ch in range(4):
            cc = slice(512 * ch, 512 * (ch + 1))
            if "stats" in L0_SKIP:
                xtoks = []
                for dc in range(DC):
                    par = dc % 2
                    ltk = p.dma("sync", xch[par], x[s, dc, :, cc],
                                waits=t["xch%d" % par].w_waits(), ch=par)
                    t["xch%d" % par].wr(ltk)
                    xnv = xn[:, 512 * dc:512 * (dc + 1)]
                    tk2 = p.op("vector", "tensor_copy", xnv, xch[par],
                               waits=[ltk] + (t["xn"].w_waits() if dc == 0 else []), tag=True)
                    t["xch%d" % par].rd(tk2)
                    xtoks.append(tk2)
                    dtk = p.dma("sync", xn0_s[s, dc, :, cc], xnv, waits=[tk2])
                    t["xn"].rd(dtk)
                t["xn"].wr(xtoks[-1])
            else:
                mm1 = mm2 = None
                for dc in range(DC):
                    par = dc % 2
                    tk = p.dma("sync", xch[par], x[s, dc, :, cc],
                               waits=t["xch%d" % par].w_waits(), ch=par)
                    t["xch%d" % par].wr(tk)
                    stk = p.op("scalar", "activation", tmpf[1], xch[par],
                               AF.Square, waits=[tk] + t["tmpf1"].w_waits(), tag=True)
                    t["tmpf1"].wr(stk); t["xch%d" % par].rd(stk)
                    mm1 = p.op("tensor", "matmul", pv0[0:1, :], ones[:, 0:1], xch[par],
                               waits=[tk] + (t["pv0"].w_waits() if dc == 0 else []),
                               start=(dc == 0), stop=(dc == DC - 1), tag=True)
                    mm2 = p.op("tensor", "matmul", pv1[0:1, :], ones[:, 0:1], tmpf[1],
                               waits=[stk] + (t["pv1"].w_waits() if dc == 0 else []),
                               start=(dc == 0), stop=(dc == DC - 1), tag=True)
                    t["tmpf1"].rd(mm2)
                    t["xch%d" % par].rd(mm1)
                t["pv0"].wr(mm1)
                t["pv1"].wr(mm2)
                mrow = c.sb["mrow"][0:1, :]
                vrow = c.sb["vrow"][0:1, :]
                m2r = c.sb["rowtmp"][0:1, 512:1024]
                tk_m = p.op("vector", "tensor_scalar_mul", mrow, pv0[0:1, :], 1.0 / D,
                            waits=[mm1] + t["mrow"].w_waits(), tag=True)
                t["pv0"].rd(tk_m); t["mrow"].wr(tk_m)
                tk_v = p.op("vector", "tensor_scalar_mul", vrow, pv1[0:1, :], 1.0 / D,
                            waits=[mm2] + t["vrow"].w_waits(), tag=True)
                t["pv1"].rd(tk_v); t["vrow"].wr(tk_v)
                tk_m2 = p.op("vector", "tensor_mul", m2r, mrow, mrow,
                             waits=[tk_m] + t["rowtmp"].w_waits(), tag=True)
                t["rowtmp"].wr(tk_m2)
                tk_v2 = p.op("vector", "tensor_sub", vrow, vrow, m2r,
                             waits=[tk_m2, tk_v], tag=True)
                t["vrow"].wr(tk_v2); t["rowtmp"].rd(tk_v2)
                av = arow[0:1, :]
                bv = brow[0:1, :]
                tk_a = row_recip_sqrt(c, av, vrow, 1e-5, waits=[tk_v2])
                t["vrow"].rd(tk_a); t["arow"].wr(tk_a)
                tk_b = p.op("vector", "scalar_tensor_tensor", bv, mrow, -1.0, av,
                            ALU.mult, ALU.mult, waits=[tk_a, tk_m] + t["brow"].w_waits(),
                            tag=True)
                t["mrow"].rd(tk_b); t["arow"].rd(tk_b); t["brow"].wr(tk_b)
                bca = p.op("tensor", "matmul", pv0[:, :], ones[0:1, :], av,
                           waits=[tk_a] + t["pv0"].w_waits(), start=True, stop=True, tag=True)
                t["arow"].rd(bca); t["pv0"].wr(bca)
                bcb = p.op("tensor", "matmul", pv1[:, :], ones[0:1, :], bv,
                           waits=[tk_b] + t["pv1"].w_waits(), start=True, stop=True, tag=True)
                t["brow"].rd(bcb); t["pv1"].wr(bcb)
                ca = p.op("scalar", "copy", waits=[bca] + t["abc"].w_waits(), tag=True,
                          out=abc, in_=pv0[:, :])
                t["abc"].wr(ca); t["pv0"].rd(ca)
                cb = p.op("scalar", "copy", waits=[bcb] + t["bbc"].w_waits(), tag=True,
                          out=bbc, in_=pv1[:, :])
                t["bbc"].wr(cb); t["pv1"].rd(cb)
                xtoks = []
                dtks = []
                for dc in range(DC):
                    par = dc % 2
                    ltk = p.dma("sync", xch[par], x[s, dc, :, cc],
                                waits=t["xch%d" % par].w_waits(), ch=par)
                    t["xch%d" % par].wr(ltk)
                    xv = xch[par]
                    xnv = xn[:, 512 * dc:512 * (dc + 1)]
                    tk1 = p.op("vector", "tensor_mul", tmpf[0], xv, abc,
                               waits=[ca, ltk] + t["tmpf0"].w_waits(), tag=True)
                    t["tmpf0"].wr(tk1); t["abc"].rd(tk1); t["xch%d" % par].rd(tk1)
                    tk2 = p.op("vector", "tensor_tensor", xnv, tmpf[0], bbc, ALU.add,
                               waits=[tk1, cb] + (t["xn"].w_waits() if dc == 0 else []),
                               tag=True)
                    t["tmpf0"].rd(tk2); t["bbc"].rd(tk2)
                    xtoks.append(tk2)
                    dtk = p.dma("sync", xn0_s[s, dc, :, cc], xnv, waits=[tk2])
                    dtks.append(dtk)
                t["xn"].wr(xtoks[-1])
                for dtk in dtks:
                    t["xn"].rd(dtk)
            for pr in range(NPAIR if "qk" not in L0_SKIP else 0):
                ps_q, trq = (pv0, t["pv0"]) if pr % 2 == 0 else (pv1, t["pv1"])
                mmq = None
                for kc in range(DC):
                    w = [xtoks[kc], wq_tok] + (trq.w_waits() if kc == 0 else [])
                    mmq = p.op("tensor", "matmul", ps_q[:, :],
                               wq_sb[:, 1024 * pr + 128 * kc:1024 * pr + 128 * (kc + 1)],
                               xn[:, 512 * kc:512 * (kc + 1)], waits=w,
                               start=(kc == 0), stop=(kc == DC - 1), tag=(kc == DC - 1))
                    t["xn"].rd(mmq)
                trq.wr(mmq)
                eq = p.op("scalar", "activation",
                          q_sb[:, 2048 * pr + cc.start:2048 * pr + cc.stop], ps_q[:, :],
                          AF.Identity, bias=c.smallc("bq", pr),
                          waits=[mmq] + t["q%d" % pr].w_waits(), tag=True)
                trq.rd(eq); t["q%d" % pr].wr(eq)
                mmk = None
                for kc in range(DC):
                    w = [xtoks[kc], wk_tok] + (trq.w_waits() if kc == 0 else [])
                    mmk = p.op("tensor", "matmul", ps_q[:, :],
                               wk_sb[:, 1024 * pr + 128 * kc:1024 * pr + 128 * (kc + 1)],
                               xn[:, 512 * kc:512 * (kc + 1)], waits=w,
                               start=(kc == 0), stop=(kc == DC - 1), tag=(kc == DC - 1))
                    t["xn"].rd(mmk)
                trq.wr(mmk)
                ek = p.op("scalar", "activation",
                          k_sb[:, 2048 * pr + cc.start:2048 * pr + cc.stop], ps_q[:, :],
                          AF.Identity, bias=c.smallc("bk", pr),
                          waits=[mmk] + t["k%d" % pr].w_waits(), tag=True)
                trq.rd(ek); t["k%d" % pr].wr(ek)
            for t4 in range(4 if "v" not in L0_SKIP else 0):
                tc = 4 * ch + t4
                for half in range(2):
                    i = (t4 * 2 + half) % 4
                    pv = pbig[:, 512 * i:512 * (i + 1)]
                    trv = t["pb%d" % i]
                    mmv = None
                    for kc in range(DC):
                        w = [xtoks[kc], wv_tok] + (trv.w_waits() if kc == 0 else [])
                        mmv = p.op("tensor", "matmul", pv,
                                   xn[:, 512 * kc + 128 * t4:512 * kc + 128 * (t4 + 1)],
                                   wsh[:, 1024 * kc + 512 * half:1024 * kc + 512 * (half + 1)],
                                   waits=w, start=(kc == 0), stop=(kc == DC - 1),
                                   tag=(kc == DC - 1))
                        t["xn"].rd(mmv); t["wsh"].rd(mmv)
                    trv.wr(mmv)
                    ev = p.op("vector", "tensor_tensor",
                              v_sb[:, 1024 * tc + 512 * half:1024 * tc + 512 * (half + 1)],
                              pv, bv_bc[:, 512 * half:512 * (half + 1)], ALU.add,
                              waits=[mmv, tbv] + t["v%d" % tc].w_waits(), tag=True)
                    trv.rd(ev); t["v%d" % tc].wr(ev)

        if DBG_L0 and s == 0:
            for pr in range(NPAIR):
                dtk = p.dma("sync", c.hbm["dbg_q"][:, 2048 * pr:2048 * (pr + 1)],
                            q_sb[:, 2048 * pr:2048 * (pr + 1)], waits=[t["q%d" % pr].w])
                t["q%d" % pr].rd(dtk)
                dtk = p.dma("sync", c.hbm["dbg_k"][:, 2048 * pr:2048 * (pr + 1)],
                            k_sb[:, 2048 * pr:2048 * (pr + 1)], waits=[t["k%d" % pr].w])
                t["k%d" % pr].rd(dtk)
            for tc in range(16):
                dtk = p.dma("sync", c.hbm["dbg_v"][:, 1024 * tc:1024 * (tc + 1)],
                            v_sb[:, 1024 * tc:1024 * (tc + 1)], waits=[t["v%d" % tc].w])
                t["v%d" % tc].rd(dtk)
        prev = None
        NSC = 16
        if "attn" in L0_SKIP:
            NSC = 0
        elif "attn1" in L0_SKIP:
            NSC = 1
        elif "attn4" in L0_SKIP:
            NSC = 4
        elif "attn8" in L0_SKIP:
            NSC = 8
        for sc in range(NSC):
            for h in range(NH):
                pr, h2 = h // 2, h % 2
                par = (sc * NH + h) % 2
                nt = sc + 1
                t_len = nt * 128
                qrows = slice(64 * h2, 64 * (h2 + 1))
                smtoks = []
                for jb in range((t_len + 511) // 512):
                    wdt = min(512, t_len - 512 * jb)
                    wts = [t["q%d" % pr].w, t["k%d" % pr].w] + t["pb%d" % jb].w_waits()
                    mm = p.op("tensor", "matmul", pbig[:, 512 * jb:512 * jb + wdt],
                              q_sb[qrows, 2048 * pr + 128 * sc:2048 * pr + 128 * (sc + 1)],
                              k_sb[qrows, 2048 * pr + 512 * jb:2048 * pr + 512 * jb + wdt],
                              waits=wts, start=True, stop=True, tag=True)
                    t["pb%d" % jb].wr(mm)
                    t["q%d" % pr].rd(mm)
                    t["k%d" % pr].rd(mm)
                    smtoks.append(mm)
                if prev is not None:
                    _l0_pt_pv(c, *prev)
                    prev = None
                tk_mask = p.op("vector", "tensor_tensor", pbig[:, t_len - 128:t_len],
                               pbig[:, t_len - 128:t_len], c.sb["mask"], ALU.add,
                               waits=[smtoks[-1]], tag=True)
                t["pb%d" % ((t_len - 1) // 512)].wr(tk_mask)
                njb = (t_len + 511) // 512
                mptoks = []
                for jb in range(njb):
                    wdt = min(512, t_len - 512 * jb)
                    w = [smtoks[jb], tk_mask]
                    if jb == 0:
                        w += t["nm4"].w_waits()
                    tk = p.op("vector", "tensor_reduce", nm4[:, jb:jb + 1],
                              pbig[:, 512 * jb:512 * jb + wdt], AX.X, ALU.max,
                              waits=w, tag=True)
                    t["pb%d" % jb].rd(tk)
                    mptoks.append(tk)
                t["nm4"].wr(mptoks[-1])
                tk_max = p.op("vector", "tensor_reduce", nmax[par], nm4[:, 0:njb],
                              AX.X, ALU.max, negate=True,
                              waits=mptoks + t["nmax" + str(par)].w_waits(), tag=True)
                t["nm4"].rd(tk_max)
                t["nmax" + str(par)].wr(tk_max)
                tk_eb = p.op("vector", "tensor_scalar_mul", ebias[par], nmax[par], 0.125,
                             waits=[tk_max] + t["ebias" + str(par)].w_waits(), tag=True)
                t["ebias" + str(par)].wr(tk_eb); t["nmax" + str(par)].rd(tk_eb)
                eptoks = []
                for jb in range(njb):
                    wdt = min(512, t_len - 512 * jb)
                    w = [tk_eb] + (t["psb%d" % par].w_waits() if jb == 0 else [])
                    if jb == 0:
                        w += t["l4"].w_waits()
                    tk = p.op("scalar", "activation", psb[par][:, 512 * jb:512 * jb + wdt],
                              pbig[:, 512 * jb:512 * jb + wdt], AF.Exp, scale=c.cp("s8"),
                              bias=ebias[par], accum_out=l4[:, jb:jb + 1],
                              waits=w, tag=True)
                    t["pb%d" % jb].rd(tk)
                    eptoks.append(tk)
                tk_exp = eptoks[-1]
                t["ebias" + str(par)].rd(tk_exp)
                t["l4"].wr(tk_exp)
                tk_ls = p.op("vector", "tensor_reduce", lsb[par], l4[:, 0:njb], AX.X,
                             ALU.add, waits=eptoks + t["lsb%d" % par].w_waits(), tag=True)
                t["l4"].rd(tk_ls)
                t["lsb%d" % par].wr(tk_ls)
                tk_r = p.op("vector", "reciprocal", rsb[par], lsb[par],
                            waits=[tk_ls] + t["rsb%d" % par].w_waits(), tag=True)
                t["lsb%d" % par].rd(tk_r); t["rsb%d" % par].wr(tk_r)
                tk_pn = p.op("vector", "tensor_scalar_mul", psb[par][:, 0:t_len],
                             psb[par][:, 0:t_len], rsb[par],
                             waits=eptoks + [tk_r], tag=True)
                t["rsb%d" % par].rd(tk_pn)
                t["psb%d" % par].wr(tk_pn)
                prev = (sc, h, par, nt)
        if prev is not None:
            _l0_pt_pv(c, *prev)

        if DBG_L0 and s == 0:
            for pr in range(NPAIR):
                dtk = p.dma("sync", c.hbm["dbg_o"][:, 2048 * pr:2048 * (pr + 1)],
                            q_sb[:, 2048 * pr:2048 * (pr + 1)], waits=[t["o%d" % pr].w])
                t["o%d" % pr].rd(dtk)
        wo_tok = None if "wload" in L0_SKIP else load_wsh([c.hbm["wo0"][mc] for mc in range(DC)])
        if "outproj" in L0_SKIP and "dummyout" in L0_SKIP:
            zk = p.op("vector", "memset", tmpf[0], 0.0,
                      waits=t["tmpf0"].w_waits(), tag=True)
            t["tmpf0"].wr(zk)
            for mc in range(DC):
                for ch4 in range(4):
                    dtk = p.dma("sync", x1_s[s, mc, :, 512 * ch4:512 * (ch4 + 1)],
                                tmpf[0], waits=[zk])
                    t["tmpf0"].rd(dtk)
        for ch in range(4 if "outproj" not in L0_SKIP else 0):
            cc = slice(512 * ch, 512 * (ch + 1))
            for mc in range(DC):
                xpar = mc % 2
                ltk = p.dma("sync", xch[xpar], x[s, mc, :, cc],
                            waits=t["xch%d" % xpar].w_waits())
                t["xch%d" % xpar].wr(ltk)
                po, tro = ((pv0, t["pv0"]), (pv1, t["pv1"]))[mc % 2]
                mmo = None
                for kc in range(DC):
                    w = [t["o%d" % kc].w, wo_tok] + (tro.w_waits() if kc == 0 else [])
                    mmo = p.op("tensor", "matmul", po[:, :],
                               wsh[:, 1024 * mc + 128 * kc:1024 * mc + 128 * (kc + 1)],
                               q_sb[:, 2048 * kc + cc.start:2048 * kc + cc.stop],
                               waits=w, start=(kc == 0), stop=(kc == DC - 1),
                               tag=(kc == DC - 1))
                    t["o%d" % kc].rd(mmo)
                    t["wsh"].rd(mmo)
                tro.wr(mmo)
                par = mc % 2
                tb = p.op("scalar", "activation", tmpf[par], po[:, :], AF.Identity,
                          bias=c.smallc("bo0", mc),
                          waits=[mmo] + t["tmpf%d" % par].w_waits(), tag=True)
                tro.rd(tb); t["tmpf%d" % par].wr(tb)
                ta = p.op("vector", "tensor_tensor", tmpf[par], tmpf[par],
                          xch[xpar], ALU.add,
                          waits=[tb, ltk], tag=True)
                t["xch%d" % xpar].rd(ta); t["tmpf%d" % par].wr(ta)
                dtk = p.dma("sync", x1_s[s, mc, :, cc], tmpf[par], waits=[ta])
                t["tmpf%d" % par].rd(dtk)
    c.pop()


def _l0_pt_pv(c, sc, h, par, nt):
    p = c.p
    t = c.trk
    pr, h2 = h // 2, h % 2
    pv2 = c.sb["pv0"]
    ptv = c.sb["ptsb"][:, 2048 * par:2048 * (par + 1)] if False else None
    ptv = c.sb["ptsb"][:, 2048 * par:2048 * (par + 1)]
    ptrk = t["ptsb%d" % par]
    pn_tok = t["psb%d" % par].w
    psbv = c.sb["psb"][:, 2048 * par:2048 * (par + 1)]
    groups = []
    for lo in range(0, nt, 4):
        hi = min(lo + 4, nt) - 1
        gname = ("pt", "ptB")[(lo // 4) % 2]
        pt = c.sb[gname]
        gtrk = t[gname]
        tk = None
        for tb in range(lo, hi + 1):
            slot = (tb % 4)
            wts = [pn_tok]
            if tb == lo:
                wts += gtrk.w_waits()
            tk = p.op("tensor", "transpose", pt[:, 128 * slot:128 * (slot + 1)],
                      psbv[:, 128 * tb:128 * (tb + 1)], c.sb["ident_b"],
                      waits=wts, tag=(tb == hi))
        gtrk.wr(tk)
        t["psb%d" % par].rd(tk)
        wdt = (hi - lo + 1) * 128
        ck = p.op("vector", "tensor_copy",
                  ptv[:, 128 * lo:128 * lo + wdt],
                  pt[:, 0:wdt],
                  waits=[tk] + (ptrk.w_waits() if lo == 0 else []), tag=True)
        gtrk.rd(ck)
        groups.append((lo, hi, ck))
    ptrk.wr(groups[-1][2])
    out = pv2[slice(64 * h2, 64 * (h2 + 1)), 0:128]
    first = True
    mm = None
    for lo, hi, ck in groups:
        for tb in range(lo, hi + 1):
            wts = [ck, t["v%d" % tb].w]
            if first:
                wts += t["pv0"].w_waits()
            mm = p.op("tensor", "matmul", out,
                      c.sb["v_sb"][:, 1024 * tb + 64 * h:1024 * tb + 64 * (h + 1)],
                      ptv[:, 128 * tb:128 * (tb + 1)],
                      waits=wts, start=first, stop=(tb == nt - 1), tag=(tb == nt - 1))
            t["v%d" % tb].rd(mm)
            ptrk.rd(mm)
            first = False
    t["pv0"].wr(mm)
    eo = p.op("scalar", "activation",
              c.sb["q_sb"][slice(64 * h2, 64 * (h2 + 1)),
                           2048 * pr + 128 * sc:2048 * pr + 128 * (sc + 1)],
              out, AF.Copy, waits=[mm] + t["q%d" % pr].w_waits(), tag=True)
    t["pv0"].rd(eo)
    t["o%d" % pr].wr(eo)


def alloc_psum_pool(c):
    c.psum("pbig", [128, 2048], F32)
    for i in range(4):
        c.trk["pb%d" % i] = Trk()
    c.psum("pv0", [128, 512], F32)
    c.psum("pv1", [128, 512], F32)
    c.psum("pv2", [128, 512], F32)
    c.psum("pt", [128, 1024], BF16)


def alloc_psum_l0(c):
    c.psum("pbig", [128, 2048], F32)
    for i in range(4):
        c.trk["pb%d" % i] = Trk()
    c.psum("pv0", [128, 512], F32)
    c.psum("pv1", [128, 512], F32)
    c.psum("pt", [128, 1024], BF16)
    c.psum("ptB", [128, 1024], BF16)



def emit_ffn0(c):
    p = c.p
    t = c.trk
    xn0_s = c.hbm["xn0_s"]
    x1_s = c.hbm["x1_s"]
    xnew_s = c.hbm["xnew_s"]
    xn2_s = c.hbm["xn2_s"]
    c.push()
    alloc_psum_pool(c)

    pbig = c.sb["pbig"]
    pv0, pv1, pv2 = c.sb["pv0"], c.sb["pv1"], c.sb["pv2"]

    wgb = c.sbuf("wgb", [128, 24 * 1024], BF16)
    wub = c.sbuf("wub", [128, 24 * 1024], BF16)
    wdb = c.sbuf("wdb", [128, 8 * 3072], BF16)
    c.push()
    stg = c.views("stg", 2, [128, 3072], F32)
    DQ = 1.0 / 127.0
    jobs = ([("wg0", wgb, mc, 1024) for mc in range(24)]
            + [("wu0", wub, mc, 1024) for mc in range(24)]
            + [("wd0", wdb, mc, 3072) for mc in range(8)])
    for j, (src, dst, mc, wdt) in enumerate(jobs):
        par = j % 2
        sv = stg[par][:, 0:wdt]
        dtk = p.dma("gpsimd", sv, c.hbm[src][mc], waits=t["stg%d" % par].w_waits(), ch=par)
        t["stg%d" % par].wr(dtk)
        ck = p.op("scalar", "mul", dst[:, wdt * mc:wdt * (mc + 1)], sv, c.cp("dq"),
                  waits=[dtk], tag=True)
        t["stg%d" % par].rd(ck)
        t[{id(wgb): "wgb", id(wub): "wub", id(wdb): "wdb"}[id(dst)]].wr(ck)
    c.pop()
    wg_tok, wu_tok, wd_tok = t["wgb"].w, t["wub"].w, t["wdb"].w

    xnch = c.sbuf("xnch", [128, DC * 256], BF16)
    h_sb = c.sbuf("h_sb", [128, 24 * 256], BF16)
    x1ch = c.views("x1ch", 2, [128, 256], F32)
    xnt = c.views("xnt", 2, [128, 256], F32)
    xnbf = c.sbuf("xnbf", [128, DC * 256], BF16)
    tmpf = c.views("tmpf", 2, [128, 256], F32)
    abc = c.sbuf("abc", [128, 256], F32)
    bbc = c.sbuf("bbc", [128, 256], F32)
    arow = c.sbuf("arow", [1, 256], F32)
    brow = c.sbuf("brow", [1, 256], F32)
    mrow = c.sbuf("mrow", [1, 256], F32)
    vrow = c.sbuf("vrow", [1, 256], F32)
    c.sbuf("rowtmp", [1, 512], F32)

    for s in range(BL):
        for ch in range(8):
            cc = slice(256 * ch, 256 * (ch + 1))
            ltoks = []
            for dc in range(DC):
                tk = p.dma("sync", xnch[:, 256 * dc:256 * (dc + 1)], xn0_s[s, dc, :, cc],
                           waits=(t["xnch"].w_waits() if dc == 0 else ()))
                ltoks.append(tk)
            t["xnch"].wr(ltoks[-1])
            for mc in range(24):
                pg = pbig[:, 512 * (mc % 2):512 * (mc % 2) + 256]
                trg = t["pb%d" % (mc % 2)]
                pu = pbig[:, 1024 + 256 * (mc % 2):1024 + 256 * (mc % 2) + 256]
                tru = t["pb2"]
                mmg = mmu = None
                for kc in range(DC):
                    w = [ltoks[kc], wg_tok] + (trg.w_waits() if kc == 0 else [])
                    mmg = p.op("tensor", "matmul", pg,
                               wgb[:, 1024 * mc + 128 * kc:1024 * mc + 128 * (kc + 1)],
                               xnch[:, 256 * kc:256 * (kc + 1)], waits=w,
                               start=(kc == 0), stop=(kc == DC - 1), tag=(kc == DC - 1))
                    t["xnch"].rd(mmg)
                    w = [ltoks[kc], wu_tok] + (tru.w_waits() if kc == 0 else [])
                    mmu = p.op("tensor", "matmul", pu,
                               wub[:, 1024 * mc + 128 * kc:1024 * mc + 128 * (kc + 1)],
                               xnch[:, 256 * kc:256 * (kc + 1)], waits=w,
                               start=(kc == 0), stop=(kc == DC - 1), tag=(kc == DC - 1))
                    t["xnch"].rd(mmu)
                trg.wr(mmg)
                tru.wr(mmu)
                par = mc % 2
                ts1 = p.op("scalar", "activation", tmpf[par], pg, AF.Silu,
                           waits=[mmg] + t["tmpf%d" % par].w_waits(), tag=True)
                trg.rd(ts1); t["tmpf%d" % par].wr(ts1)
                th = p.op("vector", "tensor_tensor", h_sb[:, 256 * mc:256 * (mc + 1)],
                          tmpf[par], pu, ALU.mult,
                          waits=[ts1, mmu] + (t["h_sb"].w_waits() if mc == 0 else []),
                          tag=True)
                t["tmpf%d" % par].rd(th); tru.rd(th)
            t["h_sb"].wr(th)
            stat_first = True
            smm = None
            for mc2 in range(DC):
                par = mc2 % 2
                pd = (pv0, pv1)[par]
                trd = t["pv%d" % par]
                mmd = None
                for kc in range(24):
                    w = [th, wd_tok] + (trd.w_waits() if kc == 0 else [])
                    mmd = p.op("tensor", "matmul", pd[:, 0:256],
                               wdb[:, 3072 * mc2 + 128 * kc:3072 * mc2 + 128 * (kc + 1)],
                               h_sb[:, 256 * kc:256 * (kc + 1)], waits=w,
                               start=(kc == 0), stop=(kc == 23), tag=(kc == 23))
                    t["h_sb"].rd(mmd)
                trd.wr(mmd)
                xl = p.dma("sync", x1ch[par], x1_s[s, mc2, :, cc],
                           waits=t["x1ch%d" % par].w_waits(), ch=par)
                t["x1ch%d" % par].wr(xl)
                ta = p.op("vector", "tensor_tensor", xnt[par], pd[:, 0:256], x1ch[par],
                          ALU.add, waits=[mmd, xl] + t["xnt%d" % par].w_waits(), tag=True)
                trd.rd(ta); t["x1ch%d" % par].rd(ta); t["xnt%d" % par].wr(ta)
                dnew = p.dma("sync", xnew_s[s, mc2, :, cc], xnt[par], waits=[ta])
                t["xnt%d" % par].rd(dnew)
                cbf = p.op("vector", "tensor_copy", xnbf[:, 256 * mc2:256 * (mc2 + 1)],
                           xnt[par], waits=[ta] + (t["xnbf"].w_waits() if mc2 == 0 else []),
                           tag=True)
                t["xnt%d" % par].rd(cbf)
                sk = p.op("scalar", "activation", tmpf[par], xnt[par], AF.Square,
                          waits=[ta] + t["tmpf%d" % par].w_waits(), tag=True)
                t["xnt%d" % par].rd(sk); t["tmpf%d" % par].wr(sk)
                w = [ta] + (t["pb3"].w_waits() if stat_first else [])
                smm = p.op("tensor", "matmul", pbig[:, 1536:1537][0:1, :].tensor[0:1, 1536:1792]
                           if False else pbig[0:1, 1536:1792], c.sb["ones_f"][:, 0:1],
                           xnt[par], waits=w, start=stat_first, stop=False, tag=True)
                t["xnt%d" % par].rd(smm)
                stat_first = False
                smm = p.op("tensor", "matmul", pbig[0:1, 1792:2048], c.sb["ones_f"][:, 0:1],
                           tmpf[par], start=False, stop=(mc2 == DC - 1),
                           waits=[sk], tag=True)
                t["tmpf%d" % par].rd(smm)
            t["pb3"].wr(smm)
            t["xnbf"].wr(cbf)
            tk_m = p.op("vector", "tensor_scalar_mul", mrow[0:1, :], pbig[0:1, 1536:1792],
                        1.0 / D, waits=[smm] + t["mrow"].w_waits(), tag=True)
            t["pb3"].rd(tk_m); t["mrow"].wr(tk_m)
            tk_v = p.op("vector", "tensor_scalar_mul", vrow[0:1, :], pbig[0:1, 1792:2048],
                        1.0 / D, waits=[smm] + t["vrow"].w_waits(), tag=True)
            t["pb3"].rd(tk_v); t["vrow"].wr(tk_v)
            m2r = c.sb["rowtmp"][0:1, 256:512]
            tk_m2 = p.op("vector", "tensor_mul", m2r, mrow[0:1, :], mrow[0:1, :],
                         waits=[tk_m] + t["rowtmp"].w_waits(), tag=True)
            t["rowtmp"].wr(tk_m2)
            tk_v2 = p.op("vector", "tensor_sub", vrow[0:1, :], vrow[0:1, :], m2r,
                         waits=[tk_m2, tk_v], tag=True)
            t["vrow"].wr(tk_v2); t["rowtmp"].rd(tk_v2)
            tk_a = row_recip_sqrt(c, arow[0:1, :], vrow[0:1, :], 1e-5, waits=[tk_v2])
            t["vrow"].rd(tk_a); t["arow"].wr(tk_a)
            tk_b = p.op("vector", "scalar_tensor_tensor", brow[0:1, :], mrow[0:1, :],
                        -1.0, arow[0:1, :], ALU.mult, ALU.mult,
                        waits=[tk_a, tk_m] + t["brow"].w_waits(), tag=True)
            t["mrow"].rd(tk_b); t["arow"].rd(tk_b); t["brow"].wr(tk_b)
            bca = p.op("tensor", "matmul", pv2[:, 0:256], c.sb["ones_f"][0:1, :],
                       arow[0:1, :], waits=[tk_a] + t["pv2"].w_waits(),
                       start=True, stop=True, tag=True)
            t["arow"].rd(bca); t["pv2"].wr(bca)
            bcb = p.op("tensor", "matmul", pbig[:, 1536:1792], c.sb["ones_f"][0:1, :],
                       brow[0:1, :], waits=[tk_b] + t["pb3"].w_waits(),
                       start=True, stop=True, tag=True)
            t["brow"].rd(bcb); t["pb3"].wr(bcb)
            ca = p.op("scalar", "copy", out=abc, in_=pv2[:, 0:256],
                      waits=[bca] + t["abc"].w_waits(), tag=True)
            t["pv2"].rd(ca); t["abc"].wr(ca)
            cb = p.op("scalar", "copy", out=bbc, in_=pbig[:, 1536:1792],
                      waits=[bcb] + t["bbc"].w_waits(), tag=True)
            t["pb3"].rd(cb); t["bbc"].wr(cb)
            dtks2 = []
            for mc2 in range(DC):
                par = mc2 % 2
                tk1 = p.op("vector", "tensor_mul", tmpf[par],
                           xnbf[:, 256 * mc2:256 * (mc2 + 1)], abc,
                           waits=[ca, cbf] + t["tmpf%d" % par].w_waits(), tag=True)
                t["abc"].rd(tk1); t["xnbf"].rd(tk1); t["tmpf%d" % par].wr(tk1)
                tk2 = p.op("vector", "tensor_tensor", xnch[:, 256 * mc2:256 * (mc2 + 1)],
                           tmpf[par], bbc, ALU.add,
                           waits=[tk1, cb] + (t["xnch"].w_waits() if mc2 == 0 else []),
                           tag=True)
                t["tmpf%d" % par].rd(tk2); t["bbc"].rd(tk2)
                dtk = p.dma("sync", xn2_s[s, mc2, :, cc],
                            xnch[:, 256 * mc2:256 * (mc2 + 1)], waits=[tk2])
                dtks2.append(dtk)
            t["xnch"].wr(tk2)
            for dtk in dtks2:
                t["xnch"].rd(dtk)
    c.pop()



_PHASES = {}


def _register():
    _PHASES["l0"] = emit_l0
    _PHASES["ffn0"] = emit_ffn0
    for nme in ("l1ab", "l1c", "l1ffn"):
        if nme not in _PHASES and ("emit_" + nme) in globals():
            _PHASES[nme] = globals()["emit_" + nme]


WEIGHT_TENS = {
    "wq": F32, "wk": F32, "wv": F32, "wo0": F32,
    "wg0": I32, "wu0": I32, "wd0": I32,
    "w1": F32, "w2": F32, "wz": F32, "wh": F32, "wo1": F32,
    "ws1": F32, "w1f": F32, "w2f": F32, "ws2": F32, "w1s": F32, "w2s": F32,
    "smalls": F32, "smrow": F32,
}

SCRATCH = {
    "xn0_s": ([BL, DC, 128, S], BF16, "l0", ("ffn0",)),
    "x1_s": ([BL, DC, 128, S], F32, "l0", ("ffn0",)),
    "xnew_s": ([BL, DC, 128, S], F32, "ffn0", ("l1c",)),
    "xn2_s": ([BL, DC, 128, S], BF16, "ffn0", ("l1ab", "l1ffn")),
    "xskip_s": ([32, 128, BL, S], BF16, "l1ab", ("l1c",)),
    "h_s": ([32, 128, BL, S], BF16, "l1ab", ("l1c",)),
    "acc_s": ([BL, DC, 128, S], F32, "l1c", ("l1ffn",)),
}

PHASE_USED_W = {
    "l0": ("wq", "wk", "wv", "wo0"),
    "ffn0": ("wg0", "wu0", "wd0"),
    "l1ab": ("w1", "w2", "wz", "wh"),
    "l1c": ("wo1",),
    "l1ffn": ("ws1", "w1f", "w2f", "ws2", "w1s", "w2s"),
}

ALL_PHASES = ("l0", "ffn0", "l1ab", "l1c", "l1ffn")


def build_group(group, prep, final=False):
    _register()
    nc = bass.Bass("TRN2", target_bir_lowering=False, debug=False)
    p = Prog(nc)
    gstack = ExitStack()
    c = Ctx(nc, p, gstack)
    c.sm = prep["smmap"]
    in_names, out_names = [], []

    def need_x(g):
        return "l0" in g

    with gstack:
        if need_x(group):
            c.hbm["x"] = nc.dram_tensor("h_x", [BL, DC, 128, S], F32, kind="ExternalInput").ap()
            in_names.append("x")
        for w in ("smalls", "smrow"):
            arr = prep[w]
            c.hbm[w] = nc.dram_tensor("h_" + w, list(arr.shape), F32, kind="ExternalInput").ap()
            in_names.append(w)
        for ph in group:
            for w in PHASE_USED_W[ph]:
                arr = prep[w]
                c.hbm[w] = nc.dram_tensor("h_" + w, list(arr.shape), WEIGHT_TENS[w],
                                          kind="ExternalInput").ap()
                in_names.append(w)
        for nme, (shp, dt, prod, cons) in SCRATCH.items():
            inside_p = prod in group
            used = [x for x in cons if x in group]
            later = [x for x in cons if x not in group and
                     ALL_PHASES.index(x) > max(ALL_PHASES.index(g) for g in group)]
            if inside_p and not later:
                kind = "Internal"
            elif inside_p and later:
                kind = "ExternalOutput"
            elif used:
                kind = "ExternalInput"
            else:
                continue
            c.hbm[nme] = nc.dram_tensor("h_" + nme, shp, dt, kind=kind).ap()
            if kind == "ExternalInput":
                in_names.append(nme)
            elif kind == "ExternalOutput":
                out_names.append(nme)
        if DBG_L0 and "l0" in group:
            for nme in ("dbg_q", "dbg_k", "dbg_v", "dbg_o"):
                c.hbm[nme] = nc.dram_tensor("h_" + nme, [128, 16384], BF16,
                                            kind="ExternalOutput").ap()
                out_names.append(nme)
        if final or group[-1] == "l1ffn":
            c.hbm["y"] = nc.dram_tensor("h_y", [BL, DC, 128, S], F32,
                                        kind="ExternalOutput").ap()
            out_names.append("y")
        emit_consts(c)
        for ph in group:
            _PHASES[ph](c)
        p.finish()
        p.replay()
    return nc, in_names, out_names


def _core_x(inputs, ci):
    xs = inputs["L0_attn_x"][BL * ci:BL * (ci + 1)]
    xs = np.ascontiguousarray(np.transpose(xs, (0, 2, 1)))
    return xs.reshape(BL, DC, 128, S)


GROUPS = [("l0", "ffn0", "l1ab", "l1c", "l1ffn")]


def kernel(**inputs):
    prep = prep_inputs(inputs)
    state = [{} for _ in range(NCORES)]
    for ci in range(NCORES):
        state[ci]["x"] = _core_x(inputs, ci)
    out = None
    for gi, group in enumerate(GROUPS):
        nc, in_names, out_names = build_group(group, prep)
        in_maps = []
        for ci in range(NCORES):
            m = {}
            for nme in in_names:
                if nme == "x":
                    m["h_x"] = state[ci]["x"]
                elif nme in prep:
                    m["h_" + nme] = prep[nme]
                else:
                    m["h_" + nme] = state[ci][nme]
            in_maps.append(m)
        res = run_bass_kernel_spmd(nc, in_maps, core_ids=list(range(NCORES)))
        for ci in range(NCORES):
            for nme in out_names:
                state[ci][nme] = np.asarray(res.results[ci]["h_" + nme])
    ys = []
    for ci in range(NCORES):
        y = state[ci]["y"].reshape(BL, D, S)
        ys.append(np.transpose(y, (0, 2, 1)))
    return np.ascontiguousarray(np.concatenate(ys, 0)).astype(np.float32)



def emit_l1ab(c):
    p = c.p
    t = c.trk
    xn2_s = c.hbm["xn2_s"]
    xskip_s = c.hbm["xskip_s"]
    h_s = c.hbm["h_s"]
    c.push()
    alloc_psum_pool(c)

    pbig = c.sb["pbig"]
    pv0, pv1, pv2 = c.sb["pv0"], c.sb["pv1"], c.sb["pv2"]

    x_in = c.sbuf("x_in", [128, 32 * 2048], BF16)
    for m in range(32):
        t["xin%d" % m] = Trk()
    carry = c.sbuf("carry", [128, 64], F32)
    nbz = c.sbuf("nbz", [128, 32], F32)
    tnb = p.op("vector", "tensor_scalar_mul", nbz, c.smallc("bz", 0, 32), -1.0, tag=True)
    t["nbz"].wr(tnb)

    for ch in range(2):
        tks = slice(1024 * ch, 1024 * (ch + 1))
        c.push()
        xn2ch = c.sbuf("xn2ch", [128, DC * 2048], BF16)
        w1t = c.views("w1t", 2, [128, 1024], BF16)
        w2t = c.views("w2t", 2, [128, 1024], BF16)
        xsk = c.views("xsk", 2, [128, 2048], BF16)
        ltoks = []
        for dc in range(DC):
            for sq in range(BL):
                tk = p.dma("sync", xn2ch[:, 2048 * dc + 1024 * sq:2048 * dc + 1024 * (sq + 1)],
                           xn2_s[sq, dc, :, tks],
                           waits=(t["xn2ch"].w_waits() if dc == 0 and sq == 0 else ()))
            ltoks.append(tk)
        t["xn2ch"].wr(tk)
        lt = ltoks[-1]
        for mp in range(32):
            par = mp % 2
            tw1 = p.dma("gpsimd", w1t[par], c.hbm["w1"][mp], waits=t["w1t%d" % par].w_waits(), ch=par)
            t["w1t%d" % par].wr(tw1)
            tw2 = p.dma("gpsimd", w2t[par], c.hbm["w2"][mp], waits=t["w2t%d" % par].w_waits(), ch=par)
            t["w2t%d" % par].wr(tw2)
            gmm = [None] * 4
            umm = [None] * 4
            spar = mp % 2
            for kc in range(DC):
                for tt in range(4):
                    pg = pbig[:, 512 * tt:512 * (tt + 1)]
                    trg = t["pb%d" % tt]
                    w = [lt, tw1] + (trg.w_waits() if kc == 0 else [])
                    gmm[tt] = p.op("tensor", "matmul", pg,
                                   w1t[par][:, 128 * kc:128 * (kc + 1)],
                                   xn2ch[:, 2048 * kc + 512 * tt:2048 * kc + 512 * (tt + 1)],
                                   waits=w, start=(kc == 0), stop=(kc == DC - 1),
                                   tag=(kc == DC - 1))
                    t["xn2ch"].rd(gmm[tt])
                    t["w1t%d" % par].rd(gmm[tt])
            for tt in range(4):
                pg = pbig[:, 512 * tt:512 * (tt + 1)]
                trg = t["pb%d" % tt]
                trg.wr(gmm[tt])
                ev = p.op("scalar", "activation",
                          x_in[:, 2048 * mp + 512 * tt:2048 * mp + 512 * (tt + 1)], pg,
                          AF.Silu, bias=c.smallc("b1", mp),
                          waits=[gmm[tt]] + (t["xin%d" % mp].w_waits() if tt == 0 else []),
                          tag=True)
                trg.rd(ev)
                if tt == 3:
                    t["xin%d" % mp].wr(ev)
            for tt in range(4):
                pu = pv0 if tt % 2 == 0 else pv1
                tru = t["pv0"] if tt % 2 == 0 else t["pv1"]
                mm2 = None
                for kc in range(DC):
                    w = [lt, tw2] + (tru.w_waits() if kc == 0 else [])
                    mm2 = p.op("tensor", "matmul", pu[:, :],
                               w2t[par][:, 128 * kc:128 * (kc + 1)],
                               xn2ch[:, 2048 * kc + 512 * tt:2048 * kc + 512 * (tt + 1)],
                               waits=w, start=(kc == 0), stop=(kc == DC - 1),
                               tag=(kc == DC - 1))
                    t["xn2ch"].rd(mm2)
                    t["w2t%d" % par].rd(mm2)
                tru.wr(mm2)
                ev2 = p.op("scalar", "activation", xsk[spar][:, 512 * tt:512 * (tt + 1)],
                           pu[:, :], AF.Silu, bias=c.smallc("b2", mp),
                           waits=[mm2] + (t["xsk%d" % spar].w_waits() if tt == 0 else []),
                           tag=True)
                tru.rd(ev2)
            t["xsk%d" % spar].wr(ev2)
            for sq in range(BL):
                dtk = p.dma("sync", xskip_s[mp, :, sq, tks],
                            xsk[spar][:, 1024 * sq:1024 * (sq + 1)], waits=[ev2])
                t["xsk%d" % spar].rd(dtk)
        c.pop()
        c.push()
        wzt = c.views("wzt", 2, [128, 4096], BF16)
        wht = c.views("wht", 2, [128, 4096], BF16)
        a_sb = c.sbuf("a_sb", [128, 2048], F32)
        z_sb = c.sbuf("z_sb", [128, 2048], F32)
        b_sb = c.sbuf("b_sb", [128, 2048], F32)
        h_bf = c.sbuf("h_bf", [128, 2048], BF16)
        for m in range(32):
            par = m % 2
            twz = p.dma("gpsimd", wzt[par], c.hbm["wz"][m], waits=t["wzt%d" % par].w_waits(), ch=par)
            t["wzt%d" % par].wr(twz)
            twh = p.dma("gpsimd", wht[par], c.hbm["wh"][m], waits=t["wht%d" % par].w_waits(), ch=par)
            t["wht%d" % par].wr(twh)
            atoks = []
            zmm = [None] * 4
            for kc in range(32):
                for tt in range(4):
                    pz = pbig[:, 512 * tt:512 * (tt + 1)]
                    trz = t["pb%d" % tt]
                    w = [t["xin%d" % kc].w, twz] + (trz.w_waits() if kc == 0 else [])
                    zmm[tt] = p.op("tensor", "matmul", pz,
                                   wzt[par][:, 128 * kc:128 * (kc + 1)],
                                   x_in[:, 2048 * kc + 512 * tt:2048 * kc + 512 * (tt + 1)],
                                   waits=w, start=(kc == 0), stop=(kc == 31),
                                   tag=(kc == 31))
                    t["xin%d" % kc].rd(zmm[tt])
                    t["wzt%d" % par].rd(zmm[tt])
            for tt in range(4):
                pz = pbig[:, 512 * tt:512 * (tt + 1)]
                trz = t["pb%d" % tt]
                mm = zmm[tt]
                trz.wr(mm)
                ea = p.op("scalar", "activation", a_sb[:, 512 * tt:512 * (tt + 1)], pz,
                          AF.Sigmoid, scale=c.cp("neg1"), bias=nbz[:, m:m + 1],
                          waits=[mm, tnb] + (t["a_sb"].w_waits() if tt == 0 else []),
                          tag=True)
                trz.rd(ea)
                ez = p.op("scalar", "activation", z_sb[:, 512 * tt:512 * (tt + 1)], pz,
                          AF.Sigmoid, bias=c.smallc("bz", m),
                          waits=[mm] + (t["z_sb"].w_waits() if tt == 0 else []), tag=True)
                trz.rd(ez)
                atoks.append((ea, ez))
            t["a_sb"].wr(atoks[-1][0])
            t["z_sb"].wr(atoks[-1][1])
            btoks = []
            hmm4 = [None] * 4
            for kc in range(32):
                for tt in range(4):
                    ph = pbig[:, 512 * tt:512 * (tt + 1)]
                    trh = t["pb%d" % tt]
                    w = [t["xin%d" % kc].w, twh] + (trh.w_waits() if kc == 0 else [])
                    hmm4[tt] = p.op("tensor", "matmul", ph,
                                    wht[par][:, 128 * kc:128 * (kc + 1)],
                                    x_in[:, 2048 * kc + 512 * tt:2048 * kc + 512 * (tt + 1)],
                                    waits=w, start=(kc == 0), stop=(kc == 31),
                                    tag=(kc == 31))
                    t["wht%d" % par].rd(hmm4[tt])
                    t["xin%d" % kc].rd(hmm4[tt])
            for tt in range(4):
                ph = pbig[:, 512 * tt:512 * (tt + 1)]
                trh = t["pb%d" % tt]
                mm = hmm4[tt]
                trh.wr(mm)
                eb = p.op("vector", "scalar_tensor_tensor",
                          b_sb[:, 512 * tt:512 * (tt + 1)], ph[:, :],
                          c.smallc("bh", m), z_sb[:, 512 * tt:512 * (tt + 1)],
                          ALU.add, ALU.mult,
                          waits=[mm, atoks[tt][1]] + (t["b_sb"].w_waits() if tt == 0 else []),
                          tag=True)
                trh.rd(eb)
                t["z_sb"].rd(eb)
                btoks.append(eb)
            t["b_sb"].wr(btoks[-1])
            stoks = []
            for sq in range(BL):
                cin = carry[:, 2 * m + sq:2 * m + sq + 1]
                init = 0.0 if ch == 0 else cin
                w = [atoks[-1][0], btoks[-1]] + ([t["carry"].w] if ch > 0 else [])
                w += t["h_bf"].w_waits() if sq == 0 else []
                sk = p.op("vector", "tensor_tensor_scan",
                          h_bf[:, 1024 * sq:1024 * (sq + 1)],
                          a_sb[:, 1024 * sq:1024 * (sq + 1)],
                          b_sb[:, 1024 * sq:1024 * (sq + 1)],
                          init, ALU.mult, ALU.add, waits=w, tag=True)
                t["a_sb"].rd(sk)
                t["b_sb"].rd(sk)
                if ch > 0:
                    t["carry"].rd(sk)
                stoks.append(sk)
            t["h_bf"].wr(stoks[-1])
            if ch == 0:
                for sq in range(BL):
                    ck = p.op("vector", "tensor_copy",
                              carry[:, 2 * m + sq:2 * m + sq + 1],
                              h_bf[:, 1024 * (sq + 1) - 1:1024 * (sq + 1)],
                              waits=[stoks[sq]], tag=True)
                    t["h_bf"].rd(ck)
                    t["carry"].wr(ck)
            for sq in range(BL):
                dtk = p.dma("sync", h_s[m, :, sq, tks],
                            h_bf[:, 1024 * sq:1024 * (sq + 1)], waits=[stoks[-1]])
                t["h_bf"].rd(dtk)
        c.pop()
    c.pop()



def emit_l1c(c):
    p = c.p
    t = c.trk
    xskip_s = c.hbm["xskip_s"]
    h_s = c.hbm["h_s"]
    xnew_s = c.hbm["xnew_s"]
    acc_s = c.hbm["acc_s"]
    c.push()
    alloc_psum_pool(c)
    pv0, pv1, pv2 = c.sb["pv0"], c.sb["pv1"], c.sb["pv2"]

    wo1t = c.sbuf("wo1t", [128, DC * 4096], BF16)
    for mc in range(DC):
        tok = p.dma("gpsimd", wo1t[:, 4096 * mc:4096 * (mc + 1)], c.hbm["wo1"][mc])
    t["wo1t"].wr(tok)
    wo_tok = tok
    ht = c.sbuf("ht", [128, 32 * 512], BF16)
    xst = c.sbuf("xst", [128, 32 * 512], BF16)
    xnt = c.views("xnt", 2, [128, 512], F32)
    tmpe = c.views("tmpe", 2, [128, 512], F32)

    for g8 in range(8):
        sq, tk0 = g8 // 4, 512 * (g8 % 4)
        tcol = slice(tk0, tk0 + 512)
        htoks = []
        for m in range(32):
            tk1 = p.dma("sync", ht[:, 512 * m:512 * (m + 1)], h_s[m, :, sq, tcol],
                        waits=(t["ht"].w_waits() if m == 0 else ()))
            tk2 = p.dma("sync", xst[:, 512 * m:512 * (m + 1)], xskip_s[m, :, sq, tcol],
                        waits=(t["xst"].w_waits() if m == 0 else ()))
            htoks.append((tk1, tk2))
        t["xst"].wr(tk2)
        stoks = []
        for m in range(32):
            sk = p.op("vector", "tensor_tensor", ht[:, 512 * m:512 * (m + 1)],
                      ht[:, 512 * m:512 * (m + 1)], xst[:, 512 * m:512 * (m + 1)],
                      ALU.add, waits=[htoks[m][0], htoks[m][1]] +
                      (t["ht"].w_waits() if m == 0 else []), tag=True)
            t["xst"].rd(sk)
            stoks.append(sk)
        t["ht"].wr(stoks[-1])
        for mc in range(DC):
            po, tro = ((pv0, t["pv0"]), (pv1, t["pv1"]), (pv2, t["pv2"]))[mc % 3]
            mm = None
            for kc in range(32):
                w = [stoks[kc], wo_tok] + (tro.w_waits() if kc == 0 else [])
                mm = p.op("tensor", "matmul", po[:, :],
                          wo1t[:, 4096 * mc + 128 * kc:4096 * mc + 128 * (kc + 1)],
                          ht[:, 512 * kc:512 * (kc + 1)],
                          waits=w, start=(kc == 0), stop=(kc == 31), tag=(kc == 31))
                t["ht"].rd(mm)
                t["wo1t"].rd(mm)
            tro.wr(mm)
            par = mc % 2
            te = p.op("scalar", "activation", tmpe[par], po[:, :], AF.Identity,
                      bias=c.smallc("bo1", mc),
                      waits=[mm] + t["tmpe%d" % par].w_waits(), tag=True)
            tro.rd(te)
            t["tmpe%d" % par].wr(te)
            xl = p.dma("sync", xnt[par], xnew_s[sq, mc, :, tcol],
                       waits=t["xnt%d" % par].w_waits(), ch=par)
            t["xnt%d" % par].wr(xl)
            ta = p.op("vector", "tensor_tensor", tmpe[par], tmpe[par], xnt[par],
                      ALU.add, waits=[te, xl], tag=True)
            t["xnt%d" % par].rd(ta)
            t["tmpe%d" % par].wr(ta)
            dtk = p.dma("sync", acc_s[sq, mc, :, tcol], tmpe[par], waits=[ta])
            t["tmpe%d" % par].rd(dtk)
    c.pop()



def emit_l1ffn(c):
    p = c.p
    t = c.trk
    xn2_s = c.hbm["xn2_s"]
    acc_s = c.hbm["acc_s"]
    y = c.hbm["y"]
    c.push()
    alloc_psum_pool(c)
    pbig = c.sb["pbig"]
    pv0, pv1, pv2 = c.sb["pv0"], c.sb["pv1"], c.sb["pv2"]
    pt = c.sb["pt"]
    ident = c.sb["ident_b"]
    ones = c.sb["ones_f"]

    def pb(i):
        return pbig[:, 512 * i:512 * (i + 1)]

    cmb = c.sbuf("cmb", [64, 1024], BF16)
    seqf = c.sbuf("seqf", [128, DC * 1024], BF16)
    egbc = c.sbuf("egbc", [128, E1], F32)
    ggc = c.sbuf("ggc", [128, DC], F32)
    b2g = c.sbuf("b2g", [128, 64], F32)
    rowp = c.sbuf("rowp", [1, 1024], F32)
    c.sbuf("rowtmp", [1, 1024], F32)
    tmpq = c.views("tmpq", 2, [128, 1024], F32)

    bm = p.op("tensor", "matmul", pv0[:, 0:E1], ones[0:1, :],
              c.sb["smrow"][0:1, D:D + E1], waits=t["pv0"].w_waits(),
              start=True, stop=True, tag=True)
    t["pv0"].wr(bm)
    te = p.op("scalar", "copy", out=egbc, in_=pv0[:, 0:E1], waits=[bm], tag=True)
    t["pv0"].rd(te); t["egbc"].wr(te)
    tg = p.op("vector", "tensor_mul", ggc, c.smallc("gs1", 0, DC), c.smallc("g1", 0, DC),
              waits=[], tag=True)
    t["ggc"].wr(tg)
    tb2 = None
    for e in range(E1):
        tb2 = p.op("vector", "tensor_scalar", b2g[:, 8 * e:8 * (e + 1)],
                   c.smallc("b2f", 8 * e, 8), egbc[:, e:e + 1], None, ALU.mult,
                   waits=[te] + (t["b2g"].w_waits() if e == 0 else []), tag=True)
        t["egbc"].rd(tb2)
    t["b2g"].wr(tb2)

    def rms_row(out_row, ps_row, waits, n):
        tmp = c.sb["rowtmp"][0:1, 0:n]
        tk = p.op("scalar", "activation", tmp, ps_row, AF.Sqrt,
                  waits=list(waits) + c.trk["rowtmp"].w_waits(), tag=True)
        c.trk["rowtmp"].wr(tk)
        tk = p.op("vector", "tensor_scalar_max", tmp, tmp, 1e-8, waits=[tk], tag=True)
        c.trk["rowtmp"].wr(tk)
        tk = p.op("vector", "reciprocal", out_row, tmp, waits=[tk], tag=True)
        c.trk["rowtmp"].rd(tk)
        tk = p.op("vector", "tensor_scalar_mul", out_row, out_row, 32.0, waits=[tk], tag=True)
        return tk

    c.push()
    xn2f = c.sbuf("xn2f", [128, DC * 4096], BF16)
    seT = c.sbuf("seT", [128, DC * 1024], BF16)
    gm = c.sbuf("gm", [128, DC * 64], BF16)
    slots = c.sbuf("slots", [128, DC * 1024], BF16)
    h1 = c.sbuf("h1", [128, 32 * 128], BF16)
    A1c = c.sbuf("A1c", [128, 32], F32)
    gates_bc = c.sbuf("gates_bc", [128, 1024], F32)
    grow = c.sbuf("grow", [1, 1024], F32)
    xgt = c.sbuf("xgt", [64, 1024], BF16)
    wst = c.views("wst", 2, [128, 4096], BF16)
    wst2 = c.views("wst2", 2, [128, 4096], BF16)

    ltk = None
    for dc in range(DC):
        for sq in range(BL):
            ltk = p.dma("sync", xn2f[:, 4096 * dc + 2048 * sq:4096 * dc + 2048 * (sq + 1)],
                        xn2_s[sq, dc, :, :],
                        waits=(t["xn2f"].w_waits() if dc == 0 and sq == 0 else ()))
    t["xn2f"].wr(ltk)

    for chk in range(8):
        cc = slice(512 * chk, 512 * (chk + 1))
        mm = None
        for dc in range(DC):
            par = dc % 2
            sk = p.op("scalar", "activation", tmpq[par][:, 0:512],
                      xn2f[:, 4096 * dc + cc.start:4096 * dc + cc.stop], AF.Square,
                      waits=[ltk] + t["tmpq%d" % par].w_waits(), tag=True)
            t["tmpq%d" % par].wr(sk); t["xn2f"].rd(sk)
            mm = p.op("tensor", "matmul", pv2[0:1, :], ones[:, 0:1], tmpq[par][:, 0:512],
                      waits=[sk] + (t["pv2"].w_waits() if dc == 0 else []),
                      start=(dc == 0), stop=(dc == DC - 1), tag=True)
            t["tmpq%d" % par].rd(mm)
        t["pv2"].wr(mm)
        a1r = rowp[0:1, 0:512]
        tk = rms_row(a1r, pv2[0:1, :], [mm], 512)
        t["pv2"].rd(tk); t["rowp"].wr(tk)
        cmms = []
        for j in range(4):
            w = [tk] + (t["pv1"].w_waits() if j == 0 else [])
            cm = p.op("tensor", "matmul", pv1[:, j:j + 1], a1r[0:1, 128 * j:128 * (j + 1)],
                      ones[0:1, 0:1], waits=w, start=True, stop=True, tag=True)
            t["rowp"].rd(cm)
            cmms.append(cm)
        t["pv1"].wr(cmms[-1])
        ce = p.op("vector", "tensor_copy", A1c[:, 4 * chk:4 * (chk + 1)], pv1[:, 0:4],
                  waits=cmms + (t["A1c"].w_waits() if chk == 0 else []), tag=True)
        t["pv1"].rd(ce)
    t["A1c"].wr(ce)

    gred = c.sbuf("gred", [128, 1536], F32)
    gtoks = []
    for dc in range(DC):
        for sq in range(BL):
            src = xn2f[:, 4096 * dc + 2048 * sq:4096 * dc + 2048 * (sq + 1)]
            tk = p.op("vector", "tensor_tensor", gred[:, 0:1024],
                      src.rearrange("p (a two) -> p a two", two=2)[:, :, 0],
                      src.rearrange("p (a two) -> p a two", two=2)[:, :, 1], ALU.add,
                      waits=[ltk] + t["gred"].w_waits(), tag=True)
            t["xn2f"].rd(tk)
            t["gred"].wr(tk)
            off, wdt = 0, 1024
            for lv in range(5):
                nw = wdt // 2
                dsto = 1024 if lv % 2 == 0 else 0
                src2 = gred[:, off:off + wdt].rearrange("p (a two) -> p a two", two=2)
                if lv < 4:
                    tk = p.op("vector", "tensor_tensor", gred[:, dsto:dsto + nw],
                              src2[:, :, 0], src2[:, :, 1], ALU.add, waits=[tk], tag=True)
                    t["gred"].wr(tk)
                else:
                    tk = p.op("vector", "scalar_tensor_tensor",
                              gm[:, 64 * dc + 32 * sq:64 * dc + 32 * (sq + 1)],
                              src2[:, :, 0], 1.0, src2[:, :, 1], ALU.bypass, ALU.add,
                              waits=[tk] + (t["gm"].w_waits() if dc == 0 and sq == 0 else []),
                              tag=True)
                    t["gred"].rd(tk)
                off, wdt = dsto, nw
            gtoks.append(tk)
    gsc = p.op("vector", "tensor_scalar_mul", gm, gm, 1.0 / 64.0,
               waits=gtoks + t["gm"].w_waits(), tag=True)
    t["gm"].wr(gsc)

    setoks = []
    for es in range(M1):
        for mcp in range(DC):
            par = ((es * 2) + mcp // 4) % 2
            if mcp % 4 == 0:
                tw = p.dma("gpsimd", wst[par], c.hbm["ws1"][es][mcp:mcp + 4],
                           waits=t["wst%d" % par].w_waits(), ch=par)
                t["wst%d" % par].wr(tw)
            tw = t["wst%d" % par].w
            pz = pb(mcp % 2)
            trz = t["pb%d" % (mcp % 2)]
            mm = None
            for kc in range(DC):
                w = [gsc, tw] + (trz.w_waits() if kc == 0 else [])
                mm = p.op("tensor", "matmul", pz[:, 0:64],
                          wst[par][:, 1024 * (mcp % 4) + 128 * kc:1024 * (mcp % 4) + 128 * (kc + 1)],
                          gm[:, 64 * kc:64 * (kc + 1)],
                          waits=w, start=(kc == 0), stop=(kc == DC - 1), tag=(kc == DC - 1))
                t["gm"].rd(mm)
                t["wst%d" % par].rd(mm)
            trz.wr(mm)
            dst = seT[:, 1024 * mcp:1024 * (mcp + 1)].rearrange(
                "q (b e) -> q b e", e=M1)[:, :, es]
            ev = p.op("vector", "tensor_copy", dst, pz[:, 0:64],
                      waits=[mm] + (t["seT"].w_waits() if es == 0 and mcp == 0 else []),
                      tag=True)
            trz.rd(ev)
            setoks.append(ev)
    t["seT"].wr(setoks[-1])

    for hf in range(2):
        mm = None
        for mcp in range(DC):
            par = mcp % 2
            sk = p.op("scalar", "activation", tmpq[par][:, 0:512],
                      seT[:, 1024 * mcp + 512 * hf:1024 * mcp + 512 * (hf + 1)], AF.Square,
                      waits=setoks[-1:] + t["tmpq%d" % par].w_waits(), tag=True)
            t["seT"].rd(sk)
            t["tmpq%d" % par].wr(sk)
            mm = p.op("tensor", "matmul", pv2[0:1, :], ones[:, 0:1], tmpq[par][:, 0:512],
                      waits=[sk] + (t["pv2"].w_waits() if mcp == 0 else []),
                      start=(mcp == 0), stop=(mcp == DC - 1), tag=True)
            t["tmpq%d" % par].rd(mm)
        t["pv2"].wr(mm)
        ser = rowp[0:1, 0:512]
        tk = rms_row(ser, pv2[0:1, :], [mm], 512)
        t["pv2"].rd(tk); t["rowp"].wr(tk)
        bc = p.op("tensor", "matmul", pv1[:, :], ones[0:1, :], ser,
                  waits=[tk] + t["pv1"].w_waits(), start=True, stop=True, tag=True)
        t["rowp"].rd(bc); t["pv1"].wr(bc)
        cbc = p.op("scalar", "copy", out=tmpq[0][:, 0:512], in_=pv1[:, :],
                   waits=[bc] + t["tmpq0"].w_waits(), tag=True)
        t["pv1"].rd(cbc); t["tmpq0"].wr(cbc)
        for mcp in range(DC):
            sl = seT[:, 1024 * mcp + 512 * hf:1024 * mcp + 512 * (hf + 1)]
            t1 = p.op("vector", "tensor_tensor", tmpq[1][:, 0:512], sl, tmpq[0][:, 0:512], ALU.mult,
                      waits=[cbc] + t["tmpq1"].w_waits() + t["seT"].r_waits(), tag=True)
            t["tmpq1"].wr(t1); t["tmpq0"].rd(t1)
            t2 = p.op("vector", "tensor_scalar", sl, tmpq[1][:, 0:512], ggc[:, mcp:mcp + 1], None,
                      ALU.mult, waits=[t1, tg] + t["seT"].w_waits(), tag=True)
            t["tmpq1"].rd(t2); t["ggc"].rd(t2)
            t["seT"].wr(t2)
    se_tok = t["seT"].w

    lrow = c.sbuf("lmax", [64, 1], F32)
    lsum = c.sbuf("lsum", [64, 1], F32)
    lrcp = c.sbuf("lrcp", [64, 1], F32)
    lgs = c.sbuf("lgs", [64, 16], F32)
    ctoks = []
    for bg in range(64):
        sq, g = bg // 32, bg % 32
        gc0 = 4096 * 0 + 2048 * sq + 64 * g
        mm = None
        for kc in range(DC):
            w = [ltk, se_tok] + (t["pv0"].w_waits() if kc == 0 else [])
            mm = p.op("tensor", "matmul", pv0[0:64, 0:16],
                      xn2f[:, 4096 * kc + 2048 * sq + 64 * g:4096 * kc + 2048 * sq + 64 * (g + 1)],
                      seT[:, 1024 * kc + 16 * bg:1024 * kc + 16 * (bg + 1)],
                      waits=w, start=(kc == 0), stop=(kc == DC - 1), tag=(kc == DC - 1))
            t["xn2f"].rd(mm)
            t["seT"].rd(mm)
        t["pv0"].wr(mm)
        a1s = A1c[64 * (g % 2):64 * (g % 2) + 64,
                  (2048 * sq + 64 * g) // 128:(2048 * sq + 64 * g) // 128 + 1]
        tk = p.op("vector", "tensor_scalar", lgs, pv0[0:64, 0:16], a1s, None, ALU.mult,
                  waits=[mm, ce] + t["lgs"].w_waits(), tag=True)
        t["pv0"].rd(tk); t["A1c"].rd(tk); t["lgs"].wr(tk)
        tmx = p.op("vector", "tensor_reduce", lrow, lgs, AX.X, ALU.max, negate=True,
                   waits=[tk] + t["lmax"].w_waits(), tag=True)
        t["lmax"].wr(tmx)
        tex = p.op("scalar", "activation", lgs, lgs, AF.Exp, bias=lrow,
                   accum_out=lsum, waits=[tmx] + t["lsum"].w_waits(), tag=True)
        t["lmax"].rd(tex); t["lsum"].wr(tex); t["lgs"].wr(tex)
        trc = p.op("vector", "reciprocal", lrcp, lsum, waits=[tex] + t["lrcp"].w_waits(),
                   tag=True)
        t["lsum"].rd(trc); t["lrcp"].wr(trc)
        tcb = p.op("vector", "tensor_scalar", cmb[:, 16 * bg:16 * (bg + 1)], lgs, lrcp,
                   None, ALU.mult, waits=[trc, tex] +
                   (t["cmb"].w_waits() if bg == 0 else []), tag=True)
        t["lgs"].rd(tcb); t["lrcp"].rd(tcb)
        ctoks.append(tcb)
    t["cmb"].wr(ctoks[-1])

    sltoks = []
    for bg in range(64):
        sq, g = bg // 32, bg % 32
        ttks = []
        for dc in range(DC):
            w = [ltk] + (t["pt"].w_waits() if dc == 0 else [])
            tk = p.op("tensor", "transpose", pt[0:64, 128 * dc:128 * (dc + 1)],
                      xn2f[:, 4096 * dc + 2048 * sq + 64 * g:4096 * dc + 2048 * sq + 64 * (g + 1)],
                      ident, waits=w, tag=(dc == DC - 1))
            t["xn2f"].rd(tk if tk else ltk)
        t["pt"].wr(tk)
        cg = p.op("vector", "tensor_copy", xgt, pt[0:64, :],
                  waits=[tk] + t["xgt"].w_waits(), tag=True)
        t["pt"].rd(cg); t["xgt"].wr(cg)
        for dc in range(DC):
            w = [cg, ctoks[bg]] + t["pv1"].w_waits()
            mm = p.op("tensor", "matmul", pv1[0:128, 0:16],
                      xgt[:, 128 * dc:128 * (dc + 1)], cmb[:, 16 * bg:16 * (bg + 1)],
                      waits=w, start=True, stop=True, tag=True)
            t["xgt"].rd(mm); t["cmb"].rd(mm); t["pv1"].wr(mm)
            ev = p.op("vector", "tensor_copy",
                      slots[:, 1024 * dc + 16 * bg:1024 * dc + 16 * (bg + 1)], pv1[:, 0:16],
                      waits=[mm] + (t["slots"].w_waits() if bg == 0 and dc == 0 else []),
                      tag=True)
            t["pv1"].rd(ev)
            sltoks.append(ev)
    t["slots"].wr(sltoks[-1])

    wgb = c.sbuf("wgb2", [128, DC], BF16)
    twg = p.op("vector", "tensor_copy", wgb, c.smallc("wg", 0, DC), tag=True)
    t["wgb2"].wr(twg)
    mm = None
    for kc in range(DC):
        w = [gsc, twg] + (t["pv0"].w_waits() if kc == 0 else [])
        mm = p.op("tensor", "matmul", pv0[0:1, 0:64], wgb[:, kc:kc + 1],
                  gm[:, 64 * kc:64 * (kc + 1)], waits=w,
                  start=(kc == 0), stop=(kc == DC - 1), tag=(kc == DC - 1))
        t["gm"].rd(mm); t["wgb2"].rd(mm)
    t["pv0"].wr(mm)
    gr = grow[0:1, 0:64]
    tk = p.op("scalar", "activation", gr, pv0[0:1, 0:64], AF.Sigmoid,
              bias=c.sb["smrow"][0:1, D + E1:D + E1 + 1],
              waits=[mm] + t["grow"].w_waits(), tag=True)
    t["pv0"].rd(tk)
    tk = p.op("vector", "tensor_scalar_max", gr, gr, 0.01, waits=[tk], tag=True)
    g16 = grow[0:1, 0:1024].rearrange("p (b e) -> p b e", e=16)
    ek0 = p.op("vector", "tensor_copy", tmpq[0][0:1, 0:64], gr, waits=[tk], tag=True)
    t["tmpq0"].wr(ek0)
    etoks2 = []
    for j in range(M1):
        ek = p.op("vector", "tensor_copy", g16[:, :, j], tmpq[0][0:1, 0:64],
                  waits=[ek0] + ([] if j else []), tag=True)
        etoks2.append(ek)
    t["grow"].wr(etoks2[-1])
    t["tmpq0"].rd(etoks2[-1])
    bc1 = p.op("tensor", "matmul", pv0[:, :], ones[0:1, :], grow[0:1, 0:512],
               waits=[etoks2[-1]] + t["pv0"].w_waits(), start=True, stop=True, tag=True)
    t["grow"].rd(bc1); t["pv0"].wr(bc1)
    bc2 = p.op("tensor", "matmul", pv1[:, :], ones[0:1, :], grow[0:1, 512:1024],
               waits=[etoks2[-1]] + t["pv1"].w_waits(), start=True, stop=True, tag=True)
    t["grow"].rd(bc2); t["pv1"].wr(bc2)
    c1 = p.op("scalar", "copy", out=gates_bc[:, 0:512], in_=pv0[:, :], waits=[bc1], tag=True)
    t["pv0"].rd(c1)
    c2 = p.op("scalar", "copy", out=gates_bc[:, 512:1024], in_=pv1[:, :], waits=[bc2], tag=True)
    t["pv1"].rd(c2)
    t["gates_bc"].wr(c2)
    gb_tok = c2

    sq1toks = []
    for e in range(E1):
        h1toks = []
        for mc in range(32):
            par = (mc // 4) % 2
            if mc % 4 == 0:
                tw = p.dma("gpsimd", wst[par], c.hbm["w1f"][e][mc:mc + 4],
                           waits=t["wst%d" % par].w_waits(), ch=par)
                t["wst%d" % par].wr(tw)
            tw = t["wst%d" % par].w
            pz = pb(mc % 2)
            trz = t["pb%d" % (mc % 2)]
            mm = None
            for kc in range(DC):
                w = [sltoks[-1], tw] + (trz.w_waits() if kc == 0 else [])
                rhs = slots[:, 1024 * kc:1024 * (kc + 1)].rearrange(
                    "q (b e2 s) -> q b e2 s", e2=E1, s=S1)[:, :, e, :]
                mm = p.op("tensor", "matmul", pz[:, 0:128],
                          wst[par][:, 1024 * (mc % 4) + 128 * kc:1024 * (mc % 4) + 128 * (kc + 1)],
                          rhs,
                          waits=w, start=(kc == 0), stop=(kc == DC - 1), tag=(kc == DC - 1))
                t["slots"].rd(mm)
                t["wst%d" % par].rd(mm)
            trz.wr(mm)
            ev = p.op("scalar", "activation", h1[:, 128 * mc:128 * (mc + 1)], pz[:, 0:128],
                      AF.Gelu, bias=c.smallc("b1f", 32 * e + mc),
                      waits=[mm] + (t["h1"].w_waits() if mc == 0 else []), tag=True)
            trz.rd(ev)
            h1toks.append(ev)
        t["h1"].wr(h1toks[-1])
        for dc in range(DC):
            par = dc % 2
            tw = p.dma("gpsimd", wst2[par], c.hbm["w2f"][e][dc],
                       waits=t["wst2%d" % par].w_waits(), ch=par)
            t["wst2%d" % par].wr(tw)
            po = (pv0, pv1)[dc % 2]
            tro = t["pv%d" % (dc % 2)]
            mm = None
            for kc in range(32):
                w = [h1toks[kc], tw] + (tro.w_waits() if kc == 0 else [])
                mm = p.op("tensor", "matmul", po[:, 0:128],
                          wst2[par][:, 128 * kc:128 * (kc + 1)],
                          h1[:, 128 * kc:128 * (kc + 1)],
                          waits=w, start=(kc == 0), stop=(kc == 31), tag=(kc == 31))
                t["h1"].rd(mm)
                t["wst2%d" % par].rd(mm)
            tro.wr(mm)
            te1 = p.op("scalar", "activation", tmpq[par][:, 0:128], po[:, 0:128],
                       AF.Identity, scale=egbc[:, e:e + 1], bias=b2g[:, 8 * e + dc:8 * e + dc + 1],
                       waits=[mm, tb2] + t["tmpq%d" % par].w_waits(), tag=True)
            tro.rd(te1); t["egbc"].rd(te1); t["b2g"].rd(te1)
            t["tmpq%d" % par].wr(te1)
            dst = seqf[:, 1024 * dc:1024 * (dc + 1)].rearrange(
                "q (b e2 s) -> q b e2 s", e2=E1, s=S1)[:, :, e, :]
            gsl = gates_bc.rearrange("q (b e2 s) -> q b e2 s", e2=E1, s=S1)[:, :, e, :]
            ev = p.op("vector", "tensor_tensor", dst, tmpq[par][:, 0:128], gsl, ALU.mult,
                      waits=[te1, gb_tok] + (t["seqf"].w_waits() if e == 0 and dc == 0 else []),
                      tag=True)
            t["tmpq%d" % par].rd(ev)
            t["gates_bc"].rd(ev)
            sq1toks.append(ev)
    t["seqf"].wr(sq1toks[-1])
    c.pop()
    seq_tok = t["seqf"].w
    cmb_tok = t["cmb"].w

    c.push()
    comb2 = c.sbuf("comb2", [128, 8 * 512], BF16)
    sl2 = c.sbuf("sl2", [128, DC * 1024], BF16)
    out2 = c.sbuf("out2", [128, DC * 1024], BF16)
    o2f = c.sbuf("o2f", [128, DC * 1024], BF16)
    wstB = c.views("wstB", 2, [128, 4096], BF16)
    wstB2 = c.views("wstB2", 2, [128, 4096], BF16)

    c.push()
    xs = c.sbuf("xs", [128, DC * 1024], BF16)
    cme = c.sbuf("cme", [128, DC * 256], BF16)
    se2T = c.sbuf("se2T", [128, DC * 1024], BF16)
    L2 = c.sbuf("L2", [128, 8 * 512], BF16)
    L2T = c.sbuf("L2T", [128, 8 * 512], BF16)
    disp = c.sbuf("disp", [128, 8 * 512], BF16)
    xstm = c.sbuf("xstm", [128, 8 * 1024], BF16)

    a2toks = []
    for hf in range(2):
        mm = None
        for dc in range(DC):
            par = dc % 2
            sk = p.op("scalar", "activation", tmpq[par][:, 0:512],
                      seqf[:, 1024 * dc + 512 * hf:1024 * dc + 512 * (hf + 1)], AF.Square,
                      waits=[seq_tok] + t["tmpq%d" % par].w_waits(), tag=True)
            t["seqf"].rd(sk)
            t["tmpq%d" % par].wr(sk)
            mm = p.op("tensor", "matmul", pv2[0:1, :], ones[:, 0:1], tmpq[par][:, 0:512],
                      waits=[sk] + (t["pv2"].w_waits() if dc == 0 else []),
                      start=(dc == 0), stop=(dc == DC - 1), tag=True)
            t["tmpq%d" % par].rd(mm)
        t["pv2"].wr(mm)
        a2r = rowp[0:1, 0:512]
        tk = rms_row(a2r, pv2[0:1, :], [mm], 512)
        t["pv2"].rd(tk); t["rowp"].wr(tk)
        bc = p.op("tensor", "matmul", pv1[:, :], ones[0:1, :], a2r,
                  waits=[tk] + t["pv1"].w_waits(), start=True, stop=True, tag=True)
        t["rowp"].rd(bc); t["pv1"].wr(bc)
        cbc = p.op("scalar", "copy", out=tmpq[0][:, 0:512], in_=pv1[:, :],
                   waits=[bc] + t["tmpq0"].w_waits(), tag=True)
        t["pv1"].rd(cbc); t["tmpq0"].wr(cbc)
        for dc in range(DC):
            sl = slice(1024 * dc + 512 * hf, 1024 * dc + 512 * (hf + 1))
            t1 = p.op("vector", "tensor_tensor", tmpq[1][:, 0:512], seqf[:, sl],
                      tmpq[0][:, 0:512], ALU.mult,
                      waits=[cbc, seq_tok] + t["tmpq1"].w_waits(), tag=True)
            t["seqf"].rd(t1)
            t["tmpq1"].wr(t1); t["tmpq0"].rd(t1)
            t2 = p.op("vector", "tensor_scalar", xs[:, sl], tmpq[1][:, 0:512],
                      c.smallc("g2", dc), None, ALU.mult,
                      waits=[t1] + (t["xs"].w_waits() if hf == 0 and dc == 0 else []),
                      tag=True)
            t["tmpq1"].rd(t2)
            a2toks.append(t2)
    t["xs"].wr(a2toks[-1])
    xs_tok = a2toks[-1]

    cmtoks = []
    for dc in range(DC):
        par = dc % 2
        src = xs[:, 1024 * dc:1024 * (dc + 1)].rearrange("q (a two) -> q a two", two=2)
        tk = p.op("vector", "tensor_tensor", tmpq[par][:, 0:512], src[:, :, 0],
                  src[:, :, 1], ALU.add, waits=[xs_tok] + t["tmpq%d" % par].w_waits(),
                  tag=True)
        t["xs"].rd(tk)
        t["tmpq%d" % par].wr(tk)
        s2 = tmpq[par][:, 0:512].rearrange("q (a two) -> q a two", two=2)
        tk = p.op("vector", "scalar_tensor_tensor", cme[:, 256 * dc:256 * (dc + 1)],
                  s2[:, :, 0], 1.0, s2[:, :, 1], ALU.bypass, ALU.add,
                  waits=[tk] + (t["cme"].w_waits() if dc == 0 else []), tag=True)
        t["tmpq%d" % par].rd(tk)
        cmtoks.append(tk)
    csc = p.op("vector", "tensor_scalar_mul", cme, cme, 0.25, waits=cmtoks, tag=True)
    t["cme"].wr(csc)

    s2toks = []
    for e2 in range(E2):
        for mcp in range(DC):
            par = ((e2 * 2) + mcp // 4) % 2
            if mcp % 4 == 0:
                tw = p.dma("gpsimd", wstB[par], c.hbm["ws2"][e2][mcp:mcp + 4],
                           waits=t["wstB%d" % par].w_waits(), ch=par)
                t["wstB%d" % par].wr(tw)
            tw = t["wstB%d" % par].w
            pz = pb(mcp % 2)
            trz = t["pb%d" % (mcp % 2)]
            mm = None
            for kc in range(DC):
                w = [csc, tw] + (trz.w_waits() if kc == 0 else [])
                mm = p.op("tensor", "matmul", pz[:, 0:256],
                          wstB[par][:, 1024 * (mcp % 4) + 128 * kc:1024 * (mcp % 4) + 128 * (kc + 1)],
                          cme[:, 256 * kc:256 * (kc + 1)],
                          waits=w, start=(kc == 0), stop=(kc == DC - 1), tag=(kc == DC - 1))
                t["cme"].rd(mm)
                t["wstB%d" % par].rd(mm)
            trz.wr(mm)
            evs = []
            for b in range(BL):
                ev = p.op("vector", "tensor_copy",
                          se2T[:, 1024 * mcp + 512 * b + 128 * e2:
                               1024 * mcp + 512 * b + 128 * (e2 + 1)],
                          pz[:, 128 * b:128 * (b + 1)],
                          waits=[mm] + (t["se2T"].w_waits() if e2 == 0 and mcp == 0 and b == 0 else []),
                          tag=True)
                trz.rd(ev)
                evs.append(ev)
            s2toks += evs
    t["se2T"].wr(s2toks[-1])
    for hf in range(2):
        mm = None
        for mcp in range(DC):
            par = mcp % 2
            sk = p.op("scalar", "activation", tmpq[par][:, 0:512],
                      se2T[:, 1024 * mcp + 512 * hf:1024 * mcp + 512 * (hf + 1)], AF.Square,
                      waits=[s2toks[-1]] + t["tmpq%d" % par].w_waits(), tag=True)
            t["se2T"].rd(sk)
            t["tmpq%d" % par].wr(sk)
            mm = p.op("tensor", "matmul", pv2[0:1, :], ones[:, 0:1], tmpq[par][:, 0:512],
                      waits=[sk] + (t["pv2"].w_waits() if mcp == 0 else []),
                      start=(mcp == 0), stop=(mcp == DC - 1), tag=True)
            t["tmpq%d" % par].rd(mm)
        t["pv2"].wr(mm)
        ser = rowp[0:1, 0:512]
        tk = rms_row(ser, pv2[0:1, :], [mm], 512)
        t["pv2"].rd(tk); t["rowp"].wr(tk)
        bc = p.op("tensor", "matmul", pv1[:, :], ones[0:1, :], ser,
                  waits=[tk] + t["pv1"].w_waits(), start=True, stop=True, tag=True)
        t["rowp"].rd(bc); t["pv1"].wr(bc)
        cbc = p.op("scalar", "copy", out=tmpq[0][:, 0:512], in_=pv1[:, :],
                   waits=[bc] + t["tmpq0"].w_waits(), tag=True)
        t["pv1"].rd(cbc); t["tmpq0"].wr(cbc)
        for mcp in range(DC):
            sl = se2T[:, 1024 * mcp + 512 * hf:1024 * mcp + 512 * (hf + 1)]
            t1 = p.op("vector", "tensor_tensor", tmpq[1][:, 0:512], sl,
                      tmpq[0][:, 0:512], ALU.mult,
                      waits=[cbc] + t["tmpq1"].w_waits() + t["se2T"].r_waits(), tag=True)
            t["tmpq1"].wr(t1); t["tmpq0"].rd(t1)
            t2 = p.op("vector", "tensor_scalar", sl, tmpq[1][:, 0:512],
                      c.smallc("gs2", mcp), None, ALU.mult,
                      waits=[t1] + t["se2T"].w_waits(), tag=True)
            t["tmpq1"].rd(t2)
            t["se2T"].wr(t2)
    se2_tok = t["se2T"].w

    l2toks = []
    for b in range(BL):
        for pc in range(4):
            mm = None
            for kc in range(DC):
                w = [xs_tok, se2_tok] + (t["pv0"].w_waits() if kc == 0 else [])
                mm = p.op("tensor", "matmul", pv0[:, :],
                          xs[:, 1024 * kc + 512 * b + 128 * pc:1024 * kc + 512 * b + 128 * (pc + 1)],
                          se2T[:, 1024 * kc + 512 * b:1024 * kc + 512 * (b + 1)],
                          waits=w, start=(kc == 0), stop=(kc == DC - 1), tag=(kc == DC - 1))
                t["xs"].rd(mm)
                t["se2T"].rd(mm)
            t["pv0"].wr(mm)
            i = b * 4 + pc
            ev = p.op("vector", "tensor_copy", L2[:, 512 * i:512 * (i + 1)], pv0[:, :],
                      waits=[mm] + (t["L2"].w_waits() if i == 0 else []), tag=True)
            t["pv0"].rd(ev)
            l2toks.append(ev)
    t["L2"].wr(l2toks[-1])
    cm2toks = []
    smax = c.sbuf("smax", [128, 1], F32)
    ssum = c.sbuf("ssum", [128, 1], F32)
    srcp = c.sbuf("srcp", [128, 1], F32)
    for i in range(8):
        sl = slice(512 * i, 512 * (i + 1))
        tmx = p.op("vector", "tensor_reduce", smax, L2[:, sl], AX.X, ALU.max,
                   negate=True, waits=[l2toks[i]] + t["smax"].w_waits(), tag=True)
        t["L2"].rd(tmx); t["smax"].wr(tmx)
        tex = p.op("scalar", "activation", comb2[:, sl], L2[:, sl], AF.Exp, bias=smax,
                   accum_out=ssum, waits=[tmx] + t["ssum"].w_waits() +
                   (t["comb2"].w_waits() if i == 0 else []), tag=True)
        t["smax"].rd(tex); t["ssum"].wr(tex); t["L2"].rd(tex)
        trc = p.op("vector", "reciprocal", srcp, ssum, waits=[tex] + t["srcp"].w_waits(),
                   tag=True)
        t["ssum"].rd(trc); t["srcp"].wr(trc)
        tcb = p.op("vector", "tensor_scalar", comb2[:, sl], comb2[:, sl], srcp, None,
                   ALU.mult, waits=[trc, tex], tag=True)
        t["srcp"].rd(tcb)
        cm2toks.append(tcb)
    t["comb2"].wr(cm2toks[-1])
    lt_toks = []
    for b in range(BL):
        for ec in range(4):
            for pc in range(4):
                w = [l2toks[b * 4 + pc]] + (t["pt"].w_waits() if pc == 0 else [])
                tk = p.op("tensor", "transpose", pt[:, 128 * pc:128 * (pc + 1)],
                          L2[:, 512 * (b * 4 + pc) + 128 * ec:512 * (b * 4 + pc) + 128 * (ec + 1)],
                          ident, waits=w, tag=(pc == 3))
                t["L2"].rd(tk if tk else l2toks[-1])
            t["pt"].wr(tk)
            i = b * 4 + ec
            ev = p.op("vector", "tensor_copy", L2T[:, 512 * i:512 * (i + 1)], pt[:, 0:512],
                      waits=[tk] + (t["L2T"].w_waits() if i == 0 else []), tag=True)
            t["pt"].rd(ev)
            lt_toks.append(ev)
    t["L2T"].wr(lt_toks[-1])
    dtoks = []
    for i in range(8):
        sl = slice(512 * i, 512 * (i + 1))
        tmx = p.op("vector", "tensor_reduce", smax, L2T[:, sl], AX.X, ALU.max,
                   negate=True, waits=[lt_toks[i]] + t["smax"].w_waits(), tag=True)
        t["L2T"].rd(tmx); t["smax"].wr(tmx)
        tex = p.op("scalar", "activation", disp[:, sl], L2T[:, sl], AF.Exp, bias=smax,
                   accum_out=ssum, waits=[tmx] + t["ssum"].w_waits() +
                   (t["disp"].w_waits() if i == 0 else []), tag=True)
        t["smax"].rd(tex); t["ssum"].wr(tex); t["L2T"].rd(tex)
        trc = p.op("vector", "reciprocal", srcp, ssum, waits=[tex] + t["srcp"].w_waits(),
                   tag=True)
        t["ssum"].rd(trc); t["srcp"].wr(trc)
        tcb = p.op("vector", "tensor_scalar", disp[:, sl], disp[:, sl], srcp, None,
                   ALU.mult, waits=[trc, tex], tag=True)
        t["srcp"].rd(tcb)
        dtoks.append(tcb)
    t["disp"].wr(dtoks[-1])
    dtm = L2
    t["dtm"] = Trk()
    t["dtm"].w = t["L2"].w
    t["dtm"].r = dict(t["L2"].r)
    dm_toks = []
    for b in range(BL):
        for pc in range(4):
            for ec in range(4):
                w = [dtoks[b * 4 + ec]] + (t["pt"].w_waits() if ec == 0 else [])
                tk = p.op("tensor", "transpose", pt[:, 128 * ec:128 * (ec + 1)],
                          disp[:, 512 * (b * 4 + ec) + 128 * pc:512 * (b * 4 + ec) + 128 * (pc + 1)],
                          ident, waits=w, tag=(ec == 3))
                t["disp"].rd(tk if tk else dtoks[-1])
            t["pt"].wr(tk)
            i = b * 4 + pc
            ev = p.op("vector", "tensor_copy", dtm[:, 512 * i:512 * (i + 1)], pt[:, 0:512],
                      waits=[tk] + (t["dtm"].w_waits() if i == 0 else []), tag=True)
            t["pt"].rd(ev)
            dm_toks.append(ev)
    t["dtm"].wr(dm_toks[-1])
    xm_toks = []
    for b in range(BL):
        for pc in range(4):
            i = b * 4 + pc
            for dc in range(DC):
                w = [xs_tok] + (t["pt"].w_waits() if dc % 8 == 0 else [])
                tk = p.op("tensor", "transpose", pt[:, 128 * (dc % 8):128 * ((dc % 8) + 1)],
                          xs[:, 1024 * dc + 512 * b + 128 * pc:1024 * dc + 512 * b + 128 * (pc + 1)],
                          ident, waits=w, tag=(dc == DC - 1))
                t["xs"].rd(tk if tk else xs_tok)
            t["pt"].wr(tk)
            ev = p.op("vector", "tensor_copy", xstm[:, 1024 * i:1024 * (i + 1)], pt[:, 0:1024],
                      waits=[tk] + (t["xstm"].w_waits() if i == 0 else []), tag=True)
            t["pt"].rd(ev)
            xm_toks.append(ev)
    t["xstm"].wr(xm_toks[-1])
    sl2toks = []
    for b in range(BL):
        for dc in range(DC):
            po = (pv0, pv1)[dc % 2]
            tro = t["pv%d" % (dc % 2)]
            mm = None
            for pc in range(4):
                i = b * 4 + pc
                w = [xm_toks[i], dm_toks[i]] + (tro.w_waits() if pc == 0 else [])
                mm = p.op("tensor", "matmul", po[:, :],
                          xstm[:, 1024 * i + 128 * dc:1024 * i + 128 * (dc + 1)],
                          dtm[:, 512 * i:512 * (i + 1)],
                          waits=w, start=(pc == 0), stop=(pc == 3), tag=(pc == 3))
                t["xstm"].rd(mm)
                t["dtm"].rd(mm)
            tro.wr(mm)
            ev = p.op("vector", "tensor_copy", sl2[:, 1024 * dc + 512 * b:1024 * dc + 512 * (b + 1)],
                      po[:, :], waits=[mm] + (t["sl2"].w_waits() if b == 0 and dc == 0 else []),
                      tag=True)
            tro.rd(ev)
            sl2toks.append(ev)
    t["sl2"].wr(sl2toks[-1])
    c.pop()
    sl2_tok = t["sl2"].w
    cb2_tok = t["comb2"].w

    c.push()
    h2 = c.sbuf("h2", [128, 32 * 256], BF16)
    o2toks = []
    for e2 in range(E2):
        h2toks = []
        for mc in range(32):
            par = (mc // 4) % 2
            if mc % 4 == 0:
                tw = p.dma("gpsimd", wstB[par], c.hbm["w1s"][e2][mc:mc + 4],
                           waits=t["wstB%d" % par].w_waits(), ch=par)
                t["wstB%d" % par].wr(tw)
            tw = t["wstB%d" % par].w
            pz = pb(mc % 2)
            trz = t["pb%d" % (mc % 2)]
            mm = None
            for kc in range(DC):
                rhs = sl2[:, 1024 * kc:1024 * (kc + 1)].rearrange(
                    "q (b n) -> q b n", b=BL)[:, :, 128 * e2:128 * (e2 + 1)]
                w = [sl2_tok, tw] + (trz.w_waits() if kc == 0 else [])
                mm = p.op("tensor", "matmul", pz[:, 0:256],
                          wstB[par][:, 1024 * (mc % 4) + 128 * kc:1024 * (mc % 4) + 128 * (kc + 1)],
                          rhs,
                          waits=w, start=(kc == 0), stop=(kc == DC - 1), tag=(kc == DC - 1))
                t["sl2"].rd(mm)
                t["wstB%d" % par].rd(mm)
            trz.wr(mm)
            ev = p.op("scalar", "activation", h2[:, 256 * mc:256 * (mc + 1)], pz[:, 0:256],
                      AF.Gelu, bias=c.smallc("b1s", 32 * e2 + mc),
                      waits=[mm] + (t["h2"].w_waits() if mc == 0 else []), tag=True)
            trz.rd(ev)
            h2toks.append(ev)
        t["h2"].wr(h2toks[-1])
        for dc in range(DC):
            par = dc % 2
            tw = p.dma("gpsimd", wstB2[par], c.hbm["w2s"][e2][dc],
                       waits=t["wstB2%d" % par].w_waits(), ch=par)
            t["wstB2%d" % par].wr(tw)
            po = (pv0, pv1)[dc % 2]
            tro = t["pv%d" % (dc % 2)]
            mm = None
            for kc in range(32):
                w = [h2toks[kc], tw] + (tro.w_waits() if kc == 0 else [])
                mm = p.op("tensor", "matmul", po[:, 0:256],
                          wstB2[par][:, 128 * kc:128 * (kc + 1)],
                          h2[:, 256 * kc:256 * (kc + 1)],
                          waits=w, start=(kc == 0), stop=(kc == 31), tag=(kc == 31))
                t["h2"].rd(mm)
                t["wstB2%d" % par].rd(mm)
            tro.wr(mm)
            dst = o2f[:, 1024 * dc:1024 * (dc + 1)].rearrange(
                "q (b n) -> q b n", b=BL)[:, :, 128 * e2:128 * (e2 + 1)]
            ev = p.op("scalar", "activation", dst, po[:, 0:256], AF.Identity,
                      bias=c.smallc("b2s", 8 * e2 + dc),
                      waits=[mm] + (t["o2f"].w_waits() if e2 == 0 and dc == 0 else []),
                      tag=True)
            tro.rd(ev)
            o2toks.append(ev)
    t["o2f"].wr(o2toks[-1])
    c.pop()

    c.push()
    o2tm = c.sbuf("o2tm", [128, 8 * 1024], BF16)
    c2t = c.sbuf("c2t", [128, 8 * 512], BF16)
    om_toks = []
    for b in range(BL):
        for ec in range(4):
            i = b * 4 + ec
            for dc in range(DC):
                w = [o2toks[-1]] + (t["pt"].w_waits() if dc % 8 == 0 else [])
                tk = p.op("tensor", "transpose", pt[:, 128 * (dc % 8):128 * ((dc % 8) + 1)],
                          o2f[:, 1024 * dc + 512 * b + 128 * ec:1024 * dc + 512 * b + 128 * (ec + 1)],
                          ident, waits=w, tag=(dc == DC - 1))
                t["o2f"].rd(tk if tk else o2toks[-1])
            t["pt"].wr(tk)
            ev = p.op("vector", "tensor_copy", o2tm[:, 1024 * i:1024 * (i + 1)], pt[:, 0:1024],
                      waits=[tk] + (t["o2tm"].w_waits() if i == 0 else []), tag=True)
            t["pt"].rd(ev)
            om_toks.append(ev)
    t["o2tm"].wr(om_toks[-1])
    ct_toks = []
    for b in range(BL):
        for ec in range(4):
            for pc in range(4):
                w = [cb2_tok] + (t["pt"].w_waits() if pc == 0 else [])
                tk = p.op("tensor", "transpose", pt[:, 128 * pc:128 * (pc + 1)],
                          comb2[:, 512 * (b * 4 + pc) + 128 * ec:512 * (b * 4 + pc) + 128 * (ec + 1)],
                          ident, waits=w, tag=(pc == 3))
                t["comb2"].rd(tk if tk else cb2_tok)
            t["pt"].wr(tk)
            i = b * 4 + ec
            ev = p.op("vector", "tensor_copy", c2t[:, 512 * i:512 * (i + 1)], pt[:, 0:512],
                      waits=[tk] + (t["c2t"].w_waits() if i == 0 else []), tag=True)
            t["pt"].rd(ev)
            ct_toks.append(ev)
    t["c2t"].wr(ct_toks[-1])
    ot_toks = []
    for b in range(BL):
        for dc in range(DC):
            po = (pv0, pv1)[dc % 2]
            tro = t["pv%d" % (dc % 2)]
            mm = None
            for ec in range(4):
                i = b * 4 + ec
                w = [om_toks[i], ct_toks[i]] + (tro.w_waits() if ec == 0 else [])
                mm = p.op("tensor", "matmul", po[:, :],
                          o2tm[:, 1024 * i + 128 * dc:1024 * i + 128 * (dc + 1)],
                          c2t[:, 512 * i:512 * (i + 1)],
                          waits=w, start=(ec == 0), stop=(ec == 3), tag=(ec == 3))
                t["o2tm"].rd(mm)
                t["c2t"].rd(mm)
            tro.wr(mm)
            ev = p.op("vector", "tensor_copy",
                      out2[:, 1024 * dc + 512 * b:1024 * dc + 512 * (b + 1)], po[:, :],
                      waits=[mm] + (t["out2"].w_waits() if b == 0 and dc == 0 else []),
                      tag=True)
            tro.rd(ev)
            ot_toks.append(ev)
    t["out2"].wr(ot_toks[-1])
    c.pop()
    out2_tok = t["out2"].w

    c.push()
    ffn_fm = c.sbuf("ffn_fm", [128, DC * 4096], BF16)
    cT = c.views("cT", 2, [16, 64], BF16)
    o2g = c.views("o2g", 2, [16, 1024], BF16)
    fsb = c.views("fsb", 2, [64, 1024], BF16)
    ff_toks = []
    for bg in range(64):
        b, g = bg // 32, bg % 32
        par = bg % 2
        tk = p.op("tensor", "transpose", pt[0:16, 0:64], cmb[:, 16 * bg:16 * (bg + 1)],
                  ident[0:64, 0:64], waits=[cmb_tok] + t["pt"].w_waits(), tag=True)
        t["cmb"].rd(tk); t["pt"].wr(tk)
        ck = p.op("vector", "tensor_copy", cT[par], pt[0:16, 0:64],
                  waits=[tk] + t["cT%d" % par].w_waits(), tag=True)
        t["pt"].rd(ck); t["cT%d" % par].wr(ck)
        gtk = None
        for dc in range(DC):
            w = [out2_tok] + (t["pt"].w_waits() if dc == 0 else [])
            gtk = p.op("tensor", "transpose", pt[0:16, 128 * (dc % 8):128 * ((dc % 8) + 1)],
                       out2[:, 1024 * dc + 512 * b + 16 * g:1024 * dc + 512 * b + 16 * (g + 1)],
                       ident, waits=w, tag=(dc == DC - 1))
            t["out2"].rd(gtk if gtk else out2_tok)
        t["pt"].wr(gtk)
        gk = p.op("vector", "tensor_copy", o2g[par], pt[0:16, 0:1024],
                  waits=[gtk] + t["o2g%d" % par].w_waits(), tag=True)
        t["pt"].rd(gk); t["o2g%d" % par].wr(gk)
        mms = []
        for hf in range(2):
            po = (pv0, pv1)[hf]
            tro = t["pv%d" % hf]
            mm = p.op("tensor", "matmul", po[0:64, :], cT[par],
                      o2g[par][:, 512 * hf:512 * (hf + 1)],
                      waits=[ck, gk] + tro.w_waits(), start=True, stop=True, tag=True)
            t["cT%d" % par].rd(mm)
            t["o2g%d" % par].rd(mm)
            tro.wr(mm)
            mms.append(mm)
        ev = p.op("vector", "tensor_copy", fsb[par][:, 0:512], pv0[0:64, :],
                  waits=[mms[0]] + t["fsb%d" % par].w_waits(), tag=True)
        t["pv0"].rd(ev)
        ev2 = p.op("vector", "tensor_copy", fsb[par][:, 512:1024], pv1[0:64, :],
                   waits=[mms[1], ev], tag=True)
        t["pv1"].rd(ev2)
        t["fsb%d" % par].wr(ev2)
        for dc in range(DC):
            tk = p.op("tensor", "transpose", pt[:, 0:64],
                      fsb[par][:, 128 * dc:128 * (dc + 1)], ident[0:64, 0:64],
                      waits=[ev2] + t["pt"].w_waits(), tag=True)
            t["fsb%d" % par].rd(tk); t["pt"].wr(tk)
            fk = p.op("vector", "tensor_copy",
                      ffn_fm[:, 4096 * dc + 2048 * b + 64 * g:4096 * dc + 2048 * b + 64 * (g + 1)],
                      pt[:, 0:64],
                      waits=[tk] + (t["ffn_fm"].w_waits() if bg == 0 and dc == 0 else []),
                      tag=True)
            t["pt"].rd(fk)
            ff_toks.append(fk)
    t["ffn_fm"].wr(ff_toks[-1])

    acct = [tmpq[0][:, 0:512], tmpq[1][:, 0:512]]
    yt = [tmpq[0][:, 512:1024], tmpq[1][:, 512:1024]]
    for i2 in range(2):
        t["acct%d" % i2] = Trk()
        t["acct%d" % i2].w = t["tmpq%d" % i2].w
        t["acct%d" % i2].r = dict(t["tmpq%d" % i2].r)
        t["yt%d" % i2] = Trk()
        t["yt%d" % i2].w = t["tmpq%d" % i2].w
    for dc in range(DC):
        for i in range(8):
            par = i % 2
            cc = slice(512 * i, 512 * (i + 1))
            b, tk0 = (512 * i) // 2048, (512 * i) % 2048
            al = p.dma("sync", acct[par], acc_s[b, dc, :, tk0:tk0 + 512],
                       waits=t["acct%d" % par].w_waits(), ch=par)
            t["acct%d" % par].wr(al)
            ta = p.op("vector", "tensor_tensor", yt[par], acct[par],
                      ffn_fm[:, 4096 * dc + cc.start:4096 * dc + cc.stop], ALU.add,
                      waits=[al, ff_toks[-1]] + t["yt%d" % par].w_waits(), tag=True)
            t["acct%d" % par].rd(ta)
            t["ffn_fm"].rd(ta)
            t["yt%d" % par].wr(ta)
            dtk = p.dma("sync", y[b, dc, :, tk0:tk0 + 512], yt[par], waits=[ta])
            t["yt%d" % par].rd(dtk)
    c.pop()
    c.pop()
    c.pop()
